# version 04
# baseline (speedup 1.0000x reference)
import numpy as np
import ml_dtypes
import concourse.bass as bass
import concourse.mybir as mybir
from concourse.bass_utils import run_bass_kernel_spmd
from concourse.bass_core_rust import IndirectOffsetOnAxis
from contextlib import ExitStack

F32 = mybir.dt.float32
BF16 = mybir.dt.bfloat16
I32 = mybir.dt.int32
U8 = mybir.dt.uint8
AF = mybir.ActivationFunctionType
ALU = mybir.AluOpType
AX = mybir.AxisListType

T = 4096
D = 1024
NT = T // 128
ND = D // 128
NC5 = T // 512
MAGIC = 1.5 * 2.0 ** 23
BF16_NP = ml_dtypes.bfloat16
LNEPS = 1e-5
RMSEPS = 1e-6



_OUTS = {
    "dma_start": ("out",), "indirect_dma_start": ("out",),
    "matmul": ("out",), "transpose": ("out",),
    "activation": ("out", "accum_out"),
    "memset": ("ap",), "memzero": ("ap",),
    "tensor_copy": ("out",), "tensor_tensor": ("out",),
    "tensor_add": ("out",), "tensor_sub": ("out",), "tensor_mul": ("out",),
    "tensor_max": ("out",),
    "tensor_scalar": ("out", "accum_out"), "tensor_single_scalar": ("out",),
    "tensor_scalar_add": ("out",), "tensor_scalar_sub": ("out",),
    "tensor_scalar_mul": ("out",), "tensor_scalar_max": ("out",),
    "tensor_scalar_min": ("out",), "tensor_relu": ("out",),
    "tensor_reduce": ("out",), "reduce_max": ("out",), "reduce_sum": ("out",),
    "reciprocal": ("out",), "scalar_tensor_tensor": ("out", "accum_out"),
    "tensor_tensor_scan": ("out",), "select": ("out",),
    "copy_predicated": ("out",), "iota": ("out",),
    "max": ("out",), "max_index": ("out",),
    "max_with_indices": ("out_max", "out_indices"),
    "match_replace": ("out",), "sign": ("out",), "square": ("out",),
    "sqrt": ("out",), "mul": ("out",), "copy": ("out",), "add": ("out",),
    "collective_compute": ("outs",),
}
_POSN = {
    "dma_start": ("out", "in_"), "matmul": ("out", "lhsT", "rhs"),
    "transpose": ("out", "in_", "identity"),
    "activation": ("out", "in_", "func", "bias", "scale", "alpha", "accum_out"),
    "memset": ("ap", "constant"), "memzero": ("ap",),
    "tensor_copy": ("out", "in_"), "tensor_tensor": ("out", "in0", "in1", "op"),
    "tensor_add": ("out", "in0", "in1"), "tensor_sub": ("out", "in0", "in1"),
    "tensor_mul": ("out", "in0", "in1"), "tensor_max": ("out", "in0", "in1"),
    "tensor_scalar": ("out", "in0", "scalar1", "scalar2", "op0", "op1", "accum_out"),
    "tensor_single_scalar": ("out", "in_", "scalar", "op"),
    "tensor_scalar_add": ("out", "in0", "scalar1"),
    "tensor_scalar_sub": ("out", "in0", "scalar1"),
    "tensor_scalar_mul": ("out", "in0", "scalar1"),
    "tensor_scalar_max": ("out", "in0", "scalar1"),
    "tensor_scalar_min": ("out", "in0", "scalar1"),
    "tensor_reduce": ("out", "in_", "axis", "op"),
    "reduce_max": ("out", "in_", "axis"), "reduce_sum": ("out", "in_", "axis"),
    "reciprocal": ("out", "in_"),
    "scalar_tensor_tensor": ("out", "in0", "scalar", "in1", "op0", "op1", "accum_out"),
    "tensor_tensor_scan": ("out", "data0", "data1", "initial", "op0", "op1"),
    "select": ("out", "mask", "on_true", "on_false"),
    "copy_predicated": ("out", "mask", "data"),
    "iota": ("out", "pattern"),
    "max": ("out", "in_"), "max_index": ("out", "in_max", "in_values"),
    "max_with_indices": ("out_max", "out_indices", "in_"),
    "match_replace": ("out", "in_to_replace", "in_values", "imm_value"),
    "square": ("out", "in_"), "sqrt": ("out", "in_"),
    "mul": ("out", "in_", "mul"), "copy": ("out", "in_"), "add": ("out", "in_", "add"),
    "indirect_dma_start": ("out", "out_offset", "in_", "in_offset"),
}
_DMA_OPS = {"dma_start", "indirect_dma_start", "collective_compute", "dma_start_transpose"}


def _ap_tensor_name(a):
    t = getattr(a, "tensor", None)
    if t is not None:
        return getattr(t, "name", None)
    return None


class _Pend:
    def __init__(self, b, engine, box):
        self.b = b
        self.engine = engine
        self.box = box


class Sem:
    def __init__(self, name):
        self.name = name
        self.h = None
        self.v = 0


class Eng:
    def __init__(self, b, name):
        self._b = b
        self._name = name

    def __getattr__(self, att):
        if att.startswith("_"):
            raise AttributeError(att)
        b = self._b
        name = self._name

        def em(*a, **kw):
            posn = _POSN.get(att, ())
            outs_names = _OUTS.get(att, ("out",))
            outs, ins = [], []
            def visit(argname, val):
                if isinstance(val, (list, tuple)):
                    for v in val:
                        visit(argname, v)
                    return
                tn = _ap_tensor_name(val)
                if tn is None:
                    sub = getattr(val, "ap", None)
                    tn2 = _ap_tensor_name(sub) if sub is not None else None
                    if tn2 is not None:
                        ins.append(tn2)
                    return
                if argname in outs_names and argname not in ("out_offset", "in_offset"):
                    outs.append(tn)
                else:
                    ins.append(tn)
            for i, val in enumerate(a):
                visit(posn[i] if i < len(posn) else f"arg{i}", val)
            for k, val in kw.items():
                visit(k, val)
            if att not in _OUTS and att not in _POSN and att not in ("nop", "wait_ge"):
                raise AssertionError(f"op {att} not in dependency tables")

            def need(dep_eng):
                return not (dep_eng == name == "tensor")
            for tn in set(ins) | set(outs):
                for weng, wval in b.writes.get(tn, {}).items():
                    if need(weng):
                        b._wait(name, weng, wval)
            for tn in set(outs):
                for reng, rval in b.reads.get(tn, {}).items():
                    if need(reng) and rval > 0:
                        b._wait(name, reng, rval)

            box = {}

            def run(eng, box=box, att=att, a=a, kw=kw):
                try:
                    box["i"] = getattr(eng, att)(*a, **kw)
                except Exception:
                    import sys
                    print("FAILING OP:", att, "tensors:",
                          [(_ap_tensor_name(x), getattr(x, "shape", None)) for x in a],
                          kw, file=sys.stderr)
                    raise
            is_dma = att in _DMA_OPS and name in b.dsems
            if is_dma:
                k = b.dslot[name]
                b.dslot[name] = (k + 1) % len(b.dsems[name])
                ssem = b.dsems[name][k]
                prev = b.dcnt[name][k]
                if prev and b.mark[name].get((name, k), 0) < prev:
                    b.ops[name].append(
                        lambda eng, s=ssem, v=16 * prev: eng.wait_ge(s.h, v))
                    b.mark[name][(name, k)] = prev
            b.ops[name].append(run)
            sem = b.esem[name]
            step = 16 if att in _DMA_OPS else 1
            isem = ssem if is_dma else sem
            def inc_fn(eng, box=box, isem=isem, step=step):
                if box["i"] is not None:
                    box["i"].then_inc(isem.h, step)
                else:
                    eng.nop().then_inc(isem.h, step)
            b.ops[name].append(inc_fn)
            sem.v += step
            v = sem.v
            if is_dma:
                b.dcnt[name][k] += 1
                b.dsnap[name].append(tuple(b.dcnt[name]))
            for tn in ins:
                b.reads.setdefault(tn, {})[name] = v
            for tn in outs:
                b.writes.setdefault(tn, {})[name] = v
                b.reads.setdefault(tn, {}).setdefault(name, 0)
            return _Pend(b, name, box)
        return em


class B:
    def __init__(self, nc, ctx):
        self.nc = nc
        self.ctx = ctx
        self.ops = {"sync": [], "gpsimd": [], "vector": [], "scalar": [], "tensor": []}
        self.sy = Eng(self, "sync")
        self.gp = Eng(self, "gpsimd")
        self.ve = Eng(self, "vector")
        self.sc = Eng(self, "scalar")
        self.te = Eng(self, "tensor")
        self.sems = []
        self.sd = self.mksem("sd")
        self.gd = self.mksem("gd")
        self.pe = self.mksem("pe")
        self.dv = self.mksem("dv")
        self.ac = self.mksem("ac")
        self.esem = {"sync": self.sd, "gpsimd": self.gd, "tensor": self.pe,
                     "vector": self.dv, "scalar": self.ac}
        self.sem_of_eng = {"sync": "sd", "gpsimd": "gd", "tensor": "pe",
                           "vector": "dv", "scalar": "ac"}
        KSL = 8
        self.dsems = {e: [self.mksem(f"{self.sem_of_eng[e]}s{k}") for k in range(KSL)]
                      for e in ("sync", "gpsimd")}
        self.dslot = {e: 0 for e in self.dsems}
        self.dcnt = {e: [0] * KSL for e in self.dsems}
        self.dsnap = {e: [] for e in self.dsems}
        self.mark = {e: {e2: 0 for e2 in self.ops} for e in self.ops}
        self.writes = {}
        self.reads = {}

    def mksem(self, name):
        s = Sem(name)
        self.sems.append(s)
        return s

    def _wait(self, engine, prod_engine, val):
        if self.mark[engine][prod_engine] >= val:
            return
        sem = self.esem[prod_engine]
        assert val <= sem.v
        if prod_engine in self.dsems:
            idx = val // 16
            if idx == 0:
                return
            counts = self.dsnap[prod_engine][idx - 1]
            for k, c in enumerate(counts):
                if c and self.mark[engine].get((prod_engine, k), 0) < c:
                    s = self.dsems[prod_engine][k]
                    self.ops[engine].append(
                        lambda eng, s=s, v=16 * c: eng.wait_ge(s.h, v))
                    self.mark[engine][(prod_engine, k)] = c
            self.mark[engine][prod_engine] = val
            return
        self.ops[engine].append(lambda eng, sem=sem, v=val: eng.wait_ge(sem.h, v))
        self.mark[engine][prod_engine] = val

    def inc(self, pend, sem, k=1):
        return self.esem[pend.engine].v

    def wait(self, engine, sem, val=None):
        if val is None:
            val = sem.v
        assert val <= sem.v, f"wait for future value of {sem.name}"
        prod = {"sd": "sync", "gd": "gpsimd", "pe": "tensor", "dv": "vector",
                "ac": "scalar"}.get(sem.name)
        if prod is not None:
            self._wait(engine, prod, val)
        else:
            self.ops[engine].append(lambda eng, sem=sem, v=val: eng.wait_ge(sem.h, v))

    def barrier(self, engine):
        for other in self.ops:
            if other != engine:
                self._wait(engine, other, self.esem[other].v)

    def sbuf(self, name, shape, dtype):
        return self.ctx.enter_context(self.nc.sbuf_tensor(name, shape, dtype))

    def psum(self, name, shape, dtype=F32):
        return self.ctx.enter_context(self.nc.psum_tensor(name, shape, dtype))

    def realize(self):
        with ExitStack() as sctx:
            for s in self.sems:
                s.h = sctx.enter_context(self.nc.semaphore(s.name))
            with self.nc.Block() as block:
                block.sync(lambda eng: [f(eng) for f in self.ops["sync"]])
                block.gpsimd(lambda eng: [f(eng) for f in self.ops["gpsimd"]])
                block.vector(lambda eng: [f(eng) for f in self.ops["vector"]])
                block.scalar(lambda eng: [f(eng) for f in self.ops["scalar"]])
                block.tensor(lambda eng: [f(eng) for f in self.ops["tensor"]])



def emit_ln_chunk(b, src, dst, stats, sq_sc, cc, eps_col=0, dcols=D):
    ve, sc = b.ve, b.sc
    b.inc(ve.tensor_reduce(stats[:, 0:1], src[:], axis=AX.X, op=ALU.add), b.dv)
    dvv = b.inc(ve.tensor_single_scalar(stats[:, 0:1], stats[:, 0:1], 1.0 / dcols, ALU.mult), b.dv)
    b.wait("scalar", b.dv, dvv)
    acv = b.inc(sc.activation(out=sq_sc[:], in_=src[:], func=AF.Square,
                              accum_out=stats[:, 1:2]), b.ac)
    b.wait("vector", b.ac, acv)
    b.inc(ve.tensor_single_scalar(stats[:, 1:2], stats[:, 1:2], 1.0 / dcols, ALU.mult), b.dv)
    b.inc(ve.tensor_mul(stats[:, 2:3], stats[:, 0:1], stats[:, 0:1]), b.dv)
    dvv = b.inc(ve.tensor_sub(stats[:, 1:2], stats[:, 1:2], stats[:, 2:3]), b.dv)
    b.wait("scalar", b.dv, dvv)
    b.inc(sc.activation(out=stats[:, 3:4], in_=stats[:, 1:2], func=AF.Ln, bias=cc[:, eps_col:eps_col + 1]), b.ac)
    acv = b.inc(sc.activation(out=stats[:, 3:4], in_=stats[:, 3:4], func=AF.Exp, scale=cc[:, 2:3]), b.ac)
    b.wait("vector", b.ac, acv)
    b.wait("vector", b.pe)
    return b.inc(ve.tensor_scalar(dst[:], src[:], stats[:, 0:1], stats[:, 3:4],
                                  ALU.subtract, ALU.mult), b.dv)


def emit_tr_to_fm(b, ntk, fm, tchunk, identb, ps_tr, affine=None, fm_stride=T):
    b.wait("tensor", b.dv)
    for j in range(ND):
        pj = j % 2
        tv = b.inc(b.te.transpose(ps_tr[pj][:], ntk[:, 128 * j:128 * (j + 1)], identb[:]), b.pe)
        b.wait("vector", b.pe, tv)
        dst = fm[:, fm_stride * j + 128 * tchunk: fm_stride * j + 128 * (tchunk + 1)]
        if affine is None:
            dvv = b.inc(b.ve.tensor_copy(dst, ps_tr[pj][:]), b.dv)
        else:
            w_c, b_c = affine
            dvv = b.inc(b.ve.tensor_scalar(dst, ps_tr[pj][:],
                                           w_c[:, j:j + 1], b_c[:, j:j + 1],
                                           ALU.mult, ALU.add), b.dv)
        b.wait("tensor", b.dv, dvv)
    return dvv


def emit_param_cols(b, src_row_ap, dst, n=ND):
    return b.inc(b.sy.dma_start(out=dst[:, 0:n],
                                in_=src_row_ap.rearrange("a (j p) -> (a p) j", p=128),
                                allow_slow_non_contiguous=True),
                 b.sd, 16)


def emit_dma_bcast_rows(b, tile, width):
    span = 1
    while span < 128:
        cp = min(span, 128 - span)
        gv = b.inc(b.gp.dma_start(out=tile[span:span + cp, 0:width],
                                  in_=tile[0:cp, 0:width]), b.gd, 16)
        b.wait("gpsimd", b.gd, gv)
        span += cp



def build_launch1(dbg=False):
    nc = bass.Bass("TRN2", target_bir_lowering=False)
    x_in = nc.dram_tensor("x", [T, D], F32, kind="ExternalInput").ap()
    w_r = nc.dram_tensor("Wr", [D, D], F32, kind="ExternalInput").ap()
    w_k = nc.dram_tensor("Wk", [D, D], F32, kind="ExternalInput").ap()
    w_v = nc.dram_tensor("Wv", [D, D], F32, kind="ExternalInput").ap()
    w_o = nc.dram_tensor("Wo", [D, D], F32, kind="ExternalInput").ap()
    decay = nc.dram_tensor("decay", [1, D], F32, kind="ExternalInput").ap()
    ln_w = nc.dram_tensor("ln_w", [1, D], F32, kind="ExternalInput").ap()
    ln1_w = nc.dram_tensor("ln1_w", [1, D], F32, kind="ExternalInput").ap()
    ln1_b = nc.dram_tensor("ln1_b", [1, D], F32, kind="ExternalInput").ap()
    qkvw = nc.dram_tensor("qkvw", [1536, D], F32, kind="ExternalInput").ap()

    x1_out = nc.dram_tensor("x1", [T, D], F32, kind="ExternalOutput").ap()
    qkv_out = nc.dram_tensor("qkv", [T, 1536], BF16, kind="ExternalOutput").ap()

    if dbg:
        dbg_xln0 = nc.dram_tensor("dbg_xln0", [128, ND * T], BF16, kind="ExternalOutput").ap()
        dbg_r = nc.dram_tensor("dbg_r", [128, T], BF16, kind="ExternalOutput").ap()
        dbg_k = nc.dram_tensor("dbg_k", [128, T], BF16, kind="ExternalOutput").ap()
        dbg_y = nc.dram_tensor("dbg_y", [D, T], BF16, kind="ExternalOutput").ap()
        dbg_ssq = nc.dram_tensor("dbg_ssq", [1, T], F32, kind="ExternalOutput").ap()
        dbg_rstd = nc.dram_tensor("dbg_rstd", [128, 32], F32, kind="ExternalOutput").ap()
        dbg_xln1 = nc.dram_tensor("dbg_xln1", [128, ND * T], BF16, kind="ExternalOutput").ap()
        dbg_kv = nc.dram_tensor("dbg_kv", [128, T], F32, kind="ExternalOutput").ap()
        dbg_scl = nc.dram_tensor("dbg_scl", [128, T], F32, kind="ExternalOutput").ap()
        dbg_st = nc.dram_tensor("dbg_st", [128, T], F32, kind="ExternalOutput").ap()
    wT_o_dram = nc.dram_tensor("wT_o", [D, D], BF16)
    y_dram = nc.dram_tensor("y_dram", [D, T], BF16)
    row_scr = nc.dram_tensor("row_scr", [1, T], F32)

    with ExitStack() as ctx:
        b = B(nc, ctx)
        idb_dram = nc.inline_tensor(np.eye(128, dtype=BF16_NP), "ident_bf")
        idf_dram = nc.inline_tensor(np.eye(32, dtype=np.float32), "ident_f32")
        pos_dram = nc.inline_tensor(np.arange(512, dtype=np.float32)[None, :].copy(), "pos_row")
        onesc_dram = nc.inline_tensor(np.ones((128, 1), dtype=BF16_NP), "ones_col_d")
        cc_np = np.zeros((128, 8), np.float32)
        cc_np[:, 0] = LNEPS; cc_np[:, 1] = RMSEPS; cc_np[:, 2] = -0.5; cc_np[:, 3] = 1.0 / D
        cc_dram = nc.inline_tensor(cc_np, "consts_cc")

        identb = b.sbuf("identb", [128, 128], BF16)
        cc = b.sbuf("cc", [128, 8], F32)
        identf = b.sbuf("identf", [32, 32], F32)
        ones_col = b.sbuf("ones_col", [128, 1], BF16)
        act_fm = b.sbuf("act_fm", [128, ND * T], BF16)
        wbig = b.sbuf("wbig", [128, 3 * ND * D], BF16)

        class _View:
            def __init__(self, base, off):
                self.base, self.off = base, off

            def __getitem__(self, key):
                p, f = key
                return self.base[p, self.off + f.start: self.off + f.stop]

        wT_r = _View(wbig, 0)
        wT_k = _View(wbig, ND * D)
        wT_v = _View(wbig, 2 * ND * D)
        wT_qkv = _View(wbig, ND * D)
        r_sb = b.sbuf("r_sb", [128, T], BF16)
        k_sb = b.sbuf("k_sb", [128, T], BF16)
        ysq = b.sbuf("ysq", [128, T], BF16)
        pos0 = b.sbuf("pos0", [128, 512], F32)
        posw_t = b.sbuf("posw_t", [128, 512], F32)
        CW = 512
        kvt = b.sbuf("kvt", [128, CW], F32)
        sc1 = b.sbuf("sc1", [128, CW], F32)
        sc2 = b.sbuf("sc2", [128, CW], F32)
        cumt = b.sbuf("cumt", [128, CW], F32)
        zeros = b.sbuf("zeros", [128, CW], F32)
        carry = b.sbuf("carry", [128, 2], F32)
        sqb = b.sbuf("sqb", [128, CW], BF16)
        xtk = b.sbuf("xtk", [128, D], F32)
        htk = b.sbuf("htk", [128, D], F32)
        ntk = b.sbuf("ntk", [128, D], BF16)
        x1tk = b.sbuf("x1tk", [128, D], F32)
        qkvtk = b.sbuf("qkvtk", [128, 1536], BF16)
        stats = b.sbuf("stats", [128, 8], F32)
        sq_sc = b.sbuf("sq_sc", [128, D], F32)
        dec_c = b.sbuf("dec_c", [128, ND], F32)
        lg_c = b.sbuf("lg_c", [128, ND], F32)
        nlg_c = b.sbuf("nlg_c", [128, ND], F32)
        lnw_c = b.sbuf("lnw_c", [128, ND], F32)
        ln1w_c = b.sbuf("ln1w_c", [128, ND], F32)
        ln1b_c = b.sbuf("ln1b_c", [128, ND], F32)
        ssq_row = b.sbuf("ssq_row", [1, T], F32)
        rstd_blk = b.sbuf("rstd_blk", [32, 128], F32)
        rstd_cols = b.sbuf("rstd_cols", [128, 32], F32)
        wrow = b.sbuf("wrow", [128, D], F32)
        wq_b = b.sbuf("wq_b", [128, D], BF16)
        wsc = b.sbuf("wsc", [128, 1], F32)
        wrec = b.sbuf("wrec", [128, 1], F32)
        ntk128 = b.sbuf("ntk128", [128, 128], BF16)
        ybuf = b.sbuf("ybuf", [128, ND * 256], BF16)

        ps_tr = [b.psum(f"ps_tr{i}", [128, 128], BF16) for i in range(2)]
        ps_mm = [b.psum(f"ps_mm{i}", [128, 512], F32) for i in range(4)]
        ps_row = b.psum("ps_row", [1, 512], F32)
        ps_t32 = b.psum("ps_t32", [128, 32], F32)

        sy, gp, ve, sc, te = b.sy, b.gp, b.ve, b.sc, b.te

        b.inc(sy.dma_start(out=identb[:], in_=idb_dram.ap()), b.sd, 16)
        b.inc(sy.dma_start(out=cc[:], in_=cc_dram.ap()), b.sd, 16)
        b.inc(sy.dma_start(out=identf[:], in_=idf_dram.ap()), b.sd, 16)
        b.inc(sy.dma_start(out=ones_col[:], in_=onesc_dram.ap()), b.sd, 16)
        b.inc(sy.dma_start(out=pos0[0:1, :], in_=pos_dram.ap()), b.sd, 16)
        emit_param_cols(b, decay, dec_c)
        emit_param_cols(b, ln_w, lnw_c)
        emit_param_cols(b, ln1_w, ln1w_c)
        emit_param_cols(b, ln1_b, ln1b_c)
        base = b.sd.v
        emit_dma_bcast_rows(b, pos0, 512)

        b.wait("vector", b.sd, base)
        b.inc(ve.memset(zeros[:], 0.0), b.dv)
        b.inc(ve.memset(ssq_row[:], 0.0), b.dv)
        b.wait("scalar", b.sd, base)
        b.inc(sc.activation(out=dec_c[:], in_=dec_c[:], func=AF.Sigmoid), b.ac)
        b.wait("vector", b.ac)
        dvv = b.inc(ve.tensor_scalar_max(lg_c[:], dec_c[:], 1e-7), b.dv)
        b.wait("scalar", b.dv, dvv)
        b.inc(sc.activation(out=lg_c[:], in_=lg_c[:], func=AF.Ln), b.ac)
        b.wait("vector", b.ac)
        b.inc(ve.tensor_scalar_mul(nlg_c[:], lg_c[:], -1.0), b.dv)

        def prep_bitlinear(w_src, dst_sb=None, dst_dram=None):
            for i in range(ND):
                b.wait("sync", b.dv)
                ld = b.inc(sy.dma_start(out=wrow[:], in_=w_src[128 * i:128 * (i + 1), :]), b.sd, 16)
                b.wait("vector", b.sd, ld)
                b.wait("vector", b.pe)
                b.inc(ve.tensor_reduce(wsc[:], wrow[:], axis=AX.X, op=ALU.add,
                                       apply_absolute_value=True), b.dv)
                b.inc(ve.tensor_scalar(wsc[:], wsc[:], 1.0 / D, 1e-5, ALU.mult, ALU.max), b.dv)
                b.inc(ve.reciprocal(wrec[:], wsc[:]), b.dv)
                b.inc(ve.tensor_scalar(sq_sc[:], wrow[:], wrec[:, 0:1], MAGIC, ALU.mult, ALU.add), b.dv)
                b.inc(ve.tensor_single_scalar(sq_sc[:], sq_sc[:], MAGIC, ALU.subtract), b.dv)
                b.inc(ve.tensor_scalar(sq_sc[:], sq_sc[:], -1.0, 1.0, ALU.max, ALU.min), b.dv)
                dvv = b.inc(ve.tensor_single_scalar(wq_b[:], sq_sc[:], wsc[:, 0:1], ALU.mult), b.dv)
                b.wait("tensor", b.dv, dvv)
                for j in range(ND):
                    pj = j % 2
                    tv = b.inc(te.transpose(ps_tr[pj][:], wq_b[:, 128 * j:128 * (j + 1)], identb[:]), b.pe)
                    b.wait("vector", b.pe, tv)
                    if dst_sb is not None:
                        dvv = b.inc(ve.tensor_copy(dst_sb[:, D * j + 128 * i: D * j + 128 * (i + 1)],
                                                   ps_tr[pj][:]), b.dv)
                        b.wait("tensor", b.dv, dvv)
                    else:
                        dvv = b.inc(ve.tensor_copy(ntk128[:], ps_tr[pj][:]), b.dv)
                        b.wait("gpsimd", b.dv, dvv)
                        gv = b.inc(gp.dma_start(
                            out=dst_dram.ap()[128 * j:128 * (j + 1), 128 * i:128 * (i + 1)],
                            in_=ntk128[:]), b.gd, 16)
                        b.wait("vector", b.gd, gv)
                        b.wait("tensor", b.dv, dvv)

        prep_bitlinear(w_r, dst_sb=wT_r)
        prep_bitlinear(w_k, dst_sb=wT_k)
        prep_bitlinear(w_v, dst_sb=wT_v)
        prep_bitlinear(w_o, dst_dram=wT_o_dram)

        for t in range(NT):
            b.wait("sync", b.dv)
            b.wait("sync", b.ac)
            ld = b.inc(sy.dma_start(out=xtk[:], in_=x_in[128 * t:128 * (t + 1), :]), b.sd, 16)
            b.wait("vector", b.sd, ld)
            b.wait("scalar", b.sd, ld)
            emit_ln_chunk(b, xtk, ntk, stats, sq_sc, cc)
            emit_tr_to_fm(b, ntk, act_fm, t, identb, ps_tr)

        if dbg:
            b.wait("gpsimd", b.dv)
            gv = b.inc(gp.dma_start(out=dbg_xln0, in_=act_fm[:, :]), b.gd, 16)
            b.wait("gpsimd", b.gd, gv)
            b.wait("vector", b.gd, gv)
            b.wait("tensor", b.gd, gv)
        for j in range(ND):
            b.wait("scalar", b.dv)
            for w_sb, dst, sig in ((wT_r, r_sb, True), (wT_k, k_sb, False)):
                for ch in range(NC5):
                    pb = ch % 2
                    b.wait("tensor", b.ac if sig else b.dv)
                    for kk in range(ND):
                        mv = b.inc(te.matmul(ps_mm[pb][:],
                                             w_sb[:, D * kk + 128 * j: D * kk + 128 * (j + 1)],
                                             act_fm[:, T * kk + 512 * ch: T * kk + 512 * (ch + 1)],
                                             start=(kk == 0), stop=(kk == ND - 1)), b.pe)
                    if sig:
                        b.wait("scalar", b.pe, mv)
                        b.inc(sc.activation(out=dst[:, 512 * ch:512 * (ch + 1)],
                                            in_=ps_mm[pb][:], func=AF.Sigmoid), b.ac)
                    else:
                        b.wait("vector", b.pe, mv)
                        b.inc(ve.tensor_copy(dst[:, 512 * ch:512 * (ch + 1)], ps_mm[pb][:]), b.dv)
            if dbg and j == 0:
                b.wait("gpsimd", b.ac)
                b.wait("gpsimd", b.dv)
                gv = b.inc(gp.dma_start(out=dbg_r, in_=r_sb[:, :]), b.gd, 16)
                gv = b.inc(gp.dma_start(out=dbg_k, in_=k_sb[:, :]), b.gd, 16)
                b.wait("gpsimd", b.gd, gv)
                b.wait("vector", b.gd, gv)
                b.wait("scalar", b.gd, gv)
            b.wait("vector", b.ac)
            for ch in range(NC5):
                pb = 2 + ch % 2
                b.wait("tensor", b.dv)
                for kk in range(ND):
                    mv = b.inc(te.matmul(ps_mm[pb][:],
                                         wT_v[:, D * kk + 128 * j: D * kk + 128 * (j + 1)],
                                         act_fm[:, T * kk + 512 * ch: T * kk + 512 * (ch + 1)],
                                         start=(kk == 0), stop=(kk == ND - 1)), b.pe)
                b.wait("vector", b.pe, mv)
                b.inc(ve.tensor_tensor(kvt[:], k_sb[:, 512 * ch:512 * (ch + 1)],
                                       ps_mm[pb][:], ALU.mult), b.dv)
                b.inc(ve.tensor_single_scalar(posw_t[:], pos0[:], float(512 * ch), ALU.add), b.dv)
                posw = posw_t[:]
                b.wait("scalar", b.dv)
                b.inc(sc.activation(out=sc1[:], in_=posw, func=AF.Exp,
                                    scale=lg_c[:, j:j + 1]), b.ac)
                ve.tensor_single_scalar(sc2[:], posw, nlg_c[:, j:j + 1], ALU.mult)
                ve.tensor_scalar_min(sc2[:], sc2[:], 23.1)
                sc.activation(out=sc2[:], in_=sc2[:], func=AF.Exp)
                b.inc(ve.tensor_scalar_min(sc2[:], sc2[:], 1e10), b.dv)
                b.inc(ve.tensor_mul(sc2[:], sc2[:], kvt[:]), b.dv)
                b.inc(ve.tensor_tensor_scan(out=cumt[:], data0=sc2[:], data1=zeros[:],
                                            initial=(0.0 if ch == 0 else carry[:, 0:1]),
                                            op0=ALU.add, op1=ALU.add), b.dv)
                b.inc(ve.tensor_copy(carry[:, 0:1], cumt[:, CW - 1:CW]), b.dv)
                if dbg and j == 0:
                    dvv0 = b.dv.v
                    b.wait("gpsimd", b.dv, dvv0)
                    gv = b.inc(gp.dma_start(out=dbg_kv[:, 512 * ch:512 * (ch + 1)], in_=kvt[:]), b.gd, 16)
                    gv = b.inc(gp.dma_start(out=dbg_scl[:, 512 * ch:512 * (ch + 1)], in_=cumt[:]), b.gd, 16)
                    b.wait("vector", b.gd, gv)
                b.inc(ve.tensor_mul(cumt[:], cumt[:], sc1[:]), b.dv)
                if dbg and j == 0:
                    dvv0 = b.dv.v
                    b.wait("gpsimd", b.dv, dvv0)
                    gv = b.inc(gp.dma_start(out=dbg_st[:, 512 * ch:512 * (ch + 1)], in_=cumt[:]), b.gd, 16)
                    b.wait("vector", b.gd, gv)
                b.inc(ve.tensor_tensor(sc2[:], cumt[:], r_sb[:, 512 * ch:512 * (ch + 1)],
                                       ALU.mult), b.dv)
                dvv = b.inc(ve.tensor_single_scalar(ysq[:, 512 * ch:512 * (ch + 1)], sc2[:],
                                             lnw_c[:, j:j + 1], ALU.mult), b.dv)
                b.wait("scalar", b.dv, dvv)
                acv = b.inc(sc.activation(out=sc2[:], in_=sc2[:], func=AF.Square), b.ac)
                b.wait("vector", b.ac, acv)
                b.wait("vector", b.pe)
                dvv = b.inc(ve.tensor_copy(sqb[:], sc2[:]), b.dv)
                b.wait("tensor", b.dv, dvv)
                mv = b.inc(te.matmul(ps_row[:, :], ones_col[:, :], sqb[:],
                                     start=True, stop=True), b.pe)
                b.wait("vector", b.pe, mv)
                b.inc(ve.tensor_tensor(ssq_row[:, 512 * ch:512 * (ch + 1)],
                                       ssq_row[:, 512 * ch:512 * (ch + 1)],
                                       ps_row[:, :], ALU.add), b.dv)
            dvv = b.dv.v
            b.wait("gpsimd", b.dv, dvv)
            gv = b.inc(gp.dma_start(out=y_dram.ap()[128 * j:128 * (j + 1), :],
                                    in_=ysq[:, :]), b.gd, 16)
            b.wait("vector", b.gd, gv)

        b.wait("scalar", b.dv)
        b.inc(sc.activation(out=ssq_row[:], in_=ssq_row[:], func=AF.Ln,
                            bias=cc[0:1, 1:2], scale=cc[0:1, 3:4]), b.ac)
        acv = b.inc(sc.activation(out=ssq_row[:], in_=ssq_row[:], func=AF.Exp, scale=cc[0:1, 2:3]), b.ac)
        b.wait("gpsimd", b.ac, acv)
        gv = b.inc(gp.dma_start(out=row_scr.ap(), in_=ssq_row[:]), b.gd, 16)
        b.wait("sync", b.gd, gv)
        ld = b.inc(sy.dma_start(out=rstd_blk[:, :],
                                in_=row_scr.ap().rearrange("a (c p) -> (a c) p", p=128)), b.sd, 16)
        b.wait("tensor", b.sd, ld)
        tv = b.inc(te.transpose(ps_t32[:, :], rstd_blk[:, :], identf[:, :]), b.pe)
        b.wait("vector", b.pe, tv)
        b.inc(ve.tensor_copy(rstd_cols[:, :], ps_t32[:, :]), b.dv)

        if dbg:
            b.wait("gpsimd", b.dv)
            gv = b.inc(gp.dma_start(out=dbg_ssq, in_=ssq_row[:, :]), b.gd, 16)
            gv = b.inc(gp.dma_start(out=dbg_rstd, in_=rstd_cols[:, :]), b.gd, 16)
            gv = b.inc(gp.dma_start(out=dbg_y, in_=y_dram.ap()), b.gd, 16)
            b.wait("gpsimd", b.gd, gv)
            b.wait("vector", b.gd, gv)
            b.wait("scalar", b.gd, gv)
        wo_sb = wT_r
        b.wait("sync", b.pe)
        for kk in range(ND):
            b.inc(sy.dma_start(out=wo_sb[:, D * kk:D * (kk + 1)],
                               in_=wT_o_dram.ap()[128 * kk:128 * (kk + 1), :]), b.sd, 16)
        for g in range(T // 256):
            b.wait("sync", b.pe)
            for kk in range(ND):
                b.inc(sy.dma_start(
                    out=ybuf[:, 256 * kk:256 * (kk + 1)],
                    in_=y_dram.ap()[128 * kk:128 * (kk + 1), 256 * g:256 * (g + 1)]), b.sd, 16)
            b.wait("tensor", b.sd)
            b.wait("tensor", b.dv)
            for tc in range(2):
                for nh in range(2):
                    pb = 2 * tc + nh
                    for kk in range(ND):
                        mv = b.inc(te.matmul(
                            ps_mm[pb][:],
                            ybuf[:, 256 * kk + 128 * tc: 256 * kk + 128 * (tc + 1)],
                            wo_sb[:, D * kk + 512 * nh: D * kk + 512 * (nh + 1)],
                            start=(kk == 0), stop=(kk == ND - 1)), b.pe)
            mm_done = b.pe.v
            for tc in range(2):
                tok0 = 256 * g + 128 * tc
                tcol = tok0 // 128
                b.wait("sync", b.dv)
                ldx = b.inc(sy.dma_start(out=xtk[:], in_=x_in[tok0:tok0 + 128, :]), b.sd, 16)
                b.wait("vector", b.sd, ldx)
                b.wait("vector", b.pe, mm_done)
                for nh in range(2):
                    pb = 2 * tc + nh
                    b.inc(ve.scalar_tensor_tensor(
                        x1tk[:, 512 * nh:512 * (nh + 1)], ps_mm[pb][:],
                        rstd_cols[:, tcol:tcol + 1], xtk[:, 512 * nh:512 * (nh + 1)],
                        ALU.mult, ALU.add), b.dv)
                dvv = b.dv.v
                b.wait("gpsimd", b.dv, dvv)
                gv = b.inc(gp.dma_start(out=x1_out[tok0:tok0 + 128, :], in_=x1tk[:]), b.gd, 16)
                b.wait("vector", b.gd, gv)

        for i in range(12):
            b.wait("sync", b.dv)
            ld = b.inc(sy.dma_start(out=wrow[:], in_=qkvw[128 * i:128 * (i + 1), :]), b.sd, 16)
            b.wait("vector", b.sd, ld)
            b.wait("vector", b.pe)
            dvv = b.inc(ve.tensor_copy(wq_b[:], wrow[:]), b.dv)
            b.wait("tensor", b.dv, dvv)
            for j in range(ND):
                pj = j % 2
                tv = b.inc(te.transpose(ps_tr[pj][:], wq_b[:, 128 * j:128 * (j + 1)], identb[:]), b.pe)
                b.wait("vector", b.pe, tv)
                dvv = b.inc(ve.tensor_copy(wT_qkv[:, 1536 * j + 128 * i: 1536 * j + 128 * (i + 1)],
                                           ps_tr[pj][:]), b.dv)
                b.wait("tensor", b.dv, dvv)

        b.wait("vector", b.pe)
        for t in range(NT):
            b.wait("sync", b.dv)
            b.wait("sync", b.ac)
            ld = b.inc(sy.dma_start(out=x1tk[:], in_=x1_out[128 * t:128 * (t + 1), :]), b.sd, 16)
            b.wait("vector", b.sd, ld)
            b.wait("scalar", b.sd, ld)
            emit_ln_chunk(b, x1tk, htk, stats, sq_sc, cc)
            emit_ln_chunk(b, htk, ntk, stats, sq_sc, cc)
            emit_tr_to_fm(b, ntk, act_fm, t, identb, ps_tr, affine=(ln1w_c, ln1b_c))
            b.wait("tensor", b.dv)
            for nh in range(3):
                pb = nh
                for kk in range(ND):
                    mv = b.inc(te.matmul(
                        ps_mm[pb][:],
                        act_fm[:, T * kk + 128 * t: T * kk + 128 * (t + 1)],
                        wT_qkv[:, 1536 * kk + 512 * nh: 1536 * kk + 512 * (nh + 1)],
                        start=(kk == 0), stop=(kk == ND - 1)), b.pe)
                b.wait("vector", b.pe, mv)
                b.inc(ve.tensor_copy(qkvtk[:, 512 * nh:512 * (nh + 1)], ps_mm[pb][:]), b.dv)
            dvv = b.dv.v
            b.wait("gpsimd", b.dv, dvv)
            gv = b.inc(gp.dma_start(out=qkv_out[128 * t:128 * (t + 1), :], in_=qkvtk[:]), b.gd, 16)
            b.wait("vector", b.gd, gv)
            b.wait("tensor", b.gd, gv)

        if dbg:
            b.wait("gpsimd", b.dv)
            gv = b.inc(gp.dma_start(out=dbg_xln1, in_=act_fm[:, :]), b.gd, 16)
            b.wait("gpsimd", b.gd, gv)
        b.wait("sync", b.gd)
        b.wait("sync", b.sd)
        b.realize()
    return nc


def _run_launch1(inputs, rows, dbg=False):
    nc = build_launch1(dbg=dbg)
    def m(row):
        return {
            "x": np.ascontiguousarray(row, np.float32),
            "Wr": inputs["L0_attn_Wr"], "Wk": inputs["L0_attn_Wk"],
            "Wv": inputs["L0_attn_Wv"], "Wo": inputs["L0_attn_Wo"],
            "decay": inputs["L0_attn_decay"].reshape(1, -1),
            "ln_w": inputs["L0_attn_ln_w"].reshape(1, -1),
            "ln1_w": inputs["L0_ffn_ln1_weight"].reshape(1, -1),
            "ln1_b": inputs["L0_ffn_ln1_bias"].reshape(1, -1),
            "qkvw": inputs["L0_ffn_qkv_weight"],
        }
    in_maps = [m(r) for r in rows]
    res = run_bass_kernel_spmd(nc, in_maps, core_ids=list(range(len(rows))))
    return res



def build_launch2():
    nc = bass.Bass("TRN2", target_bir_lowering=False)
    qkvs = nc.dram_tensor("qkvs", [T, 1536], BF16, kind="ExternalInput").ap()
    projw = nc.dram_tensor("projw", [D, D], F32, kind="ExternalInput").ap()
    po_out = nc.dram_tensor("po", [T, D], F32, kind="ExternalOutput").ap()

    NS = 8
    BL = 512
    NBT = BL // 128
    SCALE = 1.0 / 8.0

    with ExitStack() as ctx:
        b = B(nc, ctx)
        idb_dram = nc.inline_tensor(np.eye(128, dtype=BF16_NP), "ident_bf")
        identb = b.sbuf("identb", [128, 128], BF16)
        qt = [b.sbuf(f"qt{s}", [128, 1536], BF16) for s in range(NS)]
        scor = [b.sbuf(f"scor{s}", [128, 128], F32) for s in range(NS)]
        prod = b.sbuf("prod", [128, 1024], BF16)
        exps = b.sbuf("exps", [128, 128], F32)
        mx = b.sbuf("mx", [128, 16], F32)
        nmx = b.sbuf("nmx", [128, 16], F32)
        den = b.sbuf("den", [128, 16], F32)
        rden = b.sbuf("rden", [128, 16], F32)
        probs = b.sbuf("probs", [128, 128], BF16)
        tmp = b.sbuf("tmp", [128, 1024], F32)
        acc = b.sbuf("acc", [128, 1024], F32)
        attnb = b.sbuf("attnb", [128, 1024], BF16)
        attn_fm = b.sbuf("attn_fm", [128, ND * 128], BF16)
        potk = b.sbuf("potk", [128, D], F32)
        projT = b.sbuf("projT", [128, ND * D], BF16)
        wrow = b.sbuf("wrow", [128, D], F32)
        wq_b = b.sbuf("wq_b", [128, D], BF16)
        ntk128 = b.sbuf("ntk128", [128, 128], BF16)
        ps_tr = [b.psum(f"ps_tr{i}", [128, 128], BF16) for i in range(2)]
        ps_mm = [b.psum(f"ps_mm{i}", [128, 512], F32) for i in range(2)]

        sy, gp, ve, sc, te = b.sy, b.gp, b.ve, b.sc, b.te

        sy.dma_start(out=identb[:], in_=idb_dram.ap())

        for i in range(ND):
            sy.dma_start(out=wrow[:], in_=projw[128 * i:128 * (i + 1), :])
            ve.tensor_copy(wq_b[:], wrow[:])
            for j in range(ND):
                pj = j % 2
                te.transpose(ps_tr[pj][:], wq_b[:, 128 * j:128 * (j + 1)], identb[:])
                ve.tensor_copy(projT[:, D * j + 128 * i: D * j + 128 * (i + 1)], ps_tr[pj][:])

        def q3(s):
            return qt[s][:, 0:1024].rearrange("p (h d) -> p h d", d=64)

        def k3(t):
            a = qt[t][:, 1024:1280].rearrange("p (g o d) -> p g o d", o=1, d=64)
            return a.broadcast_to([128, 4, 4, 64])

        def v3(t):
            a = qt[t][:, 1280:1536].rearrange("p (g o d) -> p g o d", o=1, d=64)
            return a.broadcast_to([128, 4, 4, 64])

        for bt in range(NBT):
            for s in range(NS):
                sy.dma_start(out=qt[s][:], in_=qkvs[512 * s + 128 * bt: 512 * s + 128 * (bt + 1), :])
            for s in range(NS):
                ve.memset(scor[s][:], -1e30)
                for t in range(s + 1):
                    ve.tensor_tensor(prod[:].rearrange("p (h d) -> p h d", d=64),
                                     q3(s), k3(t), ALU.mult)
                    ve.tensor_reduce(
                        scor[s].rearrange("p (h t) -> p h t", t=8)[:, :, t:t + 1],
                        prod[:].rearrange("p (h d) -> p h d", d=64),
                        axis=AX.X, op=ALU.add)
                sc3 = scor[s].rearrange("p (h t) -> p h t", t=8)
                ve.tensor_single_scalar(scor[s][:], scor[s][:], SCALE, ALU.mult)
                ve.reduce_max(mx[:], sc3, axis=AX.X, negate=True)
                ve.tensor_tensor(exps[:].rearrange("p (h t) -> p h t", t=8), sc3,
                                 mx[:].rearrange("p (h o) -> p h o", o=1)
                                 .broadcast_to([128, 16, 8]), ALU.add)
                sc.activation(out=exps[:], in_=exps[:], func=AF.Exp)
                ve.reduce_sum(den[:], exps[:].rearrange("p (h t) -> p h t", t=8), axis=AX.X)
                ve.reciprocal(rden[:], den[:])
                ve.tensor_tensor(probs[:].rearrange("p (h t) -> p h t", t=8),
                                 exps[:].rearrange("p (h t) -> p h t", t=8),
                                 rden[:].rearrange("p (h o) -> p h o", o=1)
                                 .broadcast_to([128, 16, 8]), ALU.mult)
                ve.memset(acc[:], 0.0)
                for t in range(s + 1):
                    ve.tensor_tensor(tmp[:].rearrange("p (g o d) -> p g o d", g=4, o=4),
                                     v3(t),
                                     probs[:].rearrange("p (h t) -> p h t", t=8)[:, :, t:t + 1]
                                     .rearrange("p (g o) t -> p g o t", o=4)
                                     .broadcast_to([128, 4, 4, 64]),
                                     ALU.mult)
                    ve.tensor_add(acc[:], acc[:], tmp[:])
                ve.tensor_copy(attnb[:], acc[:])
                for j in range(ND):
                    pj = j % 2
                    te.transpose(ps_tr[pj][:], attnb[:, 128 * j:128 * (j + 1)], identb[:])
                    ve.tensor_copy(attn_fm[:, 128 * j:128 * (j + 1)], ps_tr[pj][:])
                for nh in range(2):
                    for kk in range(ND):
                        te.matmul(ps_mm[nh][:],
                                  attn_fm[:, 128 * kk:128 * (kk + 1)],
                                  projT[:, D * kk + 512 * nh: D * kk + 512 * (nh + 1)],
                                  start=(kk == 0), stop=(kk == ND - 1))
                    ve.tensor_copy(potk[:, 512 * nh:512 * (nh + 1)], ps_mm[nh][:])
                gp.dma_start(out=po_out[512 * s + 128 * bt: 512 * s + 128 * (bt + 1), :],
                             in_=potk[:])
        b.barrier("sync")
        b.realize()
    return nc


def _run_launch2(qkv_all, projw, cores=8):
    nc = build_launch2()
    in_maps = []
    for j in range(cores):
        sl = np.ascontiguousarray(
            qkv_all[:, 512 * j:512 * (j + 1), :].reshape(T, 1536))
        in_maps.append({"qkvs": sl.astype(BF16_NP),
                        "projw": np.ascontiguousarray(projw, np.float32)})
    return run_bass_kernel_spmd(nc, in_maps, core_ids=list(range(cores)))



def build_launch3a():
    nc = bass.Bass("TRN2", target_bir_lowering=False)
    x1_in = nc.dram_tensor("x1", [T, D], F32, kind="ExternalInput").ap()
    po_in = nc.dram_tensor("po", [T, D], F32, kind="ExternalInput").ap()
    ln2_w = nc.dram_tensor("ln2_w", [1, D], F32, kind="ExternalInput").ap()
    ln2_b = nc.dram_tensor("ln2_b", [1, D], F32, kind="ExternalInput").ap()
    routw = nc.dram_tensor("routw", [8, D], F32, kind="ExternalInput").ap()
    moe_w1 = nc.dram_tensor("moe_w1", [8, D, 2048], F32, kind="ExternalInput").ap()
    moe_w2 = nc.dram_tensor("moe_w2", [8, 2048, D], F32, kind="ExternalInput").ap()
    x2_out = nc.dram_tensor("x2", [T, D], F32, kind="ExternalOutput").ap()

    NE, F = 8, 2048
    NFT = F // 128

    with ExitStack() as ctx:
        b = B(nc, ctx)
        idb_dram = nc.inline_tensor(np.eye(128, dtype=BF16_NP), "ident_bf")
        cc_np = np.zeros((128, 8), np.float32)
        cc_np[:, 0] = LNEPS; cc_np[:, 1] = RMSEPS; cc_np[:, 2] = -0.5; cc_np[:, 3] = 1.0 / D
        cc_dram = nc.inline_tensor(cc_np, "consts_cc")
        identb = b.sbuf("identb", [128, 128], BF16)
        cc = b.sbuf("cc", [128, 8], F32)
        xln2_fm = b.sbuf("xln2_fm", [128, ND * T], BF16)
        w1sb = b.sbuf("w1sb", [128, ND * F], BF16)
        w2sb = b.sbuf("w2sb", [128, NFT * D], BF16)
        h_sb = b.sbuf("h_sb", [128, NFT * 512], BF16)
        routT = b.sbuf("routT", [128, ND * 8], BF16)
        wts_all = b.sbuf("wts_all", [128, 8 * NT], F32)
        ln2w_c = b.sbuf("ln2w_c", [128, ND], F32)
        ln2b_c = b.sbuf("ln2b_c", [128, ND], F32)
        x1tk = b.sbuf("x1tk", [128, D], F32)
        potk = b.sbuf("potk", [128, D], F32)
        htk = b.sbuf("htk", [128, D], F32)
        hattk = b.sbuf("hattk", [128, D], F32)
        x2ptk = b.sbuf("x2ptk", [128, D], F32)
        ntk = b.sbuf("ntk", [128, D], BF16)
        h2tk = b.sbuf("h2tk", [128, D], F32)
        stats = b.sbuf("stats", [128, 8], F32)
        sq_sc = b.sbuf("sq_sc", [128, D], F32)
        glog = b.sbuf("glog", [128, 8], F32)
        gexp = b.sbuf("gexp", [128, 8], F32)
        gcols = b.sbuf("gcols", [128, 16], F32)
        wrow = b.sbuf("wrow", [128, D], F32)
        wq_b = b.sbuf("wq_b", [128, D], BF16)
        ps_tr = [b.psum(f"ps_tr{i}", [128, 128], BF16) for i in range(2)]
        ps_h = [b.psum(f"ps_h{i}", [128, 512], F32) for i in range(2)]
        ps_o = [b.psum(f"ps_o{i}", [128, 512], F32) for i in range(2)]
        ps_r = b.psum("ps_r", [128, 8], F32)

        sy, gp, ve, sc, te = b.sy, b.gp, b.ve, b.sc, b.te

        sy.dma_start(out=identb[:], in_=idb_dram.ap())
        sy.dma_start(out=cc[:], in_=cc_dram.ap())
        emit_param_cols(b, ln2_w, ln2w_c)
        emit_param_cols(b, ln2_b, ln2b_c)
        sy.dma_start(out=wrow[0:8, :], in_=routw)
        ve.tensor_copy(wq_b[0:8, :], wrow[0:8, :])
        for j in range(ND):
            pj = j % 2
            te.transpose(ps_tr[pj][:, 0:8], wq_b[0:8, 128 * j:128 * (j + 1)], identb[0:8, 0:8])
            ve.tensor_copy(routT[:, 8 * j:8 * (j + 1)], ps_tr[pj][:, 0:8])

        for t in range(NT):
            sy.dma_start(out=x1tk[:], in_=x1_in[128 * t:128 * (t + 1), :])
            sy.dma_start(out=potk[:], in_=po_in[128 * t:128 * (t + 1), :])
            emit_ln_chunk(b, x1tk, htk, stats, sq_sc, cc)
            ve.tensor_add(hattk[:], htk[:], potk[:])
            ve.tensor_add(x2ptk[:], x1tk[:], hattk[:])
            gp.dma_start(out=x2_out[128 * t:128 * (t + 1), :], in_=x2ptk[:])
            emit_ln_chunk(b, hattk, ntk, stats, sq_sc, cc)
            emit_tr_to_fm(b, ntk, xln2_fm, t, identb, ps_tr, affine=(ln2w_c, ln2b_c))
            for kk in range(ND):
                te.matmul(ps_r[:], xln2_fm[:, T * kk + 128 * t: T * kk + 128 * (t + 1)],
                          routT[:, 8 * kk:8 * (kk + 1)],
                          start=(kk == 0), stop=(kk == ND - 1))
            ve.tensor_copy(glog[:], ps_r[:])
            ve.reduce_max(gcols[:, 0:1], glog[:], axis=AX.X, negate=True)
            ve.tensor_tensor(gexp[:], glog[:],
                             gcols[:, 0:1].broadcast_to([128, 8]), ALU.add)
            sc.activation(out=gexp[:], in_=gexp[:], func=AF.Exp)
            ve.reduce_sum(gcols[:, 1:2], gexp[:], axis=AX.X)
            ve.reciprocal(gcols[:, 2:3], gcols[:, 1:2])
            ve.tensor_tensor(glog[:], gexp[:],
                             gcols[:, 2:3].broadcast_to([128, 8]), ALU.mult)
            ve.reduce_max(gcols[:, 3:4], glog[:], axis=AX.X)
            ve.tensor_single_scalar(gexp[:], glog[:], gcols[:, 3:4], ALU.is_ge)
            ve.reduce_sum(gcols[:, 4:5], gexp[:], axis=AX.X)
            ve.scalar_tensor_tensor(gexp[:], gexp[:], -1e30, glog[:], ALU.mult, ALU.add)
            ve.reduce_max(gcols[:, 5:6], gexp[:], axis=AX.X)
            ve.tensor_single_scalar(gcols[:, 6:7], gcols[:, 4:5], 2.0, ALU.is_ge)
            ve.tensor_sub(gcols[:, 7:8], gcols[:, 3:4], gcols[:, 5:6])
            ve.tensor_mul(gcols[:, 7:8], gcols[:, 6:7], gcols[:, 7:8])
            ve.tensor_add(gcols[:, 5:6], gcols[:, 5:6], gcols[:, 7:8])
            ve.tensor_single_scalar(gexp[:], glog[:], gcols[:, 5:6], ALU.is_ge)
            ve.tensor_tensor(wts_all[:, 8 * t:8 * (t + 1)], glog[:], gexp[:], ALU.mult)

        for e in range(NE):
            for kk in range(ND):
                gp.dma_start(out=w1sb[:, F * kk:F * (kk + 1)],
                             in_=moe_w1[e, 128 * kk:128 * (kk + 1), :])
            for ft in range(NFT):
                gp.dma_start(out=w2sb[:, D * ft:D * (ft + 1)],
                             in_=moe_w2[e, 128 * ft:128 * (ft + 1), :])
            for g in range(T // 512):
                for ft in range(NFT):
                    pb = ft % 2
                    for kk in range(ND):
                        te.matmul(ps_h[pb][:],
                                  w1sb[:, F * kk + 128 * ft: F * kk + 128 * (ft + 1)],
                                  xln2_fm[:, T * kk + 512 * g: T * kk + 512 * (g + 1)],
                                  start=(kk == 0), stop=(kk == ND - 1))
                    sc.activation(out=h_sb[:, 512 * ft:512 * (ft + 1)], in_=ps_h[pb][:],
                                  func=AF.Gelu_apprx_tanh)
                for tc in range(4):
                    trow = 4 * g + tc
                    for nh in range(2):
                        pb = nh
                        for ft in range(NFT):
                            te.matmul(ps_o[pb][:],
                                      h_sb[:, 512 * ft + 128 * tc: 512 * ft + 128 * (tc + 1)],
                                      w2sb[:, D * ft + 512 * nh: D * ft + 512 * (nh + 1)],
                                      start=(ft == 0), stop=(ft == NFT - 1))
                        ve.tensor_single_scalar(h2tk[:, 512 * nh:512 * (nh + 1)], ps_o[pb][:],
                                                wts_all[:, 8 * trow + e:8 * trow + e + 1],
                                                ALU.mult)
                    gp.dma_start(out=x2_out[128 * trow:128 * (trow + 1), :], in_=h2tk[:],
                                 accum_op=ALU.add)
        b.barrier("sync")
        b.realize()
    return nc


def _run_launch3a(inputs, x1_all, po_all, cores=8):
    nc = build_launch3a()
    in_maps = []
    for c in range(cores):
        in_maps.append({
            "x1": np.ascontiguousarray(x1_all[c], np.float32),
            "po": np.ascontiguousarray(po_all[c], np.float32),
            "ln2_w": inputs["L0_ffn_ln2_weight"].reshape(1, -1),
            "ln2_b": inputs["L0_ffn_ln2_bias"].reshape(1, -1),
            "routw": inputs["L0_ffn_router_weight"],
            "moe_w1": inputs["L0_ffn_moe_w1"],
            "moe_w2": inputs["L0_ffn_moe_w2"],
        })
    return run_bass_kernel_spmd(nc, in_maps, core_ids=list(range(cores)))



def build_launch3b(stage=5):
    nc = bass.Bass("TRN2", target_bir_lowering=False)
    x2_in = nc.dram_tensor("x2", [T, D], F32, kind="ExternalInput").ap()
    wk_in = nc.dram_tensor("Wk", [D, D], F32, kind="ExternalInput").ap()
    wq_in = nc.dram_tensor("Wq", [D, D], F32, kind="ExternalInput").ap()
    wv_in = nc.dram_tensor("Wv", [D, D], F32, kind="ExternalInput").ap()
    wo_in = nc.dram_tensor("Wo", [D, D], F32, kind="ExternalInput").ap()
    bo_in = nc.dram_tensor("bo", [1, D], F32, kind="ExternalInput").ap()
    x3_out = nc.dram_tensor("x3", [T, D], F32, kind="ExternalOutput").ap()

    NL = 64
    HS = 64
    NPAIR = 8
    QS = 0.125

    with ExitStack() as ctx:
        b = B(nc, ctx)
        idb_dram = nc.inline_tensor(np.eye(128, dtype=BF16_NP), "ident_bf")
        idf_dram = nc.inline_tensor(np.eye(128, dtype=np.float32), "ident_f32")
        cc_np = np.zeros((128, 8), np.float32)
        cc_np[:, 0] = LNEPS; cc_np[:, 2] = -0.5; cc_np[:, 3] = 1.0 / D
        cc_dram = nc.inline_tensor(cc_np, "consts_cc")
        ii_np = np.concatenate([np.eye(64, dtype=np.float32)] * 2, axis=1)
        ii_dram = nc.inline_tensor(ii_np, "eye_sbs")
        onesc_dram = nc.inline_tensor(np.ones((128, 1), dtype=BF16_NP), "ones_col_d")
        onesr_dram = nc.inline_tensor(np.ones((1, 128), dtype=BF16_NP), "ones_row_d")
        onescf_dram = nc.inline_tensor(np.ones((64, 1), dtype=np.float32), "ones_col_f_d")

        identb = b.sbuf("identb", [128, 128], BF16)
        identf = b.sbuf("identf", [128, 128], F32)
        cc = b.sbuf("cc", [128, 8], F32)
        iisb = b.sbuf("iisb", [64, 128], F32)
        ones_col = b.sbuf("ones_col", [128, 1], BF16)
        ones_row = b.sbuf("ones_row", [1, 128], BF16)
        ones_col_f = b.sbuf("ones_col_f", [64, 1], F32)
        xln3_fm = b.sbuf("xln3_fm", [128, ND * T], BF16)
        attn_all = b.sbuf("attn_all", [128, NT * D], BF16)
        qfm2 = b.sbuf("qfm2", [64, 2 * T], BF16)
        kfm2 = b.sbuf("kfm2", [64, 2 * T], BF16)
        vtk = b.sbuf("vtk", [128, NT * 128], BF16)
        e1g = b.sbuf("e1g", [64, 1024], BF16)
        wslc = b.sbuf("wslc", [128, ND * 128], BF16)
        ql2 = b.sbuf("ql2", [64, 2 * NL], F32)
        kl2 = b.sbuf("kl2", [64, 2 * NL], F32)
        qlb2 = b.sbuf("qlb2", [64, 2 * NL], BF16)
        klb2 = b.sbuf("klb2", [64, 2 * NL], BF16)
        s2sm = b.sbuf("s2sm", [64, 128], F32)
        s2t = b.sbuf("s2t", [64, 128], F32)
        zcur = b.sbuf("zcur", [64, 128], F32)
        znew = b.sbuf("znew", [64, 128], F32)
        xz = b.sbuf("xz", [64, 128], F32)
        xzT = b.sbuf("xzT", [64, 128], F32)
        zT = b.sbuf("zT", [64, 128], F32)
        wtmp = b.sbuf("wtmp", [64, 128], F32)
        s3v_sb = b.sbuf("s3v_sb", [64, 132], F32)
        s3vn = b.sbuf("s3vn", [64, 128], F32)
        wsbs = b.sbuf("wsbs", [64, 128], BF16)
        rr2 = b.sbuf("rr2", [64, 8], F32)
        rrow = b.sbuf("rrow", [1, 8], F32)
        rrowb = b.sbuf("rrowb", [1, 8], BF16)
        mxc2 = b.sbuf("mxc2", [64, 8], F32)
        m3c = b.sbuf("m3c", [64, 8], F32)
        gmx2 = b.sbuf("gmx2", [64, 2], F32)
        mxrow = b.sbuf("mxrow", [1, 128], F32)
        mxrowb = b.sbuf("mxrowb", [1, 128], BF16)

        rden1 = b.sbuf("rden1", [128, 2], F32)
        xtk = b.sbuf("xtk", [128, D], F32)
        ntk = b.sbuf("ntk", [128, D], BF16)
        x3tk = b.sbuf("x3tk", [128, D], F32)
        stats = b.sbuf("stats", [128, 8], F32)
        sq_sc = b.sbuf("sq_sc", [128, D], F32)
        bo_rep = b.sbuf("bo_rep", [128, D], F32)
        _qaddr = nc.lookup_mloc(qfm2).addr
        wo_sb_h = nc.alloc_sbuf_tensor_at("wo_sb", [128, ND * D], BF16, offset=_qaddr)
        wo_sb = wo_sb_h
        attn_fm = b.sbuf("attn_fm", [128, ND * 128], BF16)

        ps_tr = [b.psum(f"ps_tr{i}", [128, 128], BF16) for i in range(2)]
        ps_big = [b.psum(f"ps_big{i}", [128, 512], F32) for i in range(2)]
        ps_s3v = b.psum("ps_s3v", [64, 132], F32)
        ps_att = [b.psum(f"ps_att{i}", [128, 132], F32) for i in range(2)]
        ps_sm = b.psum("ps_sm", [128, 128], F32)

        sy, gp, ve, sc, te = b.sy, b.gp, b.ve, b.sc, b.te

        sy.dma_start(out=identb[:], in_=idb_dram.ap())
        sy.dma_start(out=identf[:], in_=idf_dram.ap())
        sy.dma_start(out=cc[:], in_=cc_dram.ap())
        sy.dma_start(out=iisb[:], in_=ii_dram.ap())
        sy.dma_start(out=ones_col[:], in_=onesc_dram.ap())
        sy.dma_start(out=ones_row[:], in_=onesr_dram.ap())
        sy.dma_start(out=ones_col_f[:], in_=onescf_dram.ap())
        sy.dma_start(out=bo_rep[0:1, :], in_=bo_in)
        emit_dma_bcast_rows(b, bo_rep, D)

        for t in range(NT):
            sy.dma_start(out=xtk[:], in_=x2_in[128 * t:128 * (t + 1), :])
            emit_ln_chunk(b, xtk, ntk, stats, sq_sc, cc)
            emit_tr_to_fm(b, ntk, xln3_fm, t, identb, ps_tr)

        if stage >= 2:
            ve.memset(attn_all[:, 0:NT * D], 0.0)
        for hp in range(NPAIR) if stage >= 2 else []:
            c0 = 128 * hp
            for kk in range(ND):
                gp.dma_start(out=wslc[:, 128 * kk:128 * (kk + 1)],
                             in_=wq_in[128 * kk:128 * (kk + 1), c0:c0 + 128])
            for h in range(2):
                for ch in range(NC5):
                    pb = ch % 2
                    for kk in range(ND):
                        te.matmul(ps_big[pb][0:64, 0:512],
                                  wslc[:, 128 * kk + 64 * h: 128 * kk + 64 * h + 64],
                                  xln3_fm[:, T * kk + 512 * ch: T * kk + 512 * (ch + 1)],
                                  start=(kk == 0), stop=(kk == ND - 1))
                    ve.tensor_single_scalar(qfm2[0:64, T * h + 512 * ch: T * h + 512 * (ch + 1)],
                                            ps_big[pb][0:64, 0:512], QS, ALU.mult)
                    ve.tensor_reduce(ql2[0:64, 64 * h + 8 * ch: 64 * h + 8 * (ch + 1)],
                                     qfm2[0:64, T * h + 512 * ch: T * h + 512 * (ch + 1)]
                                     .rearrange("p (l s) -> p l s", s=64),
                                     axis=AX.X, op=ALU.add)
            ve.tensor_single_scalar(ql2[0:64, :], ql2[0:64, :], 1.0 / 64, ALU.mult)
            ve.tensor_copy(qlb2[0:64, :], ql2[0:64, :])
            for kk in range(ND):
                gp.dma_start(out=wslc[:, 128 * kk:128 * (kk + 1)],
                             in_=wk_in[128 * kk:128 * (kk + 1), c0:c0 + 128])
            for h in range(2):
                for ch in range(NC5):
                    pb = ch % 2
                    for kk in range(ND):
                        te.matmul(ps_big[pb][0:64, 0:512],
                                  wslc[:, 128 * kk + 64 * h: 128 * kk + 64 * h + 64],
                                  xln3_fm[:, T * kk + 512 * ch: T * kk + 512 * (ch + 1)],
                                  start=(kk == 0), stop=(kk == ND - 1))
                    ve.tensor_copy(kfm2[0:64, T * h + 512 * ch: T * h + 512 * (ch + 1)],
                                   ps_big[pb][0:64, 0:512])
                    ve.tensor_reduce(kl2[0:64, 64 * h + 8 * ch: 64 * h + 8 * (ch + 1)],
                                     kfm2[0:64, T * h + 512 * ch: T * h + 512 * (ch + 1)]
                                     .rearrange("p (l s) -> p l s", s=64),
                                     axis=AX.X, op=ALU.add)
            ve.tensor_single_scalar(kl2[0:64, :], kl2[0:64, :], 1.0 / 64, ALU.mult)
            ve.tensor_copy(klb2[0:64, :], kl2[0:64, :])
            for kk in range(ND):
                gp.dma_start(out=wslc[:, 128 * kk:128 * (kk + 1)],
                             in_=wv_in[128 * kk:128 * (kk + 1), c0:c0 + 128])
            for t in range(NT):
                pb = t % 2
                for kk in range(ND):
                    te.matmul(ps_big[pb][:, 0:128],
                              xln3_fm[:, T * kk + 128 * t: T * kk + 128 * (t + 1)],
                              wslc[:, 128 * kk:128 * (kk + 1)],
                              start=(kk == 0), stop=(kk == ND - 1))
                ve.tensor_copy(vtk[:, 128 * t:128 * (t + 1)], ps_big[pb][:, 0:128])

            if stage < 3:
                continue
            for h in range(2):
                te.matmul(ps_sm[0:64, 64 * h:64 * (h + 1)],
                          qlb2[0:64, 64 * h:64 * (h + 1)], klb2[0:64, 64 * h:64 * (h + 1)],
                          start=(h == 0), stop=(h == 1))
            for h in range(2):
                sl = ps_sm[0:64, 64 * h:64 * (h + 1)]
                ve.reduce_max(mxc2[0:64, h:h + 1], sl, axis=AX.X, negate=True)
                sc.activation(out=s2sm[:, 64 * h:64 * (h + 1)], in_=sl, func=AF.Exp,
                              bias=mxc2[0:64, h:h + 1])
                ve.reduce_sum(mxc2[0:64, 2 + h:3 + h], s2sm[:, 64 * h:64 * (h + 1)],
                              axis=AX.X)
                ve.reciprocal(mxc2[0:64, 4 + h:5 + h], mxc2[0:64, 2 + h:3 + h])
                ve.tensor_single_scalar(s2sm[:, 64 * h:64 * (h + 1)],
                                        s2sm[:, 64 * h:64 * (h + 1)],
                                        mxc2[0:64, 4 + h:5 + h], ALU.mult)
            if stage == 30:
                continue
            for h in range(2):
                te.transpose(ps_att[0][0:64, 64 * h:64 * (h + 1)],
                             s2sm[:, 64 * h:64 * (h + 1)], identf[0:64, 0:64])
            ve.tensor_copy(s2t[:], ps_att[0][0:64, 0:128])
            if stage == 31:
                continue
            te.matmul(ps_att[1][0:1, 0:128], ones_col_f[:, :],
                      s2sm[:, :], start=True, stop=True)
            ve.tensor_copy(mxrow[:, :], ps_att[1][0:1, 0:128])
            ve.reduce_max(rrow[0:1, 0:2], mxrow[:, :].rearrange("p (h n) -> p h n", n=64),
                          axis=AX.X)
            ve.tensor_copy(rrowb[0:1, 0:2], rrow[0:1, 0:2])
            te.matmul(ps_att[1][0:64, 0:2], ones_row[0:1, 0:64], rrowb[0:1, 0:2],
                      start=True, stop=True)
            ve.tensor_copy(rr2[:, 0:2], ps_att[1][0:64, 0:2])
            ve.reciprocal(rr2[:, 2:4], rr2[:, 0:2])
            for h in range(2):
                ve.tensor_single_scalar(zcur[:, 64 * h:64 * (h + 1)],
                                        s2t[:, 64 * h:64 * (h + 1)],
                                        rr2[:, 2 + h:3 + h], ALU.mult)
            if stage == 32:
                continue
            for it in range(6):
                for h in range(2):
                    te.matmul(ps_sm[0:64, 64 * h:64 * (h + 1)],
                              s2t[:, 64 * h:64 * (h + 1)],
                              zcur[:, 64 * h:64 * (h + 1)], start=(h == 0), stop=(h == 1))
                ve.tensor_copy(xz[:], ps_sm[0:64, 0:128])
                if stage == 33:
                    break
                for h in range(2):
                    te.transpose(ps_sm[0:64, 64 * h:64 * (h + 1)],
                                 xz[:, 64 * h:64 * (h + 1)], identf[0:64, 0:64])
                ve.tensor_copy(xzT[:], ps_sm[0:64, 0:128])
                ve.scalar_tensor_tensor(wtmp[:], xz[:], -1.0 / 7.0, iisb[:],
                                        ALU.mult, ALU.add)
                ve.tensor_single_scalar(wtmp[:], wtmp[:], 7.0, ALU.mult)
                for h in range(2):
                    te.matmul(ps_sm[0:64, 64 * h:64 * (h + 1)],
                              xzT[:, 64 * h:64 * (h + 1)],
                              wtmp[:, 64 * h:64 * (h + 1)], start=(h == 0), stop=(h == 1))
                ve.tensor_copy(wtmp[:], ps_sm[0:64, 0:128])
                ve.scalar_tensor_tensor(wtmp[:], wtmp[:], -1.0 / 15.0, iisb[:],
                                        ALU.mult, ALU.add)
                ve.tensor_single_scalar(wtmp[:], wtmp[:], 15.0, ALU.mult)
                for h in range(2):
                    te.matmul(ps_sm[0:64, 64 * h:64 * (h + 1)],
                              xzT[:, 64 * h:64 * (h + 1)],
                              wtmp[:, 64 * h:64 * (h + 1)], start=(h == 0), stop=(h == 1))
                ve.tensor_copy(wtmp[:], ps_sm[0:64, 0:128])
                ve.scalar_tensor_tensor(wtmp[:], wtmp[:], -1.0 / 13.0, iisb[:],
                                        ALU.mult, ALU.add)
                ve.tensor_single_scalar(wtmp[:], wtmp[:], 13.0, ALU.mult)
                for h in range(2):
                    te.transpose(ps_sm[0:64, 64 * h:64 * (h + 1)],
                                 zcur[:, 64 * h:64 * (h + 1)], identf[0:64, 0:64])
                ve.tensor_copy(zT[:], ps_sm[0:64, 0:128])
                for h in range(2):
                    te.matmul(ps_sm[0:64, 64 * h:64 * (h + 1)],
                              zT[:, 64 * h:64 * (h + 1)],
                              wtmp[:, 64 * h:64 * (h + 1)], start=(h == 0), stop=(h == 1))
                ve.tensor_single_scalar(zcur[:], ps_sm[0:64, 0:128], 0.25, ALU.mult)

            if stage < 4:
                continue
            for h in range(2):
                for ch in range(NC5):
                    pb = ch % 2
                    te.matmul(ps_big[pb][0:64, 0:512],
                              qlb2[0:64, 64 * h:64 * (h + 1)],
                              kfm2[0:64, T * h + 512 * ch: T * h + 512 * (ch + 1)],
                              start=True, stop=True)
                    if ch == 0:
                        ve.reduce_max(m3c[0:64, h:h + 1], ps_big[pb][0:64, 0:512], axis=AX.X)
                    else:
                        ve.reduce_max(m3c[0:64, 2:3], ps_big[pb][0:64, 0:512], axis=AX.X)
                        ve.tensor_max(m3c[0:64, h:h + 1], m3c[0:64, h:h + 1], m3c[0:64, 2:3])
            for h in range(2):
                te.transpose(ps_sm[0:1, 64 * h:64 * h + 64], m3c[0:64, h:h + 1],
                             identf[0:64, 0:64])
            ve.tensor_single_scalar(mxrow[0:1, 0:128], ps_sm[0:1, 0:128], -1.0, ALU.mult)
            ve.tensor_copy(mxrowb[:, :], mxrow[:, :])
            for t in range(NT):
                pb = t % 2
                for h in range(2):
                    te.matmul(ps_att[pb][:, 64 * h:64 * (h + 1)],
                              kfm2[0:64, T * h + 128 * t: T * h + 128 * (t + 1)],
                              qlb2[0:64, 64 * h:64 * (h + 1)], start=(h == 0), stop=False)
                    te.matmul(ps_att[pb][:, 64 * h:64 * (h + 1)],
                              ones_row[0:1, 0:128],
                              mxrowb[0:1, 64 * h:64 * (h + 1)], start=False,
                              stop=(h == 1))
                sc.activation(out=ntk[:, 0:128], in_=ps_att[pb][:, 0:128], func=AF.Exp)
                for h in range(2):
                    te.matmul(ps_s3v[0:64, 66 * h:66 * h + 64],
                              ntk[:, 64 * h:64 * (h + 1)],
                              vtk[:, 128 * t:128 * (t + 1)][:, 64 * h:64 * (h + 1)],
                              start=(t == 0 and h == 0), stop=False)
                    te.matmul(ps_s3v[0:64, 66 * h + 64:66 * h + 65],
                              ntk[:, 64 * h:64 * (h + 1)],
                              ones_col[:, 0:1], start=False,
                              stop=(t == NT - 1 and h == 1))
            for h in range(2):
                ve.tensor_copy(s3v_sb[:, 66 * h:66 * h + 65],
                               ps_s3v[0:64, 66 * h:66 * h + 65])
            for h in range(2):
                ve.reciprocal(rr2[:, 4 + h:5 + h], s3v_sb[:, 66 * h + 64:66 * h + 65])
                ve.tensor_single_scalar(s3vn[:, 64 * h:64 * (h + 1)],
                                        s3v_sb[:, 66 * h:66 * h + 64],
                                        rr2[:, 4 + h:5 + h], ALU.mult)
            for h in range(2):
                te.transpose(ps_sm[0:64, 64 * h:64 * (h + 1)],
                             zcur[:, 64 * h:64 * (h + 1)], identf[0:64, 0:64])
            ve.tensor_copy(zT[:], ps_sm[0:64, 0:128])
            for h in range(2):
                te.matmul(ps_att[0][0:64, 64 * h:64 * (h + 1)],
                          zT[:, 64 * h:64 * (h + 1)], s3vn[:, 64 * h:64 * (h + 1)],
                          start=(h == 0), stop=(h == 1))
            ve.tensor_copy(wsbs[:, :], ps_att[0][0:64, 0:128])

            if stage < 5:
                continue
            for h in range(2):
                for ch in range(NC5):
                    pb = ch % 2
                    te.matmul(ps_big[pb][0:64, 0:512],
                              klb2[0:64, 64 * h:64 * (h + 1)],
                              qfm2[0:64, T * h + 512 * ch: T * h + 512 * (ch + 1)],
                              start=True, stop=True)
                    if ch == 0:
                        ve.reduce_max(m3c[0:64, 4 + h:5 + h], ps_big[pb][0:64, 0:512],
                                      axis=AX.X)
                    else:
                        ve.reduce_max(m3c[0:64, 6:7], ps_big[pb][0:64, 0:512], axis=AX.X)
                        ve.tensor_max(m3c[0:64, 4 + h:5 + h], m3c[0:64, 4 + h:5 + h],
                                      m3c[0:64, 6:7])
            for h in range(2):
                te.transpose(ps_sm[0:1, 64 * h:64 * h + 64], m3c[0:64, 4 + h:5 + h],
                             identf[0:64, 0:64])
            ve.tensor_copy(mxrow[0:1, 0:128], ps_sm[0:1, 0:128])
            ve.reduce_max(rrow[0:1, 4:6],
                          mxrow[:, :].rearrange("p (h n) -> p h n", n=64), axis=AX.X)
            ve.tensor_copy(rrowb[0:1, 4:6], rrow[0:1, 4:6])
            te.matmul(ps_att[1][0:64, 4:6], ones_row[0:1, 0:64], rrowb[0:1, 4:6],
                      start=True, stop=True)
            ve.tensor_single_scalar(gmx2[0:64, 0:2], ps_att[1][0:64, 4:6], -1.0, ALU.mult)
            for ch in range(NC5):
                for h in range(2):
                    pb = h
                    te.matmul(ps_big[pb][0:64, 0:512],
                              klb2[0:64, 64 * h:64 * (h + 1)],
                              qfm2[0:64, T * h + 512 * ch: T * h + 512 * (ch + 1)],
                              start=True, stop=True)
                    sc.activation(out=e1g[0:64, 512 * h:512 * (h + 1)],
                                  in_=ps_big[pb][0:64, 0:512],
                                  func=AF.Exp, bias=gmx2[0:64, h:h + 1])
                for tc in range(4):
                    t = 4 * ch + tc
                    pb = tc % 2
                    for h in range(2):
                        te.matmul(ps_att[pb][:, 66 * h:66 * h + 64],
                                  e1g[0:64, 512 * h + 128 * tc: 512 * h + 128 * (tc + 1)],
                                  wsbs[:, 64 * h:64 * (h + 1)], start=(h == 0), stop=False)
                        te.matmul(ps_att[pb][:, 66 * h + 64:66 * h + 65],
                                  e1g[0:64, 512 * h + 128 * tc: 512 * h + 128 * (tc + 1)],
                                  ones_col[0:64, 0:1], start=False, stop=(h == 1))
                    for h in range(2):
                        ve.reciprocal(rden1[:, h:h + 1], ps_att[pb][:, 66 * h + 64:66 * h + 65])
                        ve.tensor_single_scalar(
                            attn_all[:, D * t + c0 + 64 * h: D * t + c0 + 64 * (h + 1)],
                            ps_att[pb][:, 66 * h:66 * h + 64],
                            rden1[:, h:h + 1], ALU.mult)

        b.barrier("gpsimd")
        if stage < 2:
            for t in range(NT):
                sy.dma_start(out=xtk[:], in_=x2_in[128 * t:128 * (t + 1), :])
                ve.tensor_copy(x3tk[:], xtk[:])
                gp.dma_start(out=x3_out[128 * t:128 * (t + 1), :], in_=x3tk[:])
            b.barrier("sync")
            b.realize()
            return nc
        for kk in range(ND):
            gp.dma_start(out=wo_sb[:, D * kk:D * (kk + 1)],
                         in_=wo_in[128 * kk:128 * (kk + 1), :])
        for t in range(NT):
            for j in range(ND):
                pj = j % 2
                te.transpose(ps_tr[pj][:], attn_all[:, D * t + 128 * j: D * t + 128 * (j + 1)],
                             identb[:])
                ve.tensor_copy(attn_fm[:, 128 * j:128 * (j + 1)], ps_tr[pj][:])
            sy.dma_start(out=xtk[:], in_=x2_in[128 * t:128 * (t + 1), :])
            for nh in range(2):
                pb = nh
                for kk in range(ND):
                    te.matmul(ps_big[pb][:],
                              attn_fm[:, 128 * kk:128 * (kk + 1)],
                              wo_sb[:, D * kk + 512 * nh: D * kk + 512 * (nh + 1)],
                              start=(kk == 0), stop=(kk == ND - 1))
                ve.tensor_tensor(x3tk[:, 512 * nh:512 * (nh + 1)], ps_big[pb][:],
                                 xtk[:, 512 * nh:512 * (nh + 1)], ALU.add)
            ve.tensor_add(x3tk[:], x3tk[:], bo_rep[:])
            gp.dma_start(out=x3_out[128 * t:128 * (t + 1), :], in_=x3tk[:])
        b.barrier("sync")
        b.realize()
    return nc


def _run_launch3b(inputs, x2_all, cores=8, stage=5):
    nc = build_launch3b(stage=stage)
    in_maps = []
    for c in range(cores):
        in_maps.append({
            "x2": np.ascontiguousarray(x2_all[c], np.float32),
            "Wk": inputs["L1_attn_Wk"], "Wq": inputs["L1_attn_Wq"],
            "Wv": inputs["L1_attn_Wv"], "Wo": inputs["L1_attn_Wo"],
            "bo": inputs["L1_attn_bo"].reshape(1, -1),
        })
    return run_bass_kernel_spmd(nc, in_maps, core_ids=list(range(cores)))



def build_launch3c(dbg=False):
    nc = bass.Bass("TRN2", target_bir_lowering=False)
    x3_in = nc.dram_tensor("x3", [T, D], F32, kind="ExternalInput").ap()
    gw_in = nc.dram_tensor("gw", [8, D], F32, kind="ExternalInput").ap()
    w1_in = nc.dram_tensor("w1", [8, D, 4096], F32, kind="ExternalInput").ap()
    w2_in = nc.dram_tensor("w2", [8, 4096, D], F32, kind="ExternalInput").ap()
    out_d = nc.dram_tensor("out", [T, D], F32, kind="ExternalOutput").ap()

    NE = 8
    FF = 4096
    NFT = FF // 128
    KSEL = 1024
    SLOTP = KSEL + 128
    NIT = 35
    if dbg == 3:
        NIT = 1

    if dbg:
        dbg_probs = nc.dram_tensor("dbg_probs", [128, 256], F32, kind="ExternalOutput").ap()
        dbg_lists2 = nc.dram_tensor("dbg_lists2", [128, 144], F32, kind="ExternalOutput").ap()
        dbg_lists = nc.dram_tensor("dbg_lists", [8, 2 * (KSEL + 128)], F32, kind="ExternalOutput").ap()
        dbg_tau = nc.dram_tensor("dbg_tau", [8, 16], F32, kind="ExternalOutput").ap()
        dbg_nc = nc.dram_tensor("dbg_nc", [128, 2], F32, kind="ExternalOutput").ap()
        dbg_rows = nc.dram_tensor("dbg_rows", [1, 512], F32, kind="ExternalOutput").ap()
        dbg_dst = nc.dram_tensor("dbg_dst", [128, 256], F32, kind="ExternalOutput").ap()
        dbg_ev = nc.dram_tensor("dbg_ev", [128, 512], F32, kind="ExternalOutput").ap()
        dbg_bis = nc.dram_tensor("dbg_bis", [128, 16], F32, kind="ExternalOutput").ap()
    xln4_dram = nc.dram_tensor("xln4_d", [T, D], BF16)
    probsE_d = nc.dram_tensor("probsE_d", [8, T], F32)
    lists2_d = nc.dram_tensor("lists2_d", [8 * (KSEL + 128) + 8, 2], F32)
    xsel_d = nc.dram_tensor("xsel_d", [SLOTP, D], BF16)

    with ExitStack() as ctx:
        b = B(nc, ctx)
        idb_dram = nc.inline_tensor(np.eye(128, dtype=BF16_NP), "ident_bf")
        idf_dram = nc.inline_tensor(np.eye(128, dtype=np.float32), "ident_f32")
        cc_np = np.zeros((128, 8), np.float32)
        cc_np[:, 0] = LNEPS; cc_np[:, 2] = -0.5; cc_np[:, 3] = 1.0 / D
        cc_dram = nc.inline_tensor(cc_np, "consts_cc")
        ind_np = []
        for ti in range(2):
            m = np.zeros((128, 8), dtype=BF16_NP)
            for p in range(128):
                m[p, 4 * ti + p // 32] = 1.0
            ind_np.append(m)
        indb_dram = [nc.inline_tensor(ind_np[ti], f"indb_d{ti}") for ti in range(2)]
        indtf_np = [np.ascontiguousarray(ind_np[ti].astype(np.float32).T) for ti in range(2)]
        indtf_dram = [nc.inline_tensor(indtf_np[ti], f"indtf_d{ti}") for ti in range(2)]
        chunkb_dram = nc.inline_tensor(
            (128.0 * (np.arange(128) % 32)).astype(np.float32).reshape(128, 1), "chunkb_d")
        ebase_np = np.zeros((1, 256), np.float32)
        for j in range(256):
            ebase_np[0, j] = float(SLOTP * (j // 32))
        ebase_dram = nc.inline_tensor(ebase_np, "ebase_d")
        zrow_dram = nc.inline_tensor(np.zeros((1, 256), np.float32), "zrow_d")

        identb = b.sbuf("identb", [128, 128], BF16)
        identf = b.sbuf("identf", [128, 128], F32)
        cc = b.sbuf("cc", [128, 8], F32)
        xln4_fm = b.sbuf("xln4_fm", [128, ND * T], BF16)
        h_sb = b.sbuf("h_sb", [128, NFT * KSEL], BF16)
        xsel_fm = b.sbuf("xsel_fm", [128, ND * KSEL], BF16)
        gateT = b.sbuf("gateT", [128, ND * 8], BF16)
        indb = [b.sbuf(f"indb{ti}", [128, 8], BF16) for ti in range(2)]
        indtf = [b.sbuf(f"indtf{ti}", [8, 128], F32) for ti in range(2)]
        chunkb = b.sbuf("chunkb", [128, 1], F32)
        ebase_r = b.sbuf("ebase_r", [1, 256], F32)
        zrow = b.sbuf("zrow", [1, 256], F32)
        pc = [b.sbuf(f"pc{ti}", [128, 128], F32) for ti in range(2)]
        pcm = [b.sbuf(f"pcm{ti}", [128, 128], F32) for ti in range(2)]
        evei = [b.sbuf(f"evei{ti}", [128, 256], F32) for ti in range(2)]
        maskb = b.sbuf("maskb", [128, 128], BF16)
        masku = b.sbuf("masku", [128, 128], U8)
        neg1 = b.sbuf("neg1", [128, 128], F32)
        vmask = b.sbuf("vmask", [128, 128], F32)
        incl = b.sbuf("incl", [128, 128], F32)
        dstc = [b.sbuf(f"dstc{ti}", [128, 128], F32) for ti in range(2)]
        dsti = b.sbuf("dsti", [128, 128], I32)
        ei_u = b.sbuf("ei_u", [128, 128], mybir.dt.uint32)
        zero128 = b.sbuf("zero128", [128, 128], F32)
        midrep = [b.sbuf(f"midrep{ti}", [128, 1], F32) for ti in range(2)]
        taurep = [b.sbuf(f"taurep{ti}", [128, 1], F32) for ti in range(2)]
        ncrow = [b.sbuf(f"ncrow{ti}", [128, 1], F32) for ti in range(2)]
        pcol = [b.sbuf(f"pcol{ti}", [128, 1], F32) for ti in range(2)]
        st8 = b.sbuf("st8", [8, 16], F32)
        prow = b.sbuf("prow", [1, 256], F32)
        nrow = b.sbuf("nrow", [1, 256], F32)
        nrow_i = b.sbuf("nrow_i", [1, 256], I32)
        xtk = b.sbuf("xtk", [128, D], F32)
        ntk = b.sbuf("ntk", [128, D], BF16)
        h2tk = b.sbuf("h2tk", [128, D], F32)
        stats = b.sbuf("stats", [128, 8], F32)
        sq_sc = b.sbuf("sq_sc", [128, D], F32)
        probs8 = b.sbuf("probs8", [128, 8], F32)
        gex = b.sbuf("gex", [128, 8], F32)
        gcol = b.sbuf("gcol", [128, 4], F32)
        wrow8 = b.sbuf("wrow8", [8, D], F32)
        wq8 = b.sbuf("wq8", [8, D], BF16)
        gates8 = b.sbuf("gates8", [128, 8], F32)
        xsel_sb = b.sbuf("xsel_sb", [128, D], BF16)
        gi2 = b.sbuf("gi2", [128, 16], F32)
        mx8 = b.sbuf("mx8", [128, 8], F32)
        lists_stage = b.sbuf("lists_stage", [128, 144], F32)
        idxf = b.sbuf("idxf", [128, 8], F32)
        idxi = b.sbuf("idxi", [128, 8], I32)

        _xaddr = nc.lookup_mloc(xln4_fm).addr
        w1sb = nc.alloc_sbuf_tensor_at("w1sb", [128, ND * FF], BF16, offset=_xaddr)
        w2sb = nc.alloc_sbuf_tensor_at("w2sb", [128, NFT * D], BF16, offset=_xaddr)

        ps_tr = [b.psum(f"ps_tr{i}", [128, 128], BF16) for i in range(2)]
        ps8 = [b.psum(f"ps8_{i}", [128, 512], F32) for i in range(4)]
        ps_r = b.psum("ps_r", [128, 16], F32)
        ps_rep = [ps_r[:, 8 + i:9 + i] for i in range(2)]
        ps_cnt = ps8[3][0:8, 0:128]
        ps_row = ps8[3][0:1, 0:128]

        sy, gp, ve, sc, te = b.sy, b.gp, b.ve, b.sc, b.te

        sy.dma_start(out=identb[:], in_=idb_dram.ap())
        sy.dma_start(out=identf[:], in_=idf_dram.ap())
        sy.dma_start(out=cc[:], in_=cc_dram.ap())
        for ti in range(2):
            sy.dma_start(out=indb[ti][:], in_=indb_dram[ti].ap())
            sy.dma_start(out=indtf[ti][:], in_=indtf_dram[ti].ap())
        sy.dma_start(out=chunkb[:], in_=chunkb_dram.ap())
        sy.dma_start(out=ebase_r[:], in_=ebase_dram.ap())
        sy.dma_start(out=zrow[:], in_=zrow_dram.ap())
        ve.memset(zero128[:], 0.0)
        ve.memset(neg1[:], -1.0)

        sy.dma_start(out=wrow8[:], in_=gw_in)
        ve.tensor_copy(wq8[:], wrow8[:])
        for j in range(ND):
            pj = j % 2
            te.transpose(ps_tr[pj][:, 0:8], wq8[:, 128 * j:128 * (j + 1)], identb[0:8, 0:8])
            ve.tensor_copy(gateT[:, 8 * j:8 * (j + 1)], ps_tr[pj][:, 0:8])

        gp.dma_start(out=out_d, in_=x3_in)

        ve.memset(lists_stage[:], 0.0)
        sy.dma_start(out=lists2_d.ap()[0:SLOTP * 8, :]
                     .rearrange("(p b) c -> p (b c)", p=128),
                     in_=lists_stage[:, 0:2 * (SLOTP * 8 // 128)])
        sy.dma_start(out=lists2_d.ap()[SLOTP * 8:SLOTP * 8 + 8, :],
                     in_=lists_stage[0:8, 0:2])

        for t in range(NT):
            sy.dma_start(out=xtk[:], in_=x3_in[128 * t:128 * (t + 1), :])
            emit_ln_chunk(b, xtk, ntk, stats, sq_sc, cc)
            gp.dma_start(out=xln4_dram.ap()[128 * t:128 * (t + 1), :], in_=ntk[:])
            emit_tr_to_fm(b, ntk, xln4_fm, t, identb, ps_tr)
            for kk in range(ND):
                te.matmul(ps_r[:, 0:8], xln4_fm[:, T * kk + 128 * t: T * kk + 128 * (t + 1)],
                          gateT[:, 8 * kk:8 * (kk + 1)],
                          start=(kk == 0), stop=(kk == ND - 1))
            ve.tensor_copy(probs8[:], ps_r[:, 0:8])
            ve.reduce_max(gcol[:, 0:1], probs8[:], axis=AX.X, negate=True)
            ve.tensor_tensor(gex[:], probs8[:],
                             gcol[:, 0:1].broadcast_to([128, 8]), ALU.add)
            sc.activation(out=gex[:], in_=gex[:], func=AF.Exp)
            ve.reduce_sum(gcol[:, 1:2], gex[:], axis=AX.X)
            ve.reciprocal(gcol[:, 2:3], gcol[:, 1:2])
            ve.tensor_tensor(probs8[:], gex[:],
                             gcol[:, 2:3].broadcast_to([128, 8]), ALU.mult)
            te.transpose(ps_cnt, probs8[:], identf[:, :])
            ve.tensor_copy(vmask[0:8, 0:128], ps_cnt)
            gp.dma_start(out=probsE_d.ap()[:, 128 * t:128 * (t + 1)], in_=vmask[0:8, 0:128])

        for ti in range(2):
            sy.dma_start(out=pc[ti][:],
                         in_=probsE_d.ap().rearrange("e (c w) -> (e c) w", w=128)
                         [128 * ti:128 * (ti + 1), :])
        ve.memset(st8[:, 0:1], 0.0)
        ve.memset(st8[:, 1:2], 1.0)
        for it in range(NIT):
            ve.tensor_add(st8[:, 2:3], st8[:, 0:1], st8[:, 1:2])
            ve.tensor_single_scalar(st8[:, 3:4], st8[:, 2:3], 0.5, ALU.mult)
            for ti in range(2):
                te.matmul(ps_rep[ti], indtf[ti][:, :], st8[:, 3:4],
                          start=True, stop=True)
                ve.tensor_copy(midrep[ti][:], ps_rep[ti])
                ve.tensor_single_scalar(maskb[:], pc[ti][:], midrep[ti][:, 0:1], ALU.is_ge)
                te.matmul(ps_cnt, indb[ti][:, :], maskb[:],
                          start=(ti == 0), stop=(ti == 1))
            ve.reduce_sum(st8[:, 4:5], ps_cnt, axis=AX.X)
            ve.tensor_single_scalar(st8[:, 5:6], st8[:, 4:5], float(KSEL), ALU.is_ge)
            ve.tensor_sub(st8[:, 6:7], st8[:, 3:4], st8[:, 0:1])
            ve.tensor_mul(st8[:, 7:8], st8[:, 5:6], st8[:, 6:7])
            ve.tensor_add(st8[:, 0:1], st8[:, 0:1], st8[:, 7:8])
            ve.tensor_sub(st8[:, 6:7], st8[:, 3:4], st8[:, 1:2])
            ve.tensor_mul(st8[:, 7:8], st8[:, 5:6], st8[:, 6:7])
            ve.tensor_sub(st8[:, 6:7], st8[:, 6:7], st8[:, 7:8])
            ve.tensor_add(st8[:, 1:2], st8[:, 1:2], st8[:, 6:7])
        if dbg == 3:
            b.barrier("gpsimd")
            gp.dma_start(out=dbg_bis[0:8, 0:16], in_=st8[:, :])
            gp.dma_start(out=dbg_bis[0:128, 1:2], in_=midrep[0][:, :],
                         allow_slow_non_contiguous=True)
            gp.dma_start(out=dbg_bis[0:128, 2:3], in_=pc[0][:, 0:1],
                         allow_slow_non_contiguous=True)
            gp.dma_start(out=dbg_bis[0:8, 3:4], in_=ps_cnt.tensor.ap()[0:8, 3:4]
                         if False else pc[1][0:8, 0:1],
                         allow_slow_non_contiguous=True)
            b.barrier("gpsimd")
        for ti in range(2):
            te.matmul(ps_rep[ti], indtf[ti][:, :], st8[:, 0:1],
                      start=True, stop=True)
            ve.tensor_copy(taurep[ti][:], ps_rep[ti])
            ve.tensor_scalar(masku[:], pc[ti][:], taurep[ti][:, 0:1], 0.0,
                             ALU.is_ge, ALU.add, accum_out=ncrow[ti][:, 0:1])

        for ti in range(2):
            te.transpose(ps_row, ncrow[ti][:, 0:1], identf[:, :])
            ve.tensor_copy(nrow[0:1, 128 * ti:128 * (ti + 1)], ps_row)
        ve.tensor_tensor_scan(out=prow[:], data0=nrow[:], data1=zrow[:],
                              initial=0.0, op0=ALU.add, op1=ALU.add)
        ve.tensor_sub(prow[:], prow[:], nrow[:])
        ve.tensor_tensor(prow[:].rearrange("p (e c) -> p e c", c=32),
                         prow[:].rearrange("p (e c) -> p e c", c=32),
                         prow[:, 0:256:32].rearrange("p (e o) -> p e o", o=1)
                         .broadcast_to([1, 8, 32]), ALU.subtract)
        ve.tensor_add(prow[:], prow[:], ebase_r[:])
        for ti in range(2):
            te.transpose(ps_rep[ti], prow[0:1, 128 * ti:128 * (ti + 1)],
                         identf[0:1, 0:1])
            ve.tensor_copy(pcol[ti][:], ps_rep[ti])

        for ti in range(2):
            ve.tensor_single_scalar(pcm[ti][:], pc[ti][:], taurep[ti][:, 0:1], ALU.is_ge)
            ve.scalar_tensor_tensor(pcm[ti][:], pc[ti][:], 1.0, pcm[ti][:],
                                    ALU.add, ALU.mult)
            ve.tensor_single_scalar(pcm[ti][:], pcm[ti][:], 1.0, ALU.subtract)
            vv = evei[ti][:, :].rearrange("p (w two) -> p w two", two=2)
            for r in range(16):
                ve.max(mx8[:, :], pcm[ti][:])
                ve.max_index(ei_u[:, 8 * r:8 * (r + 1)], mx8[:, :], pcm[ti][:])
                ve.tensor_copy(vv[:, 8 * r:8 * (r + 1), 0:1], mx8[:, :])
                ve.match_replace(pcm[ti][:], mx8[:, :], pcm[ti][:], -2.0)
            ve.tensor_copy(vv[:, :, 1:2], ei_u[:, :])
            ve.tensor_single_scalar(vv[:, :, 1:2], vv[:, :, 1:2],
                                    chunkb[:, 0:1], ALU.add)
            ve.tensor_single_scalar(vmask[:], vv[:, :, 0:1], taurep[ti][:, 0:1],
                                    ALU.is_ge)
            ve.tensor_tensor_scan(out=incl[:], data0=vmask[:], data1=zero128[:],
                                  initial=0.0, op0=ALU.add, op1=ALU.add)
            ve.tensor_scalar(dstc[ti][:], incl[:], -1.0, pcol[ti][:, 0:1],
                             ALU.add, ALU.add)
            ve.tensor_mul(dstc[ti][:], dstc[ti][:], vmask[:])
            ve.tensor_single_scalar(incl[:], vmask[:], -1.0, ALU.mult)
            ve.tensor_single_scalar(incl[:], incl[:], 1.0, ALU.add)
            ve.scalar_tensor_tensor(dstc[ti][:], incl[:], float(8 * SLOTP),
                                    dstc[ti][:], ALU.mult, ALU.add)
            ve.tensor_scalar_min(dstc[ti][:], dstc[ti][:], float(8 * SLOTP))

        if dbg:
            b.barrier("gpsimd")
            gp.dma_start(out=dbg_probs[:, 0:128], in_=pc[0][:, :])
            gp.dma_start(out=dbg_probs[:, 128:256], in_=pc[1][:, :])
            gp.dma_start(out=dbg_tau[:, :], in_=st8[:, :])
            gp.dma_start(out=dbg_nc[:, 0:1], in_=ncrow[0][:, :],
                         allow_slow_non_contiguous=True)
            gp.dma_start(out=dbg_nc[:, 1:2], in_=ncrow[1][:, :],
                         allow_slow_non_contiguous=True)
            gp.dma_start(out=dbg_rows[0:1, 0:256], in_=prow[:, :])
            gp.dma_start(out=dbg_rows[0:1, 256:512], in_=nrow[:, :])
            gp.dma_start(out=dbg_dst[:, 0:128], in_=dstc[0][:, :])
            gp.dma_start(out=dbg_dst[:, 128:256], in_=dstc[1][:, :])
            gp.dma_start(out=dbg_ev[:, 0:256], in_=evei[0][:, :])
            gp.dma_start(out=dbg_ev[:, 256:512], in_=evei[1][:, :])
            b.barrier("gpsimd")
        for ti in range(2):
            ve.tensor_copy(dsti[:], dstc[ti][:])
            for i in range(128) if dbg != 2 else []:
                gp.indirect_dma_start(
                    out=lists2_d.ap(),
                    out_offset=IndirectOffsetOnAxis(dsti[:, i:i + 1], 0),
                    in_=evei[ti][:, 2 * i:2 * (i + 1)],
                    in_offset=None)

        if dbg:
            b.barrier("sync")
            for bb in range(72):
                sy.dma_start(out=lists_stage[:, 2 * bb:2 * (bb + 1)],
                             in_=lists2_d.ap()[128 * bb:128 * (bb + 1), :])
            b.barrier("gpsimd")
            gp.dma_start(out=dbg_lists2[:, :], in_=lists_stage[:, :])
            b.barrier("gpsimd")

        for e2 in range(NE):
            b.barrier("gpsimd")
            for kk in range(ND):
                gp.dma_start(out=bass.AP(w1sb, 0, [[1, 128], [1, FF]])
                             if False else w1sb.ap()[:, FF * kk:FF * (kk + 1)],
                             in_=w1_in[e2, 128 * kk:128 * (kk + 1), :])
            for j in range(8):
                sy.dma_start(out=gi2[:, 2 * j:2 * (j + 1)],
                             in_=lists2_d.ap()[SLOTP * e2 + 128 * j:
                                               SLOTP * e2 + 128 * (j + 1), :])
            ve.tensor_copy(gates8[:, :], gi2[:, 0:16:2])
            ve.tensor_copy(idxf[:, :], gi2[:, 1:16:2])
            ve.tensor_scalar(idxf[:], idxf[:], 0.0, float(T - 1), ALU.max, ALU.min)
            ve.tensor_copy(idxi[:], idxf[:])
            for j in range(8):
                gp.indirect_dma_start(
                    out=xsel_sb[:, :], out_offset=None,
                    in_=xln4_dram.ap(),
                    in_offset=IndirectOffsetOnAxis(idxi[:, j:j + 1], 0))
                gp.dma_start(out=xsel_d.ap()[128 * j:128 * (j + 1), :], in_=xsel_sb[:, :])
            for kk in range(ND):
                sy.dma_start(out=xsel_fm[:, KSEL * kk:KSEL * (kk + 1)],
                             in_=xsel_d.ap()[0:KSEL, 128 * kk:128 * (kk + 1)],
                             transpose=True)
            for ft in range(NFT):
                for half in range(2):
                    pb = (2 * ft + half) % 4
                    for kk in range(ND):
                        te.matmul(ps8[pb][:, 0:512],
                                  w1sb.ap()[:, FF * kk + 128 * ft: FF * kk + 128 * (ft + 1)],
                                  xsel_fm[:, KSEL * kk + 512 * half:
                                          KSEL * kk + 512 * (half + 1)],
                                  start=(kk == 0), stop=(kk == ND - 1))
                    sc.activation(out=h_sb[:, KSEL * ft + 512 * half:
                                           KSEL * ft + 512 * (half + 1)],
                                  in_=ps8[pb][:, 0:512], func=AF.Gelu)
            b.barrier("gpsimd")
            for ft in range(NFT):
                gp.dma_start(out=w2sb.ap()[:, D * ft:D * (ft + 1)],
                             in_=w2_in[e2, 128 * ft:128 * (ft + 1), :])
            for tc in range(8):
                for nh in range(2):
                    pb = (2 * tc + nh) % 4
                    for ft in range(NFT):
                        te.matmul(ps8[pb][:, 0:512],
                                  h_sb[:, KSEL * ft + 128 * tc:
                                       KSEL * ft + 128 * (tc + 1)],
                                  w2sb.ap()[:, D * ft + 512 * nh: D * ft + 512 * (nh + 1)],
                                  start=(ft == 0), stop=(ft == NFT - 1))
                    ve.tensor_single_scalar(h2tk[:, 512 * nh:512 * (nh + 1)],
                                            ps8[pb][:, 0:512],
                                            gates8[:, tc:tc + 1], ALU.mult)
                gp.indirect_dma_start(
                    out=out_d, out_offset=IndirectOffsetOnAxis(idxi[:, tc:tc + 1], 0),
                    in_=h2tk[:, :], in_offset=None, compute_op=ALU.add)
        b.barrier("sync")
        b.realize()
    return nc


def _run_launch3c(inputs, x3_all, cores=8, dbg=False):
    nc = build_launch3c(dbg=dbg)
    in_maps = []
    for c in range(cores):
        in_maps.append({
            "x3": np.ascontiguousarray(x3_all[c], np.float32),
            "gw": inputs["L1_ffn_gate_w"],
            "w1": inputs["L1_ffn_w1"],
            "w2": inputs["L1_ffn_w2"],
        })
    return run_bass_kernel_spmd(nc, in_maps, core_ids=list(range(cores)))



def kernel(**inputs):
    x = inputs["L0_attn_x"]

    res1 = _run_launch1(inputs, [x[c] for c in range(8)])
    x1_all = np.stack([np.asarray(res1.results[c]["x1"]) for c in range(8)])
    qkv_all = np.stack([np.asarray(res1.results[c]["qkv"]) for c in range(8)])

    res2 = _run_launch2(qkv_all, inputs["L0_ffn_proj_weight"])
    po_all = np.zeros((8, T, D), np.float32)
    for j in range(8):
        po_all[:, 512 * j:512 * (j + 1), :] = \
            np.asarray(res2.results[j]["po"]).reshape(8, 512, D)

    res3a = _run_launch3a(inputs, x1_all, po_all)
    x2_all = np.stack([np.asarray(res3a.results[c]["x2"]) for c in range(8)])

    res3b = _run_launch3b(inputs, x2_all)
    x3_all = np.stack([np.asarray(res3b.results[c]["x3"]) for c in range(8)])

    res3c = _run_launch3c(inputs, x3_all)
    out = np.stack([np.asarray(res3c.results[c]["out"]) for c in range(8)])
    return out.astype(np.float32)


# version 05 (final)
# speedup vs baseline: 1.0360x; 1.0360x over previous
import numpy as np
import ml_dtypes
import concourse.bass as bass
import concourse.mybir as mybir
from concourse.bass_utils import run_bass_kernel_spmd
from concourse.bass_core_rust import IndirectOffsetOnAxis
from contextlib import ExitStack

F32 = mybir.dt.float32
BF16 = mybir.dt.bfloat16
I32 = mybir.dt.int32
U8 = mybir.dt.uint8
AF = mybir.ActivationFunctionType
ALU = mybir.AluOpType
AX = mybir.AxisListType

T = 4096
D = 1024
NT = T // 128
ND = D // 128
NC5 = T // 512
MAGIC = 1.5 * 2.0 ** 23
BF16_NP = ml_dtypes.bfloat16
LNEPS = 1e-5
RMSEPS = 1e-6



_OUTS = {
    "dma_start": ("out",), "indirect_dma_start": ("out",),
    "matmul": ("out",), "transpose": ("out",),
    "activation": ("out", "accum_out"),
    "memset": ("ap",), "memzero": ("ap",),
    "tensor_copy": ("out",), "tensor_tensor": ("out",),
    "tensor_add": ("out",), "tensor_sub": ("out",), "tensor_mul": ("out",),
    "tensor_max": ("out",),
    "tensor_scalar": ("out", "accum_out"), "tensor_single_scalar": ("out",),
    "tensor_scalar_add": ("out",), "tensor_scalar_sub": ("out",),
    "tensor_scalar_mul": ("out",), "tensor_scalar_max": ("out",),
    "tensor_scalar_min": ("out",), "tensor_relu": ("out",),
    "tensor_reduce": ("out",), "reduce_max": ("out",), "reduce_sum": ("out",),
    "reciprocal": ("out",), "scalar_tensor_tensor": ("out", "accum_out"),
    "tensor_tensor_scan": ("out",), "select": ("out",),
    "copy_predicated": ("out",), "iota": ("out",),
    "max": ("out",), "max_index": ("out",),
    "max_with_indices": ("out_max", "out_indices"),
    "match_replace": ("out",), "sign": ("out",), "square": ("out",),
    "sqrt": ("out",), "mul": ("out",), "copy": ("out",), "add": ("out",),
    "collective_compute": ("outs",),
}
_POSN = {
    "dma_start": ("out", "in_"), "matmul": ("out", "lhsT", "rhs"),
    "transpose": ("out", "in_", "identity"),
    "activation": ("out", "in_", "func", "bias", "scale", "alpha", "accum_out"),
    "memset": ("ap", "constant"), "memzero": ("ap",),
    "tensor_copy": ("out", "in_"), "tensor_tensor": ("out", "in0", "in1", "op"),
    "tensor_add": ("out", "in0", "in1"), "tensor_sub": ("out", "in0", "in1"),
    "tensor_mul": ("out", "in0", "in1"), "tensor_max": ("out", "in0", "in1"),
    "tensor_scalar": ("out", "in0", "scalar1", "scalar2", "op0", "op1", "accum_out"),
    "tensor_single_scalar": ("out", "in_", "scalar", "op"),
    "tensor_scalar_add": ("out", "in0", "scalar1"),
    "tensor_scalar_sub": ("out", "in0", "scalar1"),
    "tensor_scalar_mul": ("out", "in0", "scalar1"),
    "tensor_scalar_max": ("out", "in0", "scalar1"),
    "tensor_scalar_min": ("out", "in0", "scalar1"),
    "tensor_reduce": ("out", "in_", "axis", "op"),
    "reduce_max": ("out", "in_", "axis"), "reduce_sum": ("out", "in_", "axis"),
    "reciprocal": ("out", "in_"),
    "scalar_tensor_tensor": ("out", "in0", "scalar", "in1", "op0", "op1", "accum_out"),
    "tensor_tensor_scan": ("out", "data0", "data1", "initial", "op0", "op1"),
    "select": ("out", "mask", "on_true", "on_false"),
    "copy_predicated": ("out", "mask", "data"),
    "iota": ("out", "pattern"),
    "max": ("out", "in_"), "max_index": ("out", "in_max", "in_values"),
    "max_with_indices": ("out_max", "out_indices", "in_"),
    "match_replace": ("out", "in_to_replace", "in_values", "imm_value"),
    "square": ("out", "in_"), "sqrt": ("out", "in_"),
    "mul": ("out", "in_", "mul"), "copy": ("out", "in_"), "add": ("out", "in_", "add"),
    "indirect_dma_start": ("out", "out_offset", "in_", "in_offset"),
}
_DMA_OPS = {"dma_start", "indirect_dma_start", "collective_compute", "dma_start_transpose"}


def _ap_tensor_name(a):
    t = getattr(a, "tensor", None)
    if t is not None:
        return getattr(t, "name", None)
    return None


class _Pend:
    def __init__(self, b, engine, box):
        self.b = b
        self.engine = engine
        self.box = box


class Sem:
    def __init__(self, name):
        self.name = name
        self.h = None
        self.v = 0


class Eng:
    def __init__(self, b, name):
        self._b = b
        self._name = name

    def __getattr__(self, att):
        if att.startswith("_"):
            raise AttributeError(att)
        b = self._b
        name = self._name

        def em(*a, **kw):
            posn = _POSN.get(att, ())
            outs_names = _OUTS.get(att, ("out",))
            outs, ins = [], []
            def visit(argname, val):
                if isinstance(val, (list, tuple)):
                    for v in val:
                        visit(argname, v)
                    return
                tn = _ap_tensor_name(val)
                if tn is None:
                    sub = getattr(val, "ap", None)
                    tn2 = _ap_tensor_name(sub) if sub is not None else None
                    if tn2 is not None:
                        ins.append(tn2)
                    return
                if argname in outs_names and argname not in ("out_offset", "in_offset"):
                    outs.append(tn)
                else:
                    ins.append(tn)
            for i, val in enumerate(a):
                visit(posn[i] if i < len(posn) else f"arg{i}", val)
            for k, val in kw.items():
                visit(k, val)
            if att not in _OUTS and att not in _POSN and att not in ("nop", "wait_ge"):
                raise AssertionError(f"op {att} not in dependency tables")

            def need(dep_eng):
                return not (dep_eng == name == "tensor")
            for tn in set(ins) | set(outs):
                for weng, wval in b.writes.get(tn, {}).items():
                    if need(weng):
                        b._wait(name, weng, wval)
            for tn in set(outs):
                for reng, rval in b.reads.get(tn, {}).items():
                    if need(reng) and rval > 0:
                        b._wait(name, reng, rval)

            box = {}

            def run(eng, box=box, att=att, a=a, kw=kw):
                try:
                    box["i"] = getattr(eng, att)(*a, **kw)
                except Exception:
                    import sys
                    print("FAILING OP:", att, "tensors:",
                          [(_ap_tensor_name(x), getattr(x, "shape", None)) for x in a],
                          kw, file=sys.stderr)
                    raise
            is_dma = att in _DMA_OPS and name in b.dsems
            if is_dma:
                k = b.dslot[name]
                b.dslot[name] = (k + 1) % len(b.dsems[name])
                ssem = b.dsems[name][k]
                prev = b.dcnt[name][k]
                if prev and b.mark[name].get((name, k), 0) < prev:
                    b.ops[name].append(
                        lambda eng, s=ssem, v=16 * prev: eng.wait_ge(s.h, v))
                    b.mark[name][(name, k)] = prev
            b.ops[name].append(run)
            sem = b.esem[name]
            step = 16 if att in _DMA_OPS else 1
            isem = ssem if is_dma else sem
            def inc_fn(eng, box=box, isem=isem, step=step):
                if box["i"] is not None:
                    box["i"].then_inc(isem.h, step)
                else:
                    eng.nop().then_inc(isem.h, step)
            b.ops[name].append(inc_fn)
            sem.v += step
            v = sem.v
            if is_dma:
                b.dcnt[name][k] += 1
                b.dsnap[name].append(tuple(b.dcnt[name]))
            for tn in ins:
                b.reads.setdefault(tn, {})[name] = v
            for tn in outs:
                b.writes.setdefault(tn, {})[name] = v
                b.reads.setdefault(tn, {}).setdefault(name, 0)
            return _Pend(b, name, box)
        return em


class B:
    def __init__(self, nc, ctx):
        self.nc = nc
        self.ctx = ctx
        self.ops = {"sync": [], "gpsimd": [], "vector": [], "scalar": [], "tensor": []}
        self.sy = Eng(self, "sync")
        self.gp = Eng(self, "gpsimd")
        self.ve = Eng(self, "vector")
        self.sc = Eng(self, "scalar")
        self.te = Eng(self, "tensor")
        self.sems = []
        self.sd = self.mksem("sd")
        self.gd = self.mksem("gd")
        self.pe = self.mksem("pe")
        self.dv = self.mksem("dv")
        self.ac = self.mksem("ac")
        self.esem = {"sync": self.sd, "gpsimd": self.gd, "tensor": self.pe,
                     "vector": self.dv, "scalar": self.ac}
        self.sem_of_eng = {"sync": "sd", "gpsimd": "gd", "tensor": "pe",
                           "vector": "dv", "scalar": "ac"}
        KSL = 8
        self.dsems = {e: [self.mksem(f"{self.sem_of_eng[e]}s{k}") for k in range(KSL)]
                      for e in ("sync", "gpsimd")}
        self.dslot = {e: 0 for e in self.dsems}
        self.dcnt = {e: [0] * KSL for e in self.dsems}
        self.dsnap = {e: [] for e in self.dsems}
        self.mark = {e: {e2: 0 for e2 in self.ops} for e in self.ops}
        self.writes = {}
        self.reads = {}

    def mksem(self, name):
        s = Sem(name)
        self.sems.append(s)
        return s

    def _wait(self, engine, prod_engine, val):
        if self.mark[engine][prod_engine] >= val:
            return
        sem = self.esem[prod_engine]
        assert val <= sem.v
        if prod_engine in self.dsems:
            idx = val // 16
            if idx == 0:
                return
            counts = self.dsnap[prod_engine][idx - 1]
            for k, c in enumerate(counts):
                if c and self.mark[engine].get((prod_engine, k), 0) < c:
                    s = self.dsems[prod_engine][k]
                    self.ops[engine].append(
                        lambda eng, s=s, v=16 * c: eng.wait_ge(s.h, v))
                    self.mark[engine][(prod_engine, k)] = c
            self.mark[engine][prod_engine] = val
            return
        self.ops[engine].append(lambda eng, sem=sem, v=val: eng.wait_ge(sem.h, v))
        self.mark[engine][prod_engine] = val

    def inc(self, pend, sem, k=1):
        return self.esem[pend.engine].v

    def wait(self, engine, sem, val=None):
        if val is None:
            val = sem.v
        assert val <= sem.v, f"wait for future value of {sem.name}"
        prod = {"sd": "sync", "gd": "gpsimd", "pe": "tensor", "dv": "vector",
                "ac": "scalar"}.get(sem.name)
        if prod is not None:
            self._wait(engine, prod, val)
        else:
            self.ops[engine].append(lambda eng, sem=sem, v=val: eng.wait_ge(sem.h, v))

    def barrier(self, engine):
        for other in self.ops:
            if other != engine:
                self._wait(engine, other, self.esem[other].v)

    def sbuf(self, name, shape, dtype):
        return self.ctx.enter_context(self.nc.sbuf_tensor(name, shape, dtype))

    def psum(self, name, shape, dtype=F32):
        return self.ctx.enter_context(self.nc.psum_tensor(name, shape, dtype))

    def realize(self):
        with ExitStack() as sctx:
            for s in self.sems:
                s.h = sctx.enter_context(self.nc.semaphore(s.name))
            with self.nc.Block() as block:
                block.sync(lambda eng: [f(eng) for f in self.ops["sync"]])
                block.gpsimd(lambda eng: [f(eng) for f in self.ops["gpsimd"]])
                block.vector(lambda eng: [f(eng) for f in self.ops["vector"]])
                block.scalar(lambda eng: [f(eng) for f in self.ops["scalar"]])
                block.tensor(lambda eng: [f(eng) for f in self.ops["tensor"]])



def emit_ln_chunk(b, src, dst, stats, sq_sc, cc, eps_col=0, dcols=D):
    ve, sc = b.ve, b.sc
    b.inc(ve.tensor_reduce(stats[:, 0:1], src[:], axis=AX.X, op=ALU.add), b.dv)
    dvv = b.inc(ve.tensor_single_scalar(stats[:, 0:1], stats[:, 0:1], 1.0 / dcols, ALU.mult), b.dv)
    b.wait("scalar", b.dv, dvv)
    acv = b.inc(sc.activation(out=sq_sc[:], in_=src[:], func=AF.Square,
                              accum_out=stats[:, 1:2]), b.ac)
    b.wait("vector", b.ac, acv)
    b.inc(ve.tensor_single_scalar(stats[:, 1:2], stats[:, 1:2], 1.0 / dcols, ALU.mult), b.dv)
    b.inc(ve.tensor_mul(stats[:, 2:3], stats[:, 0:1], stats[:, 0:1]), b.dv)
    dvv = b.inc(ve.tensor_sub(stats[:, 1:2], stats[:, 1:2], stats[:, 2:3]), b.dv)
    b.wait("scalar", b.dv, dvv)
    b.inc(sc.activation(out=stats[:, 3:4], in_=stats[:, 1:2], func=AF.Ln, bias=cc[:, eps_col:eps_col + 1]), b.ac)
    acv = b.inc(sc.activation(out=stats[:, 3:4], in_=stats[:, 3:4], func=AF.Exp, scale=cc[:, 2:3]), b.ac)
    b.wait("vector", b.ac, acv)
    b.wait("vector", b.pe)
    return b.inc(ve.tensor_scalar(dst[:], src[:], stats[:, 0:1], stats[:, 3:4],
                                  ALU.subtract, ALU.mult), b.dv)


def emit_tr_to_fm(b, ntk, fm, tchunk, identb, ps_tr, affine=None, fm_stride=T):
    b.wait("tensor", b.dv)
    for j in range(ND):
        pj = j % 2
        tv = b.inc(b.te.transpose(ps_tr[pj][:], ntk[:, 128 * j:128 * (j + 1)], identb[:]), b.pe)
        b.wait("vector", b.pe, tv)
        dst = fm[:, fm_stride * j + 128 * tchunk: fm_stride * j + 128 * (tchunk + 1)]
        if affine is None:
            dvv = b.inc(b.ve.tensor_copy(dst, ps_tr[pj][:]), b.dv)
        else:
            w_c, b_c = affine
            dvv = b.inc(b.ve.tensor_scalar(dst, ps_tr[pj][:],
                                           w_c[:, j:j + 1], b_c[:, j:j + 1],
                                           ALU.mult, ALU.add), b.dv)
        b.wait("tensor", b.dv, dvv)
    return dvv


def emit_param_cols(b, src_row_ap, dst, n=ND):
    return b.inc(b.sy.dma_start(out=dst[:, 0:n],
                                in_=src_row_ap.rearrange("a (j p) -> (a p) j", p=128),
                                allow_slow_non_contiguous=True),
                 b.sd, 16)


def emit_dma_bcast_rows(b, tile, width):
    span = 1
    while span < 128:
        cp = min(span, 128 - span)
        gv = b.inc(b.gp.dma_start(out=tile[span:span + cp, 0:width],
                                  in_=tile[0:cp, 0:width]), b.gd, 16)
        b.wait("gpsimd", b.gd, gv)
        span += cp



def build_launch1(dbg=False):
    nc = bass.Bass("TRN2", target_bir_lowering=False)
    x_in = nc.dram_tensor("x", [T, D], F32, kind="ExternalInput").ap()
    w_r = nc.dram_tensor("Wr", [D, D], F32, kind="ExternalInput").ap()
    w_k = nc.dram_tensor("Wk", [D, D], F32, kind="ExternalInput").ap()
    w_v = nc.dram_tensor("Wv", [D, D], F32, kind="ExternalInput").ap()
    w_o = nc.dram_tensor("Wo", [D, D], F32, kind="ExternalInput").ap()
    decay = nc.dram_tensor("decay", [1, D], F32, kind="ExternalInput").ap()
    ln_w = nc.dram_tensor("ln_w", [1, D], F32, kind="ExternalInput").ap()
    ln1_w = nc.dram_tensor("ln1_w", [1, D], F32, kind="ExternalInput").ap()
    ln1_b = nc.dram_tensor("ln1_b", [1, D], F32, kind="ExternalInput").ap()
    qkvw = nc.dram_tensor("qkvw", [1536, D], F32, kind="ExternalInput").ap()

    x1_out = nc.dram_tensor("x1", [T, D], F32, kind="ExternalOutput").ap()
    qkv_out = nc.dram_tensor("qkv", [T, 1536], BF16, kind="ExternalOutput").ap()

    if dbg:
        dbg_xln0 = nc.dram_tensor("dbg_xln0", [128, ND * T], BF16, kind="ExternalOutput").ap()
        dbg_r = nc.dram_tensor("dbg_r", [128, T], BF16, kind="ExternalOutput").ap()
        dbg_k = nc.dram_tensor("dbg_k", [128, T], BF16, kind="ExternalOutput").ap()
        dbg_y = nc.dram_tensor("dbg_y", [D, T], BF16, kind="ExternalOutput").ap()
        dbg_ssq = nc.dram_tensor("dbg_ssq", [1, T], F32, kind="ExternalOutput").ap()
        dbg_rstd = nc.dram_tensor("dbg_rstd", [128, 32], F32, kind="ExternalOutput").ap()
        dbg_xln1 = nc.dram_tensor("dbg_xln1", [128, ND * T], BF16, kind="ExternalOutput").ap()
        dbg_kv = nc.dram_tensor("dbg_kv", [128, T], F32, kind="ExternalOutput").ap()
        dbg_scl = nc.dram_tensor("dbg_scl", [128, T], F32, kind="ExternalOutput").ap()
        dbg_st = nc.dram_tensor("dbg_st", [128, T], F32, kind="ExternalOutput").ap()
    wT_o_dram = nc.dram_tensor("wT_o", [D, D], BF16)
    y_dram = nc.dram_tensor("y_dram", [D, T], BF16)
    row_scr = nc.dram_tensor("row_scr", [1, T], F32)

    with ExitStack() as ctx:
        b = B(nc, ctx)
        idb_dram = nc.inline_tensor(np.eye(128, dtype=BF16_NP), "ident_bf")
        idf_dram = nc.inline_tensor(np.eye(32, dtype=np.float32), "ident_f32")
        pos_dram = nc.inline_tensor(np.arange(512, dtype=np.float32)[None, :].copy(), "pos_row")
        onesc_dram = nc.inline_tensor(np.ones((128, 1), dtype=BF16_NP), "ones_col_d")
        cc_np = np.zeros((128, 8), np.float32)
        cc_np[:, 0] = LNEPS; cc_np[:, 1] = RMSEPS; cc_np[:, 2] = -0.5; cc_np[:, 3] = 1.0 / D
        cc_dram = nc.inline_tensor(cc_np, "consts_cc")

        identb = b.sbuf("identb", [128, 128], BF16)
        cc = b.sbuf("cc", [128, 8], F32)
        identf = b.sbuf("identf", [32, 32], F32)
        ones_col = b.sbuf("ones_col", [128, 1], BF16)
        act_fm = b.sbuf("act_fm", [128, ND * T], BF16)
        wbig = b.sbuf("wbig", [128, 3 * ND * D], BF16)

        class _View:
            def __init__(self, base, off):
                self.base, self.off = base, off

            def __getitem__(self, key):
                p, f = key
                return self.base[p, self.off + f.start: self.off + f.stop]

        wT_r = _View(wbig, 0)
        wT_k = _View(wbig, ND * D)
        wT_v = _View(wbig, 2 * ND * D)
        wT_qkv = _View(wbig, ND * D)
        r_sb = b.sbuf("r_sb", [128, T], BF16)
        k_sb = b.sbuf("k_sb", [128, T], BF16)
        ysq = b.sbuf("ysq", [128, T], BF16)
        pos0 = b.sbuf("pos0", [128, 512], F32)
        posw_t = b.sbuf("posw_t", [128, 512], F32)
        CW = 512
        kvt = b.sbuf("kvt", [128, CW], F32)
        sc1 = b.sbuf("sc1", [128, CW], F32)
        sc2 = b.sbuf("sc2", [128, CW], F32)
        cumt = b.sbuf("cumt", [128, CW], F32)
        zeros = b.sbuf("zeros", [128, CW], F32)
        carry = b.sbuf("carry", [128, 2], F32)
        sqb = b.sbuf("sqb", [128, CW], BF16)
        xtk = b.sbuf("xtk", [128, D], F32)
        htk = b.sbuf("htk", [128, D], F32)
        ntk = b.sbuf("ntk", [128, D], BF16)
        x1tk = b.sbuf("x1tk", [128, D], F32)
        qkvtk = b.sbuf("qkvtk", [128, 1536], BF16)
        stats = b.sbuf("stats", [128, 8], F32)
        sq_sc = b.sbuf("sq_sc", [128, D], F32)
        dec_c = b.sbuf("dec_c", [128, ND], F32)
        lg_c = b.sbuf("lg_c", [128, ND], F32)
        nlg_c = b.sbuf("nlg_c", [128, ND], F32)
        lnw_c = b.sbuf("lnw_c", [128, ND], F32)
        ln1w_c = b.sbuf("ln1w_c", [128, ND], F32)
        ln1b_c = b.sbuf("ln1b_c", [128, ND], F32)
        ssq_row = b.sbuf("ssq_row", [1, T], F32)
        rstd_blk = b.sbuf("rstd_blk", [32, 128], F32)
        rstd_cols = b.sbuf("rstd_cols", [128, 32], F32)
        wrow = b.sbuf("wrow", [128, D], F32)
        wq_b = b.sbuf("wq_b", [128, D], BF16)
        wsc = b.sbuf("wsc", [128, 1], F32)
        wrec = b.sbuf("wrec", [128, 1], F32)
        ntk128 = b.sbuf("ntk128", [128, 128], BF16)
        ybuf = b.sbuf("ybuf", [128, ND * 256], BF16)

        ps_tr = [b.psum(f"ps_tr{i}", [128, 128], BF16) for i in range(2)]
        ps_mm = [b.psum(f"ps_mm{i}", [128, 512], F32) for i in range(4)]
        ps_row = b.psum("ps_row", [1, 512], F32)
        ps_t32 = b.psum("ps_t32", [128, 32], F32)

        sy, gp, ve, sc, te = b.sy, b.gp, b.ve, b.sc, b.te

        b.inc(sy.dma_start(out=identb[:], in_=idb_dram.ap()), b.sd, 16)
        b.inc(sy.dma_start(out=cc[:], in_=cc_dram.ap()), b.sd, 16)
        b.inc(sy.dma_start(out=identf[:], in_=idf_dram.ap()), b.sd, 16)
        b.inc(sy.dma_start(out=ones_col[:], in_=onesc_dram.ap()), b.sd, 16)
        b.inc(sy.dma_start(out=pos0[0:1, :], in_=pos_dram.ap()), b.sd, 16)
        emit_param_cols(b, decay, dec_c)
        emit_param_cols(b, ln_w, lnw_c)
        emit_param_cols(b, ln1_w, ln1w_c)
        emit_param_cols(b, ln1_b, ln1b_c)
        base = b.sd.v
        emit_dma_bcast_rows(b, pos0, 512)

        b.wait("vector", b.sd, base)
        b.inc(ve.memset(zeros[:], 0.0), b.dv)
        b.inc(ve.memset(ssq_row[:], 0.0), b.dv)
        b.wait("scalar", b.sd, base)
        b.inc(sc.activation(out=dec_c[:], in_=dec_c[:], func=AF.Sigmoid), b.ac)
        b.wait("vector", b.ac)
        dvv = b.inc(ve.tensor_scalar_max(lg_c[:], dec_c[:], 1e-7), b.dv)
        b.wait("scalar", b.dv, dvv)
        b.inc(sc.activation(out=lg_c[:], in_=lg_c[:], func=AF.Ln), b.ac)
        b.wait("vector", b.ac)
        b.inc(ve.tensor_scalar_mul(nlg_c[:], lg_c[:], -1.0), b.dv)

        def prep_bitlinear(w_src, dst_sb=None, dst_dram=None):
            for i in range(ND):
                b.wait("sync", b.dv)
                ld = b.inc(sy.dma_start(out=wrow[:], in_=w_src[128 * i:128 * (i + 1), :]), b.sd, 16)
                b.wait("vector", b.sd, ld)
                b.wait("vector", b.pe)
                b.inc(ve.tensor_reduce(wsc[:], wrow[:], axis=AX.X, op=ALU.add,
                                       apply_absolute_value=True), b.dv)
                b.inc(ve.tensor_scalar(wsc[:], wsc[:], 1.0 / D, 1e-5, ALU.mult, ALU.max), b.dv)
                b.inc(ve.reciprocal(wrec[:], wsc[:]), b.dv)
                b.inc(ve.tensor_scalar(sq_sc[:], wrow[:], wrec[:, 0:1], MAGIC, ALU.mult, ALU.add), b.dv)
                b.inc(ve.tensor_single_scalar(sq_sc[:], sq_sc[:], MAGIC, ALU.subtract), b.dv)
                b.inc(ve.tensor_scalar(sq_sc[:], sq_sc[:], -1.0, 1.0, ALU.max, ALU.min), b.dv)
                dvv = b.inc(ve.tensor_single_scalar(wq_b[:], sq_sc[:], wsc[:, 0:1], ALU.mult), b.dv)
                b.wait("tensor", b.dv, dvv)
                for j in range(ND):
                    pj = j % 2
                    tv = b.inc(te.transpose(ps_tr[pj][:], wq_b[:, 128 * j:128 * (j + 1)], identb[:]), b.pe)
                    b.wait("vector", b.pe, tv)
                    if dst_sb is not None:
                        dvv = b.inc(ve.tensor_copy(dst_sb[:, D * j + 128 * i: D * j + 128 * (i + 1)],
                                                   ps_tr[pj][:]), b.dv)
                        b.wait("tensor", b.dv, dvv)
                    else:
                        dvv = b.inc(ve.tensor_copy(ntk128[:], ps_tr[pj][:]), b.dv)
                        b.wait("gpsimd", b.dv, dvv)
                        gv = b.inc(gp.dma_start(
                            out=dst_dram.ap()[128 * j:128 * (j + 1), 128 * i:128 * (i + 1)],
                            in_=ntk128[:]), b.gd, 16)
                        b.wait("vector", b.gd, gv)
                        b.wait("tensor", b.dv, dvv)

        prep_bitlinear(w_r, dst_sb=wT_r)
        prep_bitlinear(w_k, dst_sb=wT_k)
        prep_bitlinear(w_v, dst_sb=wT_v)
        prep_bitlinear(w_o, dst_dram=wT_o_dram)

        for t in range(NT):
            b.wait("sync", b.dv)
            b.wait("sync", b.ac)
            ld = b.inc(sy.dma_start(out=xtk[:], in_=x_in[128 * t:128 * (t + 1), :]), b.sd, 16)
            b.wait("vector", b.sd, ld)
            b.wait("scalar", b.sd, ld)
            emit_ln_chunk(b, xtk, ntk, stats, sq_sc, cc)
            emit_tr_to_fm(b, ntk, act_fm, t, identb, ps_tr)

        if dbg:
            b.wait("gpsimd", b.dv)
            gv = b.inc(gp.dma_start(out=dbg_xln0, in_=act_fm[:, :]), b.gd, 16)
            b.wait("gpsimd", b.gd, gv)
            b.wait("vector", b.gd, gv)
            b.wait("tensor", b.gd, gv)
        for j in range(ND):
            b.wait("scalar", b.dv)
            for w_sb, dst, sig in ((wT_r, r_sb, True), (wT_k, k_sb, False)):
                for ch in range(NC5):
                    pb = ch % 2
                    b.wait("tensor", b.ac if sig else b.dv)
                    for kk in range(ND):
                        mv = b.inc(te.matmul(ps_mm[pb][:],
                                             w_sb[:, D * kk + 128 * j: D * kk + 128 * (j + 1)],
                                             act_fm[:, T * kk + 512 * ch: T * kk + 512 * (ch + 1)],
                                             start=(kk == 0), stop=(kk == ND - 1)), b.pe)
                    if sig:
                        b.wait("scalar", b.pe, mv)
                        b.inc(sc.activation(out=dst[:, 512 * ch:512 * (ch + 1)],
                                            in_=ps_mm[pb][:], func=AF.Sigmoid), b.ac)
                    else:
                        b.wait("vector", b.pe, mv)
                        b.inc(ve.tensor_copy(dst[:, 512 * ch:512 * (ch + 1)], ps_mm[pb][:]), b.dv)
            if dbg and j == 0:
                b.wait("gpsimd", b.ac)
                b.wait("gpsimd", b.dv)
                gv = b.inc(gp.dma_start(out=dbg_r, in_=r_sb[:, :]), b.gd, 16)
                gv = b.inc(gp.dma_start(out=dbg_k, in_=k_sb[:, :]), b.gd, 16)
                b.wait("gpsimd", b.gd, gv)
                b.wait("vector", b.gd, gv)
                b.wait("scalar", b.gd, gv)
            b.wait("vector", b.ac)
            for ch in range(NC5):
                pb = 2 + ch % 2
                b.wait("tensor", b.dv)
                for kk in range(ND):
                    mv = b.inc(te.matmul(ps_mm[pb][:],
                                         wT_v[:, D * kk + 128 * j: D * kk + 128 * (j + 1)],
                                         act_fm[:, T * kk + 512 * ch: T * kk + 512 * (ch + 1)],
                                         start=(kk == 0), stop=(kk == ND - 1)), b.pe)
                b.wait("vector", b.pe, mv)
                b.inc(ve.tensor_tensor(kvt[:], k_sb[:, 512 * ch:512 * (ch + 1)],
                                       ps_mm[pb][:], ALU.mult), b.dv)
                b.inc(ve.tensor_single_scalar(posw_t[:], pos0[:], float(512 * ch), ALU.add), b.dv)
                posw = posw_t[:]
                b.wait("scalar", b.dv)
                b.inc(sc.activation(out=sc1[:], in_=posw, func=AF.Exp,
                                    scale=lg_c[:, j:j + 1]), b.ac)
                ve.tensor_single_scalar(sc2[:], posw, nlg_c[:, j:j + 1], ALU.mult)
                ve.tensor_scalar_min(sc2[:], sc2[:], 23.1)
                sc.activation(out=sc2[:], in_=sc2[:], func=AF.Exp)
                b.inc(ve.tensor_scalar_min(sc2[:], sc2[:], 1e10), b.dv)
                b.inc(ve.tensor_mul(sc2[:], sc2[:], kvt[:]), b.dv)
                b.inc(ve.tensor_tensor_scan(out=cumt[:], data0=sc2[:], data1=zeros[:],
                                            initial=(0.0 if ch == 0 else carry[:, 0:1]),
                                            op0=ALU.add, op1=ALU.add), b.dv)
                b.inc(ve.tensor_copy(carry[:, 0:1], cumt[:, CW - 1:CW]), b.dv)
                if dbg and j == 0:
                    dvv0 = b.dv.v
                    b.wait("gpsimd", b.dv, dvv0)
                    gv = b.inc(gp.dma_start(out=dbg_kv[:, 512 * ch:512 * (ch + 1)], in_=kvt[:]), b.gd, 16)
                    gv = b.inc(gp.dma_start(out=dbg_scl[:, 512 * ch:512 * (ch + 1)], in_=cumt[:]), b.gd, 16)
                    b.wait("vector", b.gd, gv)
                b.inc(ve.tensor_mul(cumt[:], cumt[:], sc1[:]), b.dv)
                if dbg and j == 0:
                    dvv0 = b.dv.v
                    b.wait("gpsimd", b.dv, dvv0)
                    gv = b.inc(gp.dma_start(out=dbg_st[:, 512 * ch:512 * (ch + 1)], in_=cumt[:]), b.gd, 16)
                    b.wait("vector", b.gd, gv)
                b.inc(ve.tensor_tensor(sc2[:], cumt[:], r_sb[:, 512 * ch:512 * (ch + 1)],
                                       ALU.mult), b.dv)
                dvv = b.inc(ve.tensor_single_scalar(ysq[:, 512 * ch:512 * (ch + 1)], sc2[:],
                                             lnw_c[:, j:j + 1], ALU.mult), b.dv)
                b.wait("scalar", b.dv, dvv)
                acv = b.inc(sc.activation(out=sc2[:], in_=sc2[:], func=AF.Square), b.ac)
                b.wait("vector", b.ac, acv)
                b.wait("vector", b.pe)
                dvv = b.inc(ve.tensor_copy(sqb[:], sc2[:]), b.dv)
                b.wait("tensor", b.dv, dvv)
                mv = b.inc(te.matmul(ps_row[:, :], ones_col[:, :], sqb[:],
                                     start=True, stop=True), b.pe)
                b.wait("vector", b.pe, mv)
                b.inc(ve.tensor_tensor(ssq_row[:, 512 * ch:512 * (ch + 1)],
                                       ssq_row[:, 512 * ch:512 * (ch + 1)],
                                       ps_row[:, :], ALU.add), b.dv)
            dvv = b.dv.v
            b.wait("gpsimd", b.dv, dvv)
            gv = b.inc(gp.dma_start(out=y_dram.ap()[128 * j:128 * (j + 1), :],
                                    in_=ysq[:, :]), b.gd, 16)
            b.wait("vector", b.gd, gv)

        b.wait("scalar", b.dv)
        b.inc(sc.activation(out=ssq_row[:], in_=ssq_row[:], func=AF.Ln,
                            bias=cc[0:1, 1:2], scale=cc[0:1, 3:4]), b.ac)
        acv = b.inc(sc.activation(out=ssq_row[:], in_=ssq_row[:], func=AF.Exp, scale=cc[0:1, 2:3]), b.ac)
        b.wait("gpsimd", b.ac, acv)
        gv = b.inc(gp.dma_start(out=row_scr.ap(), in_=ssq_row[:]), b.gd, 16)
        b.wait("sync", b.gd, gv)
        ld = b.inc(sy.dma_start(out=rstd_blk[:, :],
                                in_=row_scr.ap().rearrange("a (c p) -> (a c) p", p=128)), b.sd, 16)
        b.wait("tensor", b.sd, ld)
        tv = b.inc(te.transpose(ps_t32[:, :], rstd_blk[:, :], identf[:, :]), b.pe)
        b.wait("vector", b.pe, tv)
        b.inc(ve.tensor_copy(rstd_cols[:, :], ps_t32[:, :]), b.dv)

        if dbg:
            b.wait("gpsimd", b.dv)
            gv = b.inc(gp.dma_start(out=dbg_ssq, in_=ssq_row[:, :]), b.gd, 16)
            gv = b.inc(gp.dma_start(out=dbg_rstd, in_=rstd_cols[:, :]), b.gd, 16)
            gv = b.inc(gp.dma_start(out=dbg_y, in_=y_dram.ap()), b.gd, 16)
            b.wait("gpsimd", b.gd, gv)
            b.wait("vector", b.gd, gv)
            b.wait("scalar", b.gd, gv)
        wo_sb = wT_r
        b.wait("sync", b.pe)
        for kk in range(ND):
            b.inc(sy.dma_start(out=wo_sb[:, D * kk:D * (kk + 1)],
                               in_=wT_o_dram.ap()[128 * kk:128 * (kk + 1), :]), b.sd, 16)
        for g in range(T // 256):
            b.wait("sync", b.pe)
            for kk in range(ND):
                b.inc(sy.dma_start(
                    out=ybuf[:, 256 * kk:256 * (kk + 1)],
                    in_=y_dram.ap()[128 * kk:128 * (kk + 1), 256 * g:256 * (g + 1)]), b.sd, 16)
            b.wait("tensor", b.sd)
            b.wait("tensor", b.dv)
            for tc in range(2):
                for nh in range(2):
                    pb = 2 * tc + nh
                    for kk in range(ND):
                        mv = b.inc(te.matmul(
                            ps_mm[pb][:],
                            ybuf[:, 256 * kk + 128 * tc: 256 * kk + 128 * (tc + 1)],
                            wo_sb[:, D * kk + 512 * nh: D * kk + 512 * (nh + 1)],
                            start=(kk == 0), stop=(kk == ND - 1)), b.pe)
            mm_done = b.pe.v
            for tc in range(2):
                tok0 = 256 * g + 128 * tc
                tcol = tok0 // 128
                b.wait("sync", b.dv)
                ldx = b.inc(sy.dma_start(out=xtk[:], in_=x_in[tok0:tok0 + 128, :]), b.sd, 16)
                b.wait("vector", b.sd, ldx)
                b.wait("vector", b.pe, mm_done)
                for nh in range(2):
                    pb = 2 * tc + nh
                    b.inc(ve.scalar_tensor_tensor(
                        x1tk[:, 512 * nh:512 * (nh + 1)], ps_mm[pb][:],
                        rstd_cols[:, tcol:tcol + 1], xtk[:, 512 * nh:512 * (nh + 1)],
                        ALU.mult, ALU.add), b.dv)
                dvv = b.dv.v
                b.wait("gpsimd", b.dv, dvv)
                gv = b.inc(gp.dma_start(out=x1_out[tok0:tok0 + 128, :], in_=x1tk[:]), b.gd, 16)
                b.wait("vector", b.gd, gv)

        for i in range(12):
            b.wait("sync", b.dv)
            ld = b.inc(sy.dma_start(out=wrow[:], in_=qkvw[128 * i:128 * (i + 1), :]), b.sd, 16)
            b.wait("vector", b.sd, ld)
            b.wait("vector", b.pe)
            dvv = b.inc(ve.tensor_copy(wq_b[:], wrow[:]), b.dv)
            b.wait("tensor", b.dv, dvv)
            for j in range(ND):
                pj = j % 2
                tv = b.inc(te.transpose(ps_tr[pj][:], wq_b[:, 128 * j:128 * (j + 1)], identb[:]), b.pe)
                b.wait("vector", b.pe, tv)
                dvv = b.inc(ve.tensor_copy(wT_qkv[:, 1536 * j + 128 * i: 1536 * j + 128 * (i + 1)],
                                           ps_tr[pj][:]), b.dv)
                b.wait("tensor", b.dv, dvv)

        b.wait("vector", b.pe)
        for t in range(NT):
            b.wait("sync", b.dv)
            b.wait("sync", b.ac)
            ld = b.inc(sy.dma_start(out=x1tk[:], in_=x1_out[128 * t:128 * (t + 1), :]), b.sd, 16)
            b.wait("vector", b.sd, ld)
            b.wait("scalar", b.sd, ld)
            emit_ln_chunk(b, x1tk, htk, stats, sq_sc, cc)
            emit_ln_chunk(b, htk, ntk, stats, sq_sc, cc)
            emit_tr_to_fm(b, ntk, act_fm, t, identb, ps_tr, affine=(ln1w_c, ln1b_c))
            b.wait("tensor", b.dv)
            for nh in range(3):
                pb = nh
                for kk in range(ND):
                    mv = b.inc(te.matmul(
                        ps_mm[pb][:],
                        act_fm[:, T * kk + 128 * t: T * kk + 128 * (t + 1)],
                        wT_qkv[:, 1536 * kk + 512 * nh: 1536 * kk + 512 * (nh + 1)],
                        start=(kk == 0), stop=(kk == ND - 1)), b.pe)
                b.wait("vector", b.pe, mv)
                b.inc(ve.tensor_copy(qkvtk[:, 512 * nh:512 * (nh + 1)], ps_mm[pb][:]), b.dv)
            dvv = b.dv.v
            b.wait("gpsimd", b.dv, dvv)
            gv = b.inc(gp.dma_start(out=qkv_out[128 * t:128 * (t + 1), :], in_=qkvtk[:]), b.gd, 16)
            b.wait("vector", b.gd, gv)
            b.wait("tensor", b.gd, gv)

        if dbg:
            b.wait("gpsimd", b.dv)
            gv = b.inc(gp.dma_start(out=dbg_xln1, in_=act_fm[:, :]), b.gd, 16)
            b.wait("gpsimd", b.gd, gv)
        b.wait("sync", b.gd)
        b.wait("sync", b.sd)
        b.realize()
    return nc


def _run_launch1(inputs, rows, dbg=False):
    nc = build_launch1(dbg=dbg)
    def m(row):
        return {
            "x": np.ascontiguousarray(row, np.float32),
            "Wr": inputs["L0_attn_Wr"], "Wk": inputs["L0_attn_Wk"],
            "Wv": inputs["L0_attn_Wv"], "Wo": inputs["L0_attn_Wo"],
            "decay": inputs["L0_attn_decay"].reshape(1, -1),
            "ln_w": inputs["L0_attn_ln_w"].reshape(1, -1),
            "ln1_w": inputs["L0_ffn_ln1_weight"].reshape(1, -1),
            "ln1_b": inputs["L0_ffn_ln1_bias"].reshape(1, -1),
            "qkvw": inputs["L0_ffn_qkv_weight"],
        }
    in_maps = [m(r) for r in rows]
    res = run_bass_kernel_spmd(nc, in_maps, core_ids=list(range(len(rows))))
    return res



def build_launch2():
    nc = bass.Bass("TRN2", target_bir_lowering=False)
    qkvs = nc.dram_tensor("qkvs", [T, 1536], BF16, kind="ExternalInput").ap()
    projw = nc.dram_tensor("projw", [D, D], F32, kind="ExternalInput").ap()
    po_out = nc.dram_tensor("po", [T, D], F32, kind="ExternalOutput").ap()

    NS = 8
    BL = 512
    NBT = BL // 128
    SCALE = 1.0 / 8.0

    with ExitStack() as ctx:
        b = B(nc, ctx)
        idb_dram = nc.inline_tensor(np.eye(128, dtype=BF16_NP), "ident_bf")
        identb = b.sbuf("identb", [128, 128], BF16)
        qt = [b.sbuf(f"qt{s}", [128, 1536], BF16) for s in range(NS)]
        scor = [b.sbuf(f"scor{s}", [128, 128], F32) for s in range(NS)]
        prod = b.sbuf("prod", [128, 1024], BF16)
        exps = b.sbuf("exps", [128, 128], F32)
        mx = b.sbuf("mx", [128, 16], F32)
        nmx = b.sbuf("nmx", [128, 16], F32)
        den = b.sbuf("den", [128, 16], F32)
        rden = b.sbuf("rden", [128, 16], F32)
        probs = b.sbuf("probs", [128, 128], BF16)
        tmp = b.sbuf("tmp", [128, 1024], F32)
        acc = b.sbuf("acc", [128, 1024], F32)
        attnb = b.sbuf("attnb", [128, 1024], BF16)
        attn_fm = b.sbuf("attn_fm", [128, ND * 128], BF16)
        potk = b.sbuf("potk", [128, D], F32)
        projT = b.sbuf("projT", [128, ND * D], BF16)
        wrow = b.sbuf("wrow", [128, D], F32)
        wq_b = b.sbuf("wq_b", [128, D], BF16)
        ntk128 = b.sbuf("ntk128", [128, 128], BF16)
        ps_tr = [b.psum(f"ps_tr{i}", [128, 128], BF16) for i in range(2)]
        ps_mm = [b.psum(f"ps_mm{i}", [128, 512], F32) for i in range(2)]

        sy, gp, ve, sc, te = b.sy, b.gp, b.ve, b.sc, b.te

        sy.dma_start(out=identb[:], in_=idb_dram.ap())

        for i in range(ND):
            sy.dma_start(out=wrow[:], in_=projw[128 * i:128 * (i + 1), :])
            ve.tensor_copy(wq_b[:], wrow[:])
            for j in range(ND):
                pj = j % 2
                te.transpose(ps_tr[pj][:], wq_b[:, 128 * j:128 * (j + 1)], identb[:])
                ve.tensor_copy(projT[:, D * j + 128 * i: D * j + 128 * (i + 1)], ps_tr[pj][:])

        def q3(s):
            return qt[s][:, 0:1024].rearrange("p (h d) -> p h d", d=64)

        def k3(t):
            a = qt[t][:, 1024:1280].rearrange("p (g o d) -> p g o d", o=1, d=64)
            return a.broadcast_to([128, 4, 4, 64])

        def v3(t):
            a = qt[t][:, 1280:1536].rearrange("p (g o d) -> p g o d", o=1, d=64)
            return a.broadcast_to([128, 4, 4, 64])

        for bt in range(NBT):
            for s in range(NS):
                sy.dma_start(out=qt[s][:], in_=qkvs[512 * s + 128 * bt: 512 * s + 128 * (bt + 1), :])
            for s in range(NS):
                ve.memset(scor[s][:], -1e30)
                for t in range(s + 1):
                    ve.tensor_tensor(prod[:].rearrange("p (h d) -> p h d", d=64),
                                     q3(s), k3(t), ALU.mult)
                    ve.tensor_reduce(
                        scor[s].rearrange("p (h t) -> p h t", t=8)[:, :, t:t + 1],
                        prod[:].rearrange("p (h d) -> p h d", d=64),
                        axis=AX.X, op=ALU.add)
                sc3 = scor[s].rearrange("p (h t) -> p h t", t=8)
                ve.tensor_single_scalar(scor[s][:], scor[s][:], SCALE, ALU.mult)
                ve.reduce_max(mx[:], sc3, axis=AX.X, negate=True)
                ve.tensor_tensor(exps[:].rearrange("p (h t) -> p h t", t=8), sc3,
                                 mx[:].rearrange("p (h o) -> p h o", o=1)
                                 .broadcast_to([128, 16, 8]), ALU.add)
                sc.activation(out=exps[:], in_=exps[:], func=AF.Exp)
                ve.reduce_sum(den[:], exps[:].rearrange("p (h t) -> p h t", t=8), axis=AX.X)
                ve.reciprocal(rden[:], den[:])
                ve.tensor_tensor(probs[:].rearrange("p (h t) -> p h t", t=8),
                                 exps[:].rearrange("p (h t) -> p h t", t=8),
                                 rden[:].rearrange("p (h o) -> p h o", o=1)
                                 .broadcast_to([128, 16, 8]), ALU.mult)
                ve.memset(acc[:], 0.0)
                for t in range(s + 1):
                    ve.tensor_tensor(tmp[:].rearrange("p (g o d) -> p g o d", g=4, o=4),
                                     v3(t),
                                     probs[:].rearrange("p (h t) -> p h t", t=8)[:, :, t:t + 1]
                                     .rearrange("p (g o) t -> p g o t", o=4)
                                     .broadcast_to([128, 4, 4, 64]),
                                     ALU.mult)
                    ve.tensor_add(acc[:], acc[:], tmp[:])
                ve.tensor_copy(attnb[:], acc[:])
                for j in range(ND):
                    pj = j % 2
                    te.transpose(ps_tr[pj][:], attnb[:, 128 * j:128 * (j + 1)], identb[:])
                    ve.tensor_copy(attn_fm[:, 128 * j:128 * (j + 1)], ps_tr[pj][:])
                for nh in range(2):
                    for kk in range(ND):
                        te.matmul(ps_mm[nh][:],
                                  attn_fm[:, 128 * kk:128 * (kk + 1)],
                                  projT[:, D * kk + 512 * nh: D * kk + 512 * (nh + 1)],
                                  start=(kk == 0), stop=(kk == ND - 1))
                    ve.tensor_copy(potk[:, 512 * nh:512 * (nh + 1)], ps_mm[nh][:])
                gp.dma_start(out=po_out[512 * s + 128 * bt: 512 * s + 128 * (bt + 1), :],
                             in_=potk[:])
        b.barrier("sync")
        b.realize()
    return nc


def _run_launch2(qkv_all, projw, cores=8):
    nc = build_launch2()
    in_maps = []
    for j in range(cores):
        sl = np.ascontiguousarray(
            qkv_all[:, 512 * j:512 * (j + 1), :].reshape(T, 1536))
        in_maps.append({"qkvs": sl.astype(BF16_NP),
                        "projw": np.ascontiguousarray(projw, np.float32)})
    return run_bass_kernel_spmd(nc, in_maps, core_ids=list(range(cores)))



def build_launch3a():
    nc = bass.Bass("TRN2", target_bir_lowering=False)
    x1_in = nc.dram_tensor("x1", [T, D], F32, kind="ExternalInput").ap()
    po_in = nc.dram_tensor("po", [T, D], F32, kind="ExternalInput").ap()
    ln2_w = nc.dram_tensor("ln2_w", [1, D], F32, kind="ExternalInput").ap()
    ln2_b = nc.dram_tensor("ln2_b", [1, D], F32, kind="ExternalInput").ap()
    routw = nc.dram_tensor("routw", [8, D], F32, kind="ExternalInput").ap()
    moe_w1 = nc.dram_tensor("moe_w1", [8, D, 2048], F32, kind="ExternalInput").ap()
    moe_w2 = nc.dram_tensor("moe_w2", [8, 2048, D], F32, kind="ExternalInput").ap()
    x2_out = nc.dram_tensor("x2", [T, D], F32, kind="ExternalOutput").ap()

    NE, F = 8, 2048
    NFT = F // 128

    with ExitStack() as ctx:
        b = B(nc, ctx)
        idb_dram = nc.inline_tensor(np.eye(128, dtype=BF16_NP), "ident_bf")
        cc_np = np.zeros((128, 8), np.float32)
        cc_np[:, 0] = LNEPS; cc_np[:, 1] = RMSEPS; cc_np[:, 2] = -0.5; cc_np[:, 3] = 1.0 / D
        cc_dram = nc.inline_tensor(cc_np, "consts_cc")
        identb = b.sbuf("identb", [128, 128], BF16)
        cc = b.sbuf("cc", [128, 8], F32)
        xln2_fm = b.sbuf("xln2_fm", [128, ND * T], BF16)
        w1sb = b.sbuf("w1sb", [128, ND * F], BF16)
        w2sb = b.sbuf("w2sb", [128, NFT * D], BF16)
        h_sb = b.sbuf("h_sb", [128, NFT * 512], BF16)
        routT = b.sbuf("routT", [128, ND * 8], BF16)
        wts_all = b.sbuf("wts_all", [128, 8 * NT], F32)
        ln2w_c = b.sbuf("ln2w_c", [128, ND], F32)
        ln2b_c = b.sbuf("ln2b_c", [128, ND], F32)
        x1tk = b.sbuf("x1tk", [128, D], F32)
        potk = b.sbuf("potk", [128, D], F32)
        htk = b.sbuf("htk", [128, D], F32)
        hattk = b.sbuf("hattk", [128, D], F32)
        x2ptk = b.sbuf("x2ptk", [128, D], F32)
        ntk = b.sbuf("ntk", [128, D], BF16)
        h2tk = b.sbuf("h2tk", [128, D], F32)
        stats = b.sbuf("stats", [128, 8], F32)
        sq_sc = b.sbuf("sq_sc", [128, D], F32)
        glog = b.sbuf("glog", [128, 8], F32)
        gexp = b.sbuf("gexp", [128, 8], F32)
        gcols = b.sbuf("gcols", [128, 16], F32)
        wrow = b.sbuf("wrow", [128, D], F32)
        wq_b = b.sbuf("wq_b", [128, D], BF16)
        ps_tr = [b.psum(f"ps_tr{i}", [128, 128], BF16) for i in range(2)]
        ps_h = [b.psum(f"ps_h{i}", [128, 512], F32) for i in range(2)]
        ps_o = [b.psum(f"ps_o{i}", [128, 512], F32) for i in range(2)]
        ps_r = b.psum("ps_r", [128, 8], F32)

        sy, gp, ve, sc, te = b.sy, b.gp, b.ve, b.sc, b.te

        sy.dma_start(out=identb[:], in_=idb_dram.ap())
        sy.dma_start(out=cc[:], in_=cc_dram.ap())
        emit_param_cols(b, ln2_w, ln2w_c)
        emit_param_cols(b, ln2_b, ln2b_c)
        sy.dma_start(out=wrow[0:8, :], in_=routw)
        ve.tensor_copy(wq_b[0:8, :], wrow[0:8, :])
        for j in range(ND):
            pj = j % 2
            te.transpose(ps_tr[pj][:, 0:8], wq_b[0:8, 128 * j:128 * (j + 1)], identb[0:8, 0:8])
            ve.tensor_copy(routT[:, 8 * j:8 * (j + 1)], ps_tr[pj][:, 0:8])

        for t in range(NT):
            sy.dma_start(out=x1tk[:], in_=x1_in[128 * t:128 * (t + 1), :])
            sy.dma_start(out=potk[:], in_=po_in[128 * t:128 * (t + 1), :])
            emit_ln_chunk(b, x1tk, htk, stats, sq_sc, cc)
            ve.tensor_add(hattk[:], htk[:], potk[:])
            ve.tensor_add(x2ptk[:], x1tk[:], hattk[:])
            gp.dma_start(out=x2_out[128 * t:128 * (t + 1), :], in_=x2ptk[:])
            emit_ln_chunk(b, hattk, ntk, stats, sq_sc, cc)
            emit_tr_to_fm(b, ntk, xln2_fm, t, identb, ps_tr, affine=(ln2w_c, ln2b_c))
            for kk in range(ND):
                te.matmul(ps_r[:], xln2_fm[:, T * kk + 128 * t: T * kk + 128 * (t + 1)],
                          routT[:, 8 * kk:8 * (kk + 1)],
                          start=(kk == 0), stop=(kk == ND - 1))
            ve.tensor_copy(glog[:], ps_r[:])
            ve.reduce_max(gcols[:, 0:1], glog[:], axis=AX.X, negate=True)
            ve.tensor_tensor(gexp[:], glog[:],
                             gcols[:, 0:1].broadcast_to([128, 8]), ALU.add)
            sc.activation(out=gexp[:], in_=gexp[:], func=AF.Exp)
            ve.reduce_sum(gcols[:, 1:2], gexp[:], axis=AX.X)
            ve.reciprocal(gcols[:, 2:3], gcols[:, 1:2])
            ve.tensor_tensor(glog[:], gexp[:],
                             gcols[:, 2:3].broadcast_to([128, 8]), ALU.mult)
            ve.reduce_max(gcols[:, 3:4], glog[:], axis=AX.X)
            ve.tensor_single_scalar(gexp[:], glog[:], gcols[:, 3:4], ALU.is_ge)
            ve.reduce_sum(gcols[:, 4:5], gexp[:], axis=AX.X)
            ve.scalar_tensor_tensor(gexp[:], gexp[:], -1e30, glog[:], ALU.mult, ALU.add)
            ve.reduce_max(gcols[:, 5:6], gexp[:], axis=AX.X)
            ve.tensor_single_scalar(gcols[:, 6:7], gcols[:, 4:5], 2.0, ALU.is_ge)
            ve.tensor_sub(gcols[:, 7:8], gcols[:, 3:4], gcols[:, 5:6])
            ve.tensor_mul(gcols[:, 7:8], gcols[:, 6:7], gcols[:, 7:8])
            ve.tensor_add(gcols[:, 5:6], gcols[:, 5:6], gcols[:, 7:8])
            ve.tensor_single_scalar(gexp[:], glog[:], gcols[:, 5:6], ALU.is_ge)
            ve.tensor_tensor(wts_all[:, 8 * t:8 * (t + 1)], glog[:], gexp[:], ALU.mult)

        for e in range(NE):
            for kk in range(ND):
                gp.dma_start(out=w1sb[:, F * kk:F * (kk + 1)],
                             in_=moe_w1[e, 128 * kk:128 * (kk + 1), :])
            for ft in range(NFT):
                gp.dma_start(out=w2sb[:, D * ft:D * (ft + 1)],
                             in_=moe_w2[e, 128 * ft:128 * (ft + 1), :])
            for g in range(T // 512):
                for ft in range(NFT):
                    pb = ft % 2
                    for kk in range(ND):
                        te.matmul(ps_h[pb][:],
                                  w1sb[:, F * kk + 128 * ft: F * kk + 128 * (ft + 1)],
                                  xln2_fm[:, T * kk + 512 * g: T * kk + 512 * (g + 1)],
                                  start=(kk == 0), stop=(kk == ND - 1))
                    sc.activation(out=h_sb[:, 512 * ft:512 * (ft + 1)], in_=ps_h[pb][:],
                                  func=AF.Gelu_apprx_tanh)
                for tc in range(4):
                    trow = 4 * g + tc
                    for nh in range(2):
                        pb = nh
                        for ft in range(NFT):
                            te.matmul(ps_o[pb][:],
                                      h_sb[:, 512 * ft + 128 * tc: 512 * ft + 128 * (tc + 1)],
                                      w2sb[:, D * ft + 512 * nh: D * ft + 512 * (nh + 1)],
                                      start=(ft == 0), stop=(ft == NFT - 1))
                        ve.tensor_single_scalar(h2tk[:, 512 * nh:512 * (nh + 1)], ps_o[pb][:],
                                                wts_all[:, 8 * trow + e:8 * trow + e + 1],
                                                ALU.mult)
                    gp.dma_start(out=x2_out[128 * trow:128 * (trow + 1), :], in_=h2tk[:],
                                 accum_op=ALU.add)
        b.barrier("sync")
        b.realize()
    return nc


def _run_launch3a(inputs, x1_all, po_all, cores=8):
    nc = build_launch3a()
    in_maps = []
    for c in range(cores):
        in_maps.append({
            "x1": np.ascontiguousarray(x1_all[c], np.float32),
            "po": np.ascontiguousarray(po_all[c], np.float32),
            "ln2_w": inputs["L0_ffn_ln2_weight"].reshape(1, -1),
            "ln2_b": inputs["L0_ffn_ln2_bias"].reshape(1, -1),
            "routw": inputs["L0_ffn_router_weight"],
            "moe_w1": inputs["L0_ffn_moe_w1"],
            "moe_w2": inputs["L0_ffn_moe_w2"],
        })
    return run_bass_kernel_spmd(nc, in_maps, core_ids=list(range(cores)))



def build_launch3b(stage=5):
    nc = bass.Bass("TRN2", target_bir_lowering=False)
    x2_in = nc.dram_tensor("x2", [T, D], F32, kind="ExternalInput").ap()
    wk_in = nc.dram_tensor("Wk", [D, D], F32, kind="ExternalInput").ap()
    wq_in = nc.dram_tensor("Wq", [D, D], F32, kind="ExternalInput").ap()
    wv_in = nc.dram_tensor("Wv", [D, D], F32, kind="ExternalInput").ap()
    wo_in = nc.dram_tensor("Wo", [D, D], F32, kind="ExternalInput").ap()
    bo_in = nc.dram_tensor("bo", [1, D], F32, kind="ExternalInput").ap()
    x3_out = nc.dram_tensor("x3", [T, D], F32, kind="ExternalOutput").ap()

    NL = 64
    HS = 64
    NPAIR = 8
    QS = 0.125

    with ExitStack() as ctx:
        b = B(nc, ctx)
        idb_dram = nc.inline_tensor(np.eye(128, dtype=BF16_NP), "ident_bf")
        idf_dram = nc.inline_tensor(np.eye(128, dtype=np.float32), "ident_f32")
        cc_np = np.zeros((128, 8), np.float32)
        cc_np[:, 0] = LNEPS; cc_np[:, 2] = -0.5; cc_np[:, 3] = 1.0 / D
        cc_dram = nc.inline_tensor(cc_np, "consts_cc")
        ii_np = np.concatenate([np.eye(64, dtype=np.float32)] * 2, axis=1)
        ii_dram = nc.inline_tensor(ii_np, "eye_sbs")
        onesc_dram = nc.inline_tensor(np.ones((128, 1), dtype=BF16_NP), "ones_col_d")
        onesr_dram = nc.inline_tensor(np.ones((1, 128), dtype=BF16_NP), "ones_row_d")
        onescf_dram = nc.inline_tensor(np.ones((64, 1), dtype=np.float32), "ones_col_f_d")

        identb = b.sbuf("identb", [128, 128], BF16)
        identf = b.sbuf("identf", [128, 128], F32)
        cc = b.sbuf("cc", [128, 8], F32)
        iisb = b.sbuf("iisb", [64, 128], F32)
        ones_col = b.sbuf("ones_col", [128, 1], BF16)
        ones_row = b.sbuf("ones_row", [1, 128], BF16)
        ones_col_f = b.sbuf("ones_col_f", [64, 1], F32)
        xln3_fm = b.sbuf("xln3_fm", [128, ND * T], BF16)
        attn_all = b.sbuf("attn_all", [128, NT * D], BF16)
        qfm2 = b.sbuf("qfm2", [64, 2 * T], BF16)
        kfm2 = b.sbuf("kfm2", [64, 2 * T], BF16)
        vtk = b.sbuf("vtk", [128, NT * 128], BF16)
        e1g = b.sbuf("e1g", [64, 1024], BF16)
        wslc = b.sbuf("wslc", [128, ND * 128], BF16)
        ql2 = b.sbuf("ql2", [64, 2 * NL], F32)
        kl2 = b.sbuf("kl2", [64, 2 * NL], F32)
        qlb2 = b.sbuf("qlb2", [64, 2 * NL], BF16)
        klb2 = b.sbuf("klb2", [64, 2 * NL], BF16)
        s2sm = b.sbuf("s2sm", [64, 128], F32)
        s2t = b.sbuf("s2t", [64, 128], F32)
        zcur = b.sbuf("zcur", [64, 128], F32)
        znew = b.sbuf("znew", [64, 128], F32)
        xz = b.sbuf("xz", [64, 128], F32)
        xzT = b.sbuf("xzT", [64, 128], F32)
        zT = b.sbuf("zT", [64, 128], F32)
        wtmp = b.sbuf("wtmp", [64, 128], F32)
        s3v_sb = b.sbuf("s3v_sb", [64, 132], F32)
        s3vn = b.sbuf("s3vn", [64, 128], F32)
        wsbs = b.sbuf("wsbs", [64, 128], BF16)
        rr2 = b.sbuf("rr2", [64, 8], F32)
        rrow = b.sbuf("rrow", [1, 8], F32)
        rrowb = b.sbuf("rrowb", [1, 8], BF16)
        mxc2 = b.sbuf("mxc2", [64, 8], F32)
        m3c = b.sbuf("m3c", [64, 8], F32)
        gmx2 = b.sbuf("gmx2", [64, 2], F32)
        mxrow = b.sbuf("mxrow", [1, 128], F32)
        mxrowb = b.sbuf("mxrowb", [1, 128], BF16)

        rden1 = b.sbuf("rden1", [128, 2], F32)
        xtk = b.sbuf("xtk", [128, D], F32)
        ntk = b.sbuf("ntk", [128, D], BF16)
        x3tk = b.sbuf("x3tk", [128, D], F32)
        stats = b.sbuf("stats", [128, 8], F32)
        sq_sc = b.sbuf("sq_sc", [128, D], F32)
        bo_rep = b.sbuf("bo_rep", [128, D], F32)
        _qaddr = nc.lookup_mloc(qfm2).addr
        wo_sb_h = nc.alloc_sbuf_tensor_at("wo_sb", [128, ND * D], BF16, offset=_qaddr)
        wo_sb = wo_sb_h
        attn_fm = b.sbuf("attn_fm", [128, ND * 128], BF16)

        ps_tr = [b.psum(f"ps_tr{i}", [128, 128], BF16) for i in range(2)]
        ps_big = [b.psum(f"ps_big{i}", [128, 512], F32) for i in range(2)]
        ps_s3v = b.psum("ps_s3v", [64, 132], F32)
        ps_att = [b.psum(f"ps_att{i}", [128, 132], F32) for i in range(2)]
        ps_sm = b.psum("ps_sm", [128, 128], F32)

        sy, gp, ve, sc, te = b.sy, b.gp, b.ve, b.sc, b.te

        sy.dma_start(out=identb[:], in_=idb_dram.ap())
        sy.dma_start(out=identf[:], in_=idf_dram.ap())
        sy.dma_start(out=cc[:], in_=cc_dram.ap())
        sy.dma_start(out=iisb[:], in_=ii_dram.ap())
        sy.dma_start(out=ones_col[:], in_=onesc_dram.ap())
        sy.dma_start(out=ones_row[:], in_=onesr_dram.ap())
        sy.dma_start(out=ones_col_f[:], in_=onescf_dram.ap())
        sy.dma_start(out=bo_rep[0:1, :], in_=bo_in)
        emit_dma_bcast_rows(b, bo_rep, D)

        for t in range(NT):
            sy.dma_start(out=xtk[:], in_=x2_in[128 * t:128 * (t + 1), :])
            emit_ln_chunk(b, xtk, ntk, stats, sq_sc, cc)
            emit_tr_to_fm(b, ntk, xln3_fm, t, identb, ps_tr)

        if stage >= 2:
            ve.memset(attn_all[:, 0:NT * D], 0.0)
        for hp in range(NPAIR) if stage >= 2 else []:
            c0 = 128 * hp
            for kk in range(ND):
                gp.dma_start(out=wslc[:, 128 * kk:128 * (kk + 1)],
                             in_=wq_in[128 * kk:128 * (kk + 1), c0:c0 + 128])
            for h in range(2):
                for ch in range(NC5):
                    pb = ch % 2
                    for kk in range(ND):
                        te.matmul(ps_big[pb][0:64, 0:512],
                                  wslc[:, 128 * kk + 64 * h: 128 * kk + 64 * h + 64],
                                  xln3_fm[:, T * kk + 512 * ch: T * kk + 512 * (ch + 1)],
                                  start=(kk == 0), stop=(kk == ND - 1))
                    ve.tensor_single_scalar(qfm2[0:64, T * h + 512 * ch: T * h + 512 * (ch + 1)],
                                            ps_big[pb][0:64, 0:512], QS, ALU.mult)
                    ve.tensor_reduce(ql2[0:64, 64 * h + 8 * ch: 64 * h + 8 * (ch + 1)],
                                     qfm2[0:64, T * h + 512 * ch: T * h + 512 * (ch + 1)]
                                     .rearrange("p (l s) -> p l s", s=64),
                                     axis=AX.X, op=ALU.add)
            ve.tensor_single_scalar(ql2[0:64, :], ql2[0:64, :], 1.0 / 64, ALU.mult)
            ve.tensor_copy(qlb2[0:64, :], ql2[0:64, :])
            for kk in range(ND):
                gp.dma_start(out=wslc[:, 128 * kk:128 * (kk + 1)],
                             in_=wk_in[128 * kk:128 * (kk + 1), c0:c0 + 128])
            for h in range(2):
                for ch in range(NC5):
                    pb = ch % 2
                    for kk in range(ND):
                        te.matmul(ps_big[pb][0:64, 0:512],
                                  wslc[:, 128 * kk + 64 * h: 128 * kk + 64 * h + 64],
                                  xln3_fm[:, T * kk + 512 * ch: T * kk + 512 * (ch + 1)],
                                  start=(kk == 0), stop=(kk == ND - 1))
                    ve.tensor_copy(kfm2[0:64, T * h + 512 * ch: T * h + 512 * (ch + 1)],
                                   ps_big[pb][0:64, 0:512])
                    ve.tensor_reduce(kl2[0:64, 64 * h + 8 * ch: 64 * h + 8 * (ch + 1)],
                                     kfm2[0:64, T * h + 512 * ch: T * h + 512 * (ch + 1)]
                                     .rearrange("p (l s) -> p l s", s=64),
                                     axis=AX.X, op=ALU.add)
            ve.tensor_single_scalar(kl2[0:64, :], kl2[0:64, :], 1.0 / 64, ALU.mult)
            ve.tensor_copy(klb2[0:64, :], kl2[0:64, :])
            for kk in range(ND):
                gp.dma_start(out=wslc[:, 128 * kk:128 * (kk + 1)],
                             in_=wv_in[128 * kk:128 * (kk + 1), c0:c0 + 128])
            for t in range(NT):
                pb = t % 2
                for kk in range(ND):
                    te.matmul(ps_big[pb][:, 0:128],
                              xln3_fm[:, T * kk + 128 * t: T * kk + 128 * (t + 1)],
                              wslc[:, 128 * kk:128 * (kk + 1)],
                              start=(kk == 0), stop=(kk == ND - 1))
                ve.tensor_copy(vtk[:, 128 * t:128 * (t + 1)], ps_big[pb][:, 0:128])

            if stage < 3:
                continue
            for h in range(2):
                te.matmul(ps_sm[0:64, 64 * h:64 * (h + 1)],
                          qlb2[0:64, 64 * h:64 * (h + 1)], klb2[0:64, 64 * h:64 * (h + 1)],
                          start=(h == 0), stop=(h == 1))
            for h in range(2):
                sl = ps_sm[0:64, 64 * h:64 * (h + 1)]
                ve.reduce_max(mxc2[0:64, h:h + 1], sl, axis=AX.X, negate=True)
                sc.activation(out=s2sm[:, 64 * h:64 * (h + 1)], in_=sl, func=AF.Exp,
                              bias=mxc2[0:64, h:h + 1])
                ve.reduce_sum(mxc2[0:64, 2 + h:3 + h], s2sm[:, 64 * h:64 * (h + 1)],
                              axis=AX.X)
                ve.reciprocal(mxc2[0:64, 4 + h:5 + h], mxc2[0:64, 2 + h:3 + h])
                ve.tensor_single_scalar(s2sm[:, 64 * h:64 * (h + 1)],
                                        s2sm[:, 64 * h:64 * (h + 1)],
                                        mxc2[0:64, 4 + h:5 + h], ALU.mult)
            if stage == 30:
                continue
            for h in range(2):
                te.transpose(ps_att[0][0:64, 64 * h:64 * (h + 1)],
                             s2sm[:, 64 * h:64 * (h + 1)], identf[0:64, 0:64])
            ve.tensor_copy(s2t[:], ps_att[0][0:64, 0:128])
            if stage == 31:
                continue
            te.matmul(ps_att[1][0:1, 0:128], ones_col_f[:, :],
                      s2sm[:, :], start=True, stop=True)
            ve.tensor_copy(mxrow[:, :], ps_att[1][0:1, 0:128])
            ve.reduce_max(rrow[0:1, 0:2], mxrow[:, :].rearrange("p (h n) -> p h n", n=64),
                          axis=AX.X)
            ve.tensor_copy(rrowb[0:1, 0:2], rrow[0:1, 0:2])
            te.matmul(ps_att[1][0:64, 0:2], ones_row[0:1, 0:64], rrowb[0:1, 0:2],
                      start=True, stop=True)
            ve.tensor_copy(rr2[:, 0:2], ps_att[1][0:64, 0:2])
            ve.reciprocal(rr2[:, 2:4], rr2[:, 0:2])
            for h in range(2):
                ve.tensor_single_scalar(zcur[:, 64 * h:64 * (h + 1)],
                                        s2t[:, 64 * h:64 * (h + 1)],
                                        rr2[:, 2 + h:3 + h], ALU.mult)
            if stage == 32:
                continue
            for it in range(6):
                for h in range(2):
                    te.matmul(ps_sm[0:64, 64 * h:64 * (h + 1)],
                              s2t[:, 64 * h:64 * (h + 1)],
                              zcur[:, 64 * h:64 * (h + 1)], start=(h == 0), stop=(h == 1))
                ve.tensor_copy(xz[:], ps_sm[0:64, 0:128])
                if stage == 33:
                    break
                for h in range(2):
                    te.transpose(ps_sm[0:64, 64 * h:64 * (h + 1)],
                                 xz[:, 64 * h:64 * (h + 1)], identf[0:64, 0:64])
                ve.tensor_copy(xzT[:], ps_sm[0:64, 0:128])
                ve.scalar_tensor_tensor(wtmp[:], xz[:], -1.0 / 7.0, iisb[:],
                                        ALU.mult, ALU.add)
                ve.tensor_single_scalar(wtmp[:], wtmp[:], 7.0, ALU.mult)
                for h in range(2):
                    te.matmul(ps_sm[0:64, 64 * h:64 * (h + 1)],
                              xzT[:, 64 * h:64 * (h + 1)],
                              wtmp[:, 64 * h:64 * (h + 1)], start=(h == 0), stop=(h == 1))
                ve.tensor_copy(wtmp[:], ps_sm[0:64, 0:128])
                ve.scalar_tensor_tensor(wtmp[:], wtmp[:], -1.0 / 15.0, iisb[:],
                                        ALU.mult, ALU.add)
                ve.tensor_single_scalar(wtmp[:], wtmp[:], 15.0, ALU.mult)
                for h in range(2):
                    te.matmul(ps_sm[0:64, 64 * h:64 * (h + 1)],
                              xzT[:, 64 * h:64 * (h + 1)],
                              wtmp[:, 64 * h:64 * (h + 1)], start=(h == 0), stop=(h == 1))
                ve.tensor_copy(wtmp[:], ps_sm[0:64, 0:128])
                ve.scalar_tensor_tensor(wtmp[:], wtmp[:], -1.0 / 13.0, iisb[:],
                                        ALU.mult, ALU.add)
                ve.tensor_single_scalar(wtmp[:], wtmp[:], 13.0, ALU.mult)
                for h in range(2):
                    te.transpose(ps_sm[0:64, 64 * h:64 * (h + 1)],
                                 zcur[:, 64 * h:64 * (h + 1)], identf[0:64, 0:64])
                ve.tensor_copy(zT[:], ps_sm[0:64, 0:128])
                for h in range(2):
                    te.matmul(ps_sm[0:64, 64 * h:64 * (h + 1)],
                              zT[:, 64 * h:64 * (h + 1)],
                              wtmp[:, 64 * h:64 * (h + 1)], start=(h == 0), stop=(h == 1))
                ve.tensor_single_scalar(zcur[:], ps_sm[0:64, 0:128], 0.25, ALU.mult)

            if stage < 4:
                continue
            for h in range(2):
                for ch in range(NC5):
                    pb = ch % 2
                    te.matmul(ps_big[pb][0:64, 0:512],
                              qlb2[0:64, 64 * h:64 * (h + 1)],
                              kfm2[0:64, T * h + 512 * ch: T * h + 512 * (ch + 1)],
                              start=True, stop=True)
                    if ch == 0:
                        ve.reduce_max(m3c[0:64, h:h + 1], ps_big[pb][0:64, 0:512], axis=AX.X)
                    else:
                        ve.reduce_max(m3c[0:64, 2:3], ps_big[pb][0:64, 0:512], axis=AX.X)
                        ve.tensor_max(m3c[0:64, h:h + 1], m3c[0:64, h:h + 1], m3c[0:64, 2:3])
            for h in range(2):
                te.transpose(ps_sm[0:1, 64 * h:64 * h + 64], m3c[0:64, h:h + 1],
                             identf[0:64, 0:64])
            ve.tensor_single_scalar(mxrow[0:1, 0:128], ps_sm[0:1, 0:128], -1.0, ALU.mult)
            ve.tensor_copy(mxrowb[:, :], mxrow[:, :])
            for t in range(NT):
                pb = t % 2
                for h in range(2):
                    te.matmul(ps_att[pb][:, 64 * h:64 * (h + 1)],
                              kfm2[0:64, T * h + 128 * t: T * h + 128 * (t + 1)],
                              qlb2[0:64, 64 * h:64 * (h + 1)], start=(h == 0), stop=False)
                    te.matmul(ps_att[pb][:, 64 * h:64 * (h + 1)],
                              ones_row[0:1, 0:128],
                              mxrowb[0:1, 64 * h:64 * (h + 1)], start=False,
                              stop=(h == 1))
                sc.activation(out=ntk[:, 0:128], in_=ps_att[pb][:, 0:128], func=AF.Exp)
                for h in range(2):
                    te.matmul(ps_s3v[0:64, 66 * h:66 * h + 64],
                              ntk[:, 64 * h:64 * (h + 1)],
                              vtk[:, 128 * t:128 * (t + 1)][:, 64 * h:64 * (h + 1)],
                              start=(t == 0 and h == 0), stop=False)
                    te.matmul(ps_s3v[0:64, 66 * h + 64:66 * h + 65],
                              ntk[:, 64 * h:64 * (h + 1)],
                              ones_col[:, 0:1], start=False,
                              stop=(t == NT - 1 and h == 1))
            for h in range(2):
                ve.tensor_copy(s3v_sb[:, 66 * h:66 * h + 65],
                               ps_s3v[0:64, 66 * h:66 * h + 65])
            for h in range(2):
                ve.reciprocal(rr2[:, 4 + h:5 + h], s3v_sb[:, 66 * h + 64:66 * h + 65])
                ve.tensor_single_scalar(s3vn[:, 64 * h:64 * (h + 1)],
                                        s3v_sb[:, 66 * h:66 * h + 64],
                                        rr2[:, 4 + h:5 + h], ALU.mult)
            for h in range(2):
                te.transpose(ps_sm[0:64, 64 * h:64 * (h + 1)],
                             zcur[:, 64 * h:64 * (h + 1)], identf[0:64, 0:64])
            ve.tensor_copy(zT[:], ps_sm[0:64, 0:128])
            for h in range(2):
                te.matmul(ps_att[0][0:64, 64 * h:64 * (h + 1)],
                          zT[:, 64 * h:64 * (h + 1)], s3vn[:, 64 * h:64 * (h + 1)],
                          start=(h == 0), stop=(h == 1))
            ve.tensor_copy(wsbs[:, :], ps_att[0][0:64, 0:128])

            if stage < 5:
                continue
            for h in range(2):
                for ch in range(NC5):
                    pb = ch % 2
                    te.matmul(ps_big[pb][0:64, 0:512],
                              klb2[0:64, 64 * h:64 * (h + 1)],
                              qfm2[0:64, T * h + 512 * ch: T * h + 512 * (ch + 1)],
                              start=True, stop=True)
                    if ch == 0:
                        ve.reduce_max(m3c[0:64, 4 + h:5 + h], ps_big[pb][0:64, 0:512],
                                      axis=AX.X)
                    else:
                        ve.reduce_max(m3c[0:64, 6:7], ps_big[pb][0:64, 0:512], axis=AX.X)
                        ve.tensor_max(m3c[0:64, 4 + h:5 + h], m3c[0:64, 4 + h:5 + h],
                                      m3c[0:64, 6:7])
            for h in range(2):
                te.transpose(ps_sm[0:1, 64 * h:64 * h + 64], m3c[0:64, 4 + h:5 + h],
                             identf[0:64, 0:64])
            ve.tensor_copy(mxrow[0:1, 0:128], ps_sm[0:1, 0:128])
            ve.reduce_max(rrow[0:1, 4:6],
                          mxrow[:, :].rearrange("p (h n) -> p h n", n=64), axis=AX.X)
            ve.tensor_copy(rrowb[0:1, 4:6], rrow[0:1, 4:6])
            te.matmul(ps_att[1][0:64, 4:6], ones_row[0:1, 0:64], rrowb[0:1, 4:6],
                      start=True, stop=True)
            ve.tensor_single_scalar(gmx2[0:64, 0:2], ps_att[1][0:64, 4:6], -1.0, ALU.mult)
            for ch in range(NC5):
                for h in range(2):
                    pb = h
                    te.matmul(ps_big[pb][0:64, 0:512],
                              klb2[0:64, 64 * h:64 * (h + 1)],
                              qfm2[0:64, T * h + 512 * ch: T * h + 512 * (ch + 1)],
                              start=True, stop=True)
                    sc.activation(out=e1g[0:64, 512 * h:512 * (h + 1)],
                                  in_=ps_big[pb][0:64, 0:512],
                                  func=AF.Exp, bias=gmx2[0:64, h:h + 1])
                for tc in range(4):
                    t = 4 * ch + tc
                    pb = tc % 2
                    for h in range(2):
                        te.matmul(ps_att[pb][:, 66 * h:66 * h + 64],
                                  e1g[0:64, 512 * h + 128 * tc: 512 * h + 128 * (tc + 1)],
                                  wsbs[:, 64 * h:64 * (h + 1)], start=(h == 0), stop=False)
                        te.matmul(ps_att[pb][:, 66 * h + 64:66 * h + 65],
                                  e1g[0:64, 512 * h + 128 * tc: 512 * h + 128 * (tc + 1)],
                                  ones_col[0:64, 0:1], start=False, stop=(h == 1))
                    for h in range(2):
                        ve.reciprocal(rden1[:, h:h + 1], ps_att[pb][:, 66 * h + 64:66 * h + 65])
                        ve.tensor_single_scalar(
                            attn_all[:, D * t + c0 + 64 * h: D * t + c0 + 64 * (h + 1)],
                            ps_att[pb][:, 66 * h:66 * h + 64],
                            rden1[:, h:h + 1], ALU.mult)

        b.barrier("gpsimd")
        if stage < 2:
            for t in range(NT):
                sy.dma_start(out=xtk[:], in_=x2_in[128 * t:128 * (t + 1), :])
                ve.tensor_copy(x3tk[:], xtk[:])
                gp.dma_start(out=x3_out[128 * t:128 * (t + 1), :], in_=x3tk[:])
            b.barrier("sync")
            b.realize()
            return nc
        for kk in range(ND):
            gp.dma_start(out=wo_sb[:, D * kk:D * (kk + 1)],
                         in_=wo_in[128 * kk:128 * (kk + 1), :])
        for t in range(NT):
            for j in range(ND):
                pj = j % 2
                te.transpose(ps_tr[pj][:], attn_all[:, D * t + 128 * j: D * t + 128 * (j + 1)],
                             identb[:])
                ve.tensor_copy(attn_fm[:, 128 * j:128 * (j + 1)], ps_tr[pj][:])
            sy.dma_start(out=xtk[:], in_=x2_in[128 * t:128 * (t + 1), :])
            for nh in range(2):
                pb = nh
                for kk in range(ND):
                    te.matmul(ps_big[pb][:],
                              attn_fm[:, 128 * kk:128 * (kk + 1)],
                              wo_sb[:, D * kk + 512 * nh: D * kk + 512 * (nh + 1)],
                              start=(kk == 0), stop=(kk == ND - 1))
                ve.tensor_tensor(x3tk[:, 512 * nh:512 * (nh + 1)], ps_big[pb][:],
                                 xtk[:, 512 * nh:512 * (nh + 1)], ALU.add)
            ve.tensor_add(x3tk[:], x3tk[:], bo_rep[:])
            gp.dma_start(out=x3_out[128 * t:128 * (t + 1), :], in_=x3tk[:])
        b.barrier("sync")
        b.realize()
    return nc


def _run_launch3b(inputs, x2_all, cores=8, stage=5):
    nc = build_launch3b(stage=stage)
    in_maps = []
    for c in range(cores):
        in_maps.append({
            "x2": np.ascontiguousarray(x2_all[c], np.float32),
            "Wk": inputs["L1_attn_Wk"], "Wq": inputs["L1_attn_Wq"],
            "Wv": inputs["L1_attn_Wv"], "Wo": inputs["L1_attn_Wo"],
            "bo": inputs["L1_attn_bo"].reshape(1, -1),
        })
    return run_bass_kernel_spmd(nc, in_maps, core_ids=list(range(cores)))



def build_launch3c(dbg=False):
    nc = bass.Bass("TRN2", target_bir_lowering=False)
    x3_in = nc.dram_tensor("x3", [T, D], F32, kind="ExternalInput").ap()
    gw_in = nc.dram_tensor("gw", [8, D], F32, kind="ExternalInput").ap()
    w1_in = nc.dram_tensor("w1", [8, D, 4096], F32, kind="ExternalInput").ap()
    w2_in = nc.dram_tensor("w2", [8, 4096, D], F32, kind="ExternalInput").ap()
    out_d = nc.dram_tensor("out", [T, D], F32, kind="ExternalOutput").ap()

    NE = 8
    FF = 4096
    NFT = FF // 128
    KSEL = 1024
    SLOTP = KSEL + 128
    NIT = 35
    if dbg == 3:
        NIT = 1

    if dbg:
        dbg_probs = nc.dram_tensor("dbg_probs", [128, 256], F32, kind="ExternalOutput").ap()
        dbg_lists2 = nc.dram_tensor("dbg_lists2", [128, 144], F32, kind="ExternalOutput").ap()
        dbg_lists = nc.dram_tensor("dbg_lists", [8, 2 * (KSEL + 128)], F32, kind="ExternalOutput").ap()
        dbg_tau = nc.dram_tensor("dbg_tau", [8, 16], F32, kind="ExternalOutput").ap()
        dbg_nc = nc.dram_tensor("dbg_nc", [128, 2], F32, kind="ExternalOutput").ap()
        dbg_rows = nc.dram_tensor("dbg_rows", [1, 512], F32, kind="ExternalOutput").ap()
        dbg_dst = nc.dram_tensor("dbg_dst", [128, 256], F32, kind="ExternalOutput").ap()
        dbg_ev = nc.dram_tensor("dbg_ev", [128, 512], F32, kind="ExternalOutput").ap()
        dbg_bis = nc.dram_tensor("dbg_bis", [128, 16], F32, kind="ExternalOutput").ap()
    xln4_dram = nc.dram_tensor("xln4_d", [T, D], BF16)
    probsE_d = nc.dram_tensor("probsE_d", [8, T], F32)
    lists2_d = nc.dram_tensor("lists2_d", [8 * (KSEL + 128) + 8, 2], F32)
    xsel_d = nc.dram_tensor("xsel_d", [SLOTP, D], BF16)

    with ExitStack() as ctx:
        b = B(nc, ctx)
        idb_dram = nc.inline_tensor(np.eye(128, dtype=BF16_NP), "ident_bf")
        idf_dram = nc.inline_tensor(np.eye(128, dtype=np.float32), "ident_f32")
        cc_np = np.zeros((128, 8), np.float32)
        cc_np[:, 0] = LNEPS; cc_np[:, 2] = -0.5; cc_np[:, 3] = 1.0 / D
        cc_dram = nc.inline_tensor(cc_np, "consts_cc")
        ind_np = []
        for ti in range(2):
            m = np.zeros((128, 8), dtype=BF16_NP)
            for p in range(128):
                m[p, 4 * ti + p // 32] = 1.0
            ind_np.append(m)
        indb_dram = [nc.inline_tensor(ind_np[ti], f"indb_d{ti}") for ti in range(2)]
        indtf_np = [np.ascontiguousarray(ind_np[ti].astype(np.float32).T) for ti in range(2)]
        indtf_dram = [nc.inline_tensor(indtf_np[ti], f"indtf_d{ti}") for ti in range(2)]
        chunkb_dram = nc.inline_tensor(
            (128.0 * (np.arange(128) % 32)).astype(np.float32).reshape(128, 1), "chunkb_d")
        ebase_np = np.zeros((1, 256), np.float32)
        for j in range(256):
            ebase_np[0, j] = float(SLOTP * (j // 32))
        ebase_dram = nc.inline_tensor(ebase_np, "ebase_d")
        zrow_dram = nc.inline_tensor(np.zeros((1, 256), np.float32), "zrow_d")

        identb = b.sbuf("identb", [128, 128], BF16)
        identf = b.sbuf("identf", [128, 128], F32)
        cc = b.sbuf("cc", [128, 8], F32)
        xln4_fm = b.sbuf("xln4_fm", [128, ND * T], BF16)
        h_sb = b.sbuf("h_sb", [128, NFT * KSEL], BF16)
        xsel_fm = b.sbuf("xsel_fm", [128, ND * KSEL], BF16)
        gateT = b.sbuf("gateT", [128, ND * 8], BF16)
        indb = [b.sbuf(f"indb{ti}", [128, 8], BF16) for ti in range(2)]
        indtf = [b.sbuf(f"indtf{ti}", [8, 128], F32) for ti in range(2)]
        chunkb = b.sbuf("chunkb", [128, 1], F32)
        ebase_r = b.sbuf("ebase_r", [1, 256], F32)
        zrow = b.sbuf("zrow", [1, 256], F32)
        pc = [b.sbuf(f"pc{ti}", [128, 128], F32) for ti in range(2)]
        pcm = [b.sbuf(f"pcm{ti}", [128, 128], F32) for ti in range(2)]
        evei = [b.sbuf(f"evei{ti}", [128, 256], F32) for ti in range(2)]
        maskb = b.sbuf("maskb", [128, 128], BF16)
        masku = b.sbuf("masku", [128, 128], U8)
        neg1 = b.sbuf("neg1", [128, 128], F32)
        vmask = b.sbuf("vmask", [128, 128], F32)
        incl = b.sbuf("incl", [128, 128], F32)
        dstc = [b.sbuf(f"dstc{ti}", [128, 128], F32) for ti in range(2)]
        dsti = b.sbuf("dsti", [128, 128], I32)
        ei_u = b.sbuf("ei_u", [128, 128], mybir.dt.uint32)
        zero128 = b.sbuf("zero128", [128, 128], F32)
        midrep = [b.sbuf(f"midrep{ti}", [128, 1], F32) for ti in range(2)]
        taurep = [b.sbuf(f"taurep{ti}", [128, 1], F32) for ti in range(2)]
        ncrow = [b.sbuf(f"ncrow{ti}", [128, 1], F32) for ti in range(2)]
        pcol = [b.sbuf(f"pcol{ti}", [128, 1], F32) for ti in range(2)]
        st8 = b.sbuf("st8", [8, 16], F32)
        prow = b.sbuf("prow", [1, 256], F32)
        nrow = b.sbuf("nrow", [1, 256], F32)
        nrow_i = b.sbuf("nrow_i", [1, 256], I32)
        xtk = b.sbuf("xtk", [128, D], F32)
        ntk = b.sbuf("ntk", [128, D], BF16)
        h2tk = b.sbuf("h2tk", [128, D], F32)
        stats = b.sbuf("stats", [128, 8], F32)
        sq_sc = b.sbuf("sq_sc", [128, D], F32)
        probs8 = b.sbuf("probs8", [128, 8], F32)
        gex = b.sbuf("gex", [128, 8], F32)
        gcol = b.sbuf("gcol", [128, 4], F32)
        wrow8 = b.sbuf("wrow8", [8, D], F32)
        wq8 = b.sbuf("wq8", [8, D], BF16)
        gates8 = b.sbuf("gates8", [128, 8], F32)
        xsel_sb = b.sbuf("xsel_sb", [128, D], BF16)
        gi2 = b.sbuf("gi2", [128, 16], F32)
        mx8 = b.sbuf("mx8", [128, 8], F32)
        lists_stage = b.sbuf("lists_stage", [128, 144], F32)
        idxf = b.sbuf("idxf", [128, 8], F32)
        idxi = b.sbuf("idxi", [128, 8], I32)

        _xaddr = nc.lookup_mloc(xln4_fm).addr
        w1sb = nc.alloc_sbuf_tensor_at("w1sb", [128, ND * FF], BF16, offset=_xaddr)
        w2sb = nc.alloc_sbuf_tensor_at("w2sb", [128, NFT * D], BF16, offset=_xaddr)

        ps_tr = [b.psum(f"ps_tr{i}", [128, 128], BF16) for i in range(2)]
        ps8 = [b.psum(f"ps8_{i}", [128, 512], F32) for i in range(4)]
        ps_r = b.psum("ps_r", [128, 16], F32)
        ps_rep = [ps_r[:, 8 + i:9 + i] for i in range(2)]
        ps_cnt = ps8[3][0:8, 0:128]
        ps_row = ps8[3][0:1, 0:128]

        sy, gp, ve, sc, te = b.sy, b.gp, b.ve, b.sc, b.te

        sy.dma_start(out=identb[:], in_=idb_dram.ap())
        sy.dma_start(out=identf[:], in_=idf_dram.ap())
        sy.dma_start(out=cc[:], in_=cc_dram.ap())
        for ti in range(2):
            sy.dma_start(out=indb[ti][:], in_=indb_dram[ti].ap())
            sy.dma_start(out=indtf[ti][:], in_=indtf_dram[ti].ap())
        sy.dma_start(out=chunkb[:], in_=chunkb_dram.ap())
        sy.dma_start(out=ebase_r[:], in_=ebase_dram.ap())
        sy.dma_start(out=zrow[:], in_=zrow_dram.ap())
        ve.memset(zero128[:], 0.0)
        ve.memset(neg1[:], -1.0)

        sy.dma_start(out=wrow8[:], in_=gw_in)
        ve.tensor_copy(wq8[:], wrow8[:])
        for j in range(ND):
            pj = j % 2
            te.transpose(ps_tr[pj][:, 0:8], wq8[:, 128 * j:128 * (j + 1)], identb[0:8, 0:8])
            ve.tensor_copy(gateT[:, 8 * j:8 * (j + 1)], ps_tr[pj][:, 0:8])

        gp.dma_start(out=out_d, in_=x3_in)

        ve.memset(lists_stage[:], 0.0)
        sy.dma_start(out=lists2_d.ap()[0:SLOTP * 8, :]
                     .rearrange("(p b) c -> p (b c)", p=128),
                     in_=lists_stage[:, 0:2 * (SLOTP * 8 // 128)])
        sy.dma_start(out=lists2_d.ap()[SLOTP * 8:SLOTP * 8 + 8, :],
                     in_=lists_stage[0:8, 0:2])

        for t in range(NT):
            sy.dma_start(out=xtk[:], in_=x3_in[128 * t:128 * (t + 1), :])
            emit_ln_chunk(b, xtk, ntk, stats, sq_sc, cc)
            gp.dma_start(out=xln4_dram.ap()[128 * t:128 * (t + 1), :], in_=ntk[:])
            emit_tr_to_fm(b, ntk, xln4_fm, t, identb, ps_tr)
            for kk in range(ND):
                te.matmul(ps_r[:, 0:8], xln4_fm[:, T * kk + 128 * t: T * kk + 128 * (t + 1)],
                          gateT[:, 8 * kk:8 * (kk + 1)],
                          start=(kk == 0), stop=(kk == ND - 1))
            ve.tensor_copy(probs8[:], ps_r[:, 0:8])
            ve.reduce_max(gcol[:, 0:1], probs8[:], axis=AX.X, negate=True)
            ve.tensor_tensor(gex[:], probs8[:],
                             gcol[:, 0:1].broadcast_to([128, 8]), ALU.add)
            sc.activation(out=gex[:], in_=gex[:], func=AF.Exp)
            ve.reduce_sum(gcol[:, 1:2], gex[:], axis=AX.X)
            ve.reciprocal(gcol[:, 2:3], gcol[:, 1:2])
            ve.tensor_tensor(probs8[:], gex[:],
                             gcol[:, 2:3].broadcast_to([128, 8]), ALU.mult)
            te.transpose(ps_cnt, probs8[:], identf[:, :])
            ve.tensor_copy(vmask[0:8, 0:128], ps_cnt)
            gp.dma_start(out=probsE_d.ap()[:, 128 * t:128 * (t + 1)], in_=vmask[0:8, 0:128])

        for ti in range(2):
            sy.dma_start(out=pc[ti][:],
                         in_=probsE_d.ap().rearrange("e (c w) -> (e c) w", w=128)
                         [128 * ti:128 * (ti + 1), :])
        ve.memset(st8[:, 0:1], 0.0)
        ve.memset(st8[:, 1:2], 1.0)
        for it in range(NIT):
            ve.tensor_add(st8[:, 2:3], st8[:, 0:1], st8[:, 1:2])
            ve.tensor_single_scalar(st8[:, 3:4], st8[:, 2:3], 0.5, ALU.mult)
            for ti in range(2):
                te.matmul(ps_rep[ti], indtf[ti][:, :], st8[:, 3:4],
                          start=True, stop=True)
                ve.tensor_copy(midrep[ti][:], ps_rep[ti])
                ve.tensor_single_scalar(maskb[:], pc[ti][:], midrep[ti][:, 0:1], ALU.is_ge)
                te.matmul(ps_cnt, indb[ti][:, :], maskb[:],
                          start=(ti == 0), stop=(ti == 1))
            ve.reduce_sum(st8[:, 4:5], ps_cnt, axis=AX.X)
            ve.tensor_single_scalar(st8[:, 5:6], st8[:, 4:5], float(KSEL), ALU.is_ge)
            ve.tensor_sub(st8[:, 6:7], st8[:, 3:4], st8[:, 0:1])
            ve.tensor_mul(st8[:, 7:8], st8[:, 5:6], st8[:, 6:7])
            ve.tensor_add(st8[:, 0:1], st8[:, 0:1], st8[:, 7:8])
            ve.tensor_sub(st8[:, 6:7], st8[:, 3:4], st8[:, 1:2])
            ve.tensor_mul(st8[:, 7:8], st8[:, 5:6], st8[:, 6:7])
            ve.tensor_sub(st8[:, 6:7], st8[:, 6:7], st8[:, 7:8])
            ve.tensor_add(st8[:, 1:2], st8[:, 1:2], st8[:, 6:7])
        if dbg == 3:
            b.barrier("gpsimd")
            gp.dma_start(out=dbg_bis[0:8, 0:16], in_=st8[:, :])
            gp.dma_start(out=dbg_bis[0:128, 1:2], in_=midrep[0][:, :],
                         allow_slow_non_contiguous=True)
            gp.dma_start(out=dbg_bis[0:128, 2:3], in_=pc[0][:, 0:1],
                         allow_slow_non_contiguous=True)
            gp.dma_start(out=dbg_bis[0:8, 3:4], in_=ps_cnt.tensor.ap()[0:8, 3:4]
                         if False else pc[1][0:8, 0:1],
                         allow_slow_non_contiguous=True)
            b.barrier("gpsimd")
        for ti in range(2):
            te.matmul(ps_rep[ti], indtf[ti][:, :], st8[:, 0:1],
                      start=True, stop=True)
            ve.tensor_copy(taurep[ti][:], ps_rep[ti])
            ve.tensor_scalar(masku[:], pc[ti][:], taurep[ti][:, 0:1], 0.0,
                             ALU.is_ge, ALU.add, accum_out=ncrow[ti][:, 0:1])

        for ti in range(2):
            te.transpose(ps_row, ncrow[ti][:, 0:1], identf[:, :])
            ve.tensor_copy(nrow[0:1, 128 * ti:128 * (ti + 1)], ps_row)
        ve.tensor_tensor_scan(out=prow[:], data0=nrow[:], data1=zrow[:],
                              initial=0.0, op0=ALU.add, op1=ALU.add)
        ve.tensor_sub(prow[:], prow[:], nrow[:])
        ve.tensor_tensor(prow[:].rearrange("p (e c) -> p e c", c=32),
                         prow[:].rearrange("p (e c) -> p e c", c=32),
                         prow[:, 0:256:32].rearrange("p (e o) -> p e o", o=1)
                         .broadcast_to([1, 8, 32]), ALU.subtract)
        ve.tensor_add(prow[:], prow[:], ebase_r[:])
        for ti in range(2):
            te.transpose(ps_rep[ti], prow[0:1, 128 * ti:128 * (ti + 1)],
                         identf[0:1, 0:1])
            ve.tensor_copy(pcol[ti][:], ps_rep[ti])

        for ti in range(2):
            ve.tensor_single_scalar(pcm[ti][:], pc[ti][:], taurep[ti][:, 0:1], ALU.is_ge)
            ve.scalar_tensor_tensor(pcm[ti][:], pc[ti][:], 1.0, pcm[ti][:],
                                    ALU.add, ALU.mult)
            ve.tensor_single_scalar(pcm[ti][:], pcm[ti][:], 1.0, ALU.subtract)
            vv = evei[ti][:, :].rearrange("p (w two) -> p w two", two=2)
            for r in range(16):
                ve.max(mx8[:, :], pcm[ti][:])
                ve.max_index(ei_u[:, 8 * r:8 * (r + 1)], mx8[:, :], pcm[ti][:])
                ve.tensor_copy(vv[:, 8 * r:8 * (r + 1), 0:1], mx8[:, :])
                ve.match_replace(pcm[ti][:], mx8[:, :], pcm[ti][:], -2.0)
            ve.tensor_copy(vv[:, :, 1:2], ei_u[:, :])
            ve.tensor_single_scalar(vv[:, :, 1:2], vv[:, :, 1:2],
                                    chunkb[:, 0:1], ALU.add)
            ve.tensor_single_scalar(vmask[:], vv[:, :, 0:1], taurep[ti][:, 0:1],
                                    ALU.is_ge)
            ve.tensor_tensor_scan(out=incl[:], data0=vmask[:], data1=zero128[:],
                                  initial=0.0, op0=ALU.add, op1=ALU.add)
            ve.tensor_scalar(dstc[ti][:], incl[:], -1.0, pcol[ti][:, 0:1],
                             ALU.add, ALU.add)
            ve.tensor_mul(dstc[ti][:], dstc[ti][:], vmask[:])
            ve.tensor_single_scalar(incl[:], vmask[:], -1.0, ALU.mult)
            ve.tensor_single_scalar(incl[:], incl[:], 1.0, ALU.add)
            ve.scalar_tensor_tensor(dstc[ti][:], incl[:], float(8 * SLOTP),
                                    dstc[ti][:], ALU.mult, ALU.add)
            ve.tensor_scalar_min(dstc[ti][:], dstc[ti][:], float(8 * SLOTP))

        if dbg:
            b.barrier("gpsimd")
            gp.dma_start(out=dbg_probs[:, 0:128], in_=pc[0][:, :])
            gp.dma_start(out=dbg_probs[:, 128:256], in_=pc[1][:, :])
            gp.dma_start(out=dbg_tau[:, :], in_=st8[:, :])
            gp.dma_start(out=dbg_nc[:, 0:1], in_=ncrow[0][:, :],
                         allow_slow_non_contiguous=True)
            gp.dma_start(out=dbg_nc[:, 1:2], in_=ncrow[1][:, :],
                         allow_slow_non_contiguous=True)
            gp.dma_start(out=dbg_rows[0:1, 0:256], in_=prow[:, :])
            gp.dma_start(out=dbg_rows[0:1, 256:512], in_=nrow[:, :])
            gp.dma_start(out=dbg_dst[:, 0:128], in_=dstc[0][:, :])
            gp.dma_start(out=dbg_dst[:, 128:256], in_=dstc[1][:, :])
            gp.dma_start(out=dbg_ev[:, 0:256], in_=evei[0][:, :])
            gp.dma_start(out=dbg_ev[:, 256:512], in_=evei[1][:, :])
            b.barrier("gpsimd")
        for ti in range(2):
            ve.tensor_copy(dsti[:], dstc[ti][:])
            for i in range(128) if dbg != 2 else []:
                gp.indirect_dma_start(
                    out=lists2_d.ap(),
                    out_offset=IndirectOffsetOnAxis(dsti[:, i:i + 1], 0),
                    in_=evei[ti][:, 2 * i:2 * (i + 1)],
                    in_offset=None)

        if dbg:
            b.barrier("sync")
            for bb in range(72):
                sy.dma_start(out=lists_stage[:, 2 * bb:2 * (bb + 1)],
                             in_=lists2_d.ap()[128 * bb:128 * (bb + 1), :])
            b.barrier("gpsimd")
            gp.dma_start(out=dbg_lists2[:, :], in_=lists_stage[:, :])
            b.barrier("gpsimd")

        for e2 in range(NE):
            b.barrier("gpsimd")
            for kk in range(ND):
                gp.dma_start(out=bass.AP(w1sb, 0, [[1, 128], [1, FF]])
                             if False else w1sb.ap()[:, FF * kk:FF * (kk + 1)],
                             in_=w1_in[e2, 128 * kk:128 * (kk + 1), :])
            for j in range(8):
                sy.dma_start(out=gi2[:, 2 * j:2 * (j + 1)],
                             in_=lists2_d.ap()[SLOTP * e2 + 128 * j:
                                               SLOTP * e2 + 128 * (j + 1), :])
            ve.tensor_copy(gates8[:, :], gi2[:, 0:16:2])
            ve.tensor_copy(idxf[:, :], gi2[:, 1:16:2])
            ve.tensor_scalar(idxf[:], idxf[:], 0.0, float(T - 1), ALU.max, ALU.min)
            ve.tensor_copy(idxi[:], idxf[:])
            for j in range(8):
                gp.indirect_dma_start(
                    out=xsel_sb[:, :], out_offset=None,
                    in_=xln4_dram.ap(),
                    in_offset=IndirectOffsetOnAxis(idxi[:, j:j + 1], 0))
                for kk in range(ND):
                    pj = kk % 2
                    te.transpose(ps_tr[pj][:], xsel_sb[:, 128 * kk:128 * (kk + 1)],
                                 identb[:])
                    ve.tensor_copy(xsel_fm[:, KSEL * kk + 128 * j:
                                           KSEL * kk + 128 * (j + 1)], ps_tr[pj][:])
            for ft in range(NFT):
                for half in range(2):
                    pb = (2 * ft + half) % 4
                    for kk in range(ND):
                        te.matmul(ps8[pb][:, 0:512],
                                  w1sb.ap()[:, FF * kk + 128 * ft: FF * kk + 128 * (ft + 1)],
                                  xsel_fm[:, KSEL * kk + 512 * half:
                                          KSEL * kk + 512 * (half + 1)],
                                  start=(kk == 0), stop=(kk == ND - 1))
                    sc.activation(out=h_sb[:, KSEL * ft + 512 * half:
                                           KSEL * ft + 512 * (half + 1)],
                                  in_=ps8[pb][:, 0:512], func=AF.Gelu)
            b.barrier("gpsimd")
            for ft in range(NFT):
                gp.dma_start(out=w2sb.ap()[:, D * ft:D * (ft + 1)],
                             in_=w2_in[e2, 128 * ft:128 * (ft + 1), :])
            for tc in range(8):
                for nh in range(2):
                    pb = (2 * tc + nh) % 4
                    for ft in range(NFT):
                        te.matmul(ps8[pb][:, 0:512],
                                  h_sb[:, KSEL * ft + 128 * tc:
                                       KSEL * ft + 128 * (tc + 1)],
                                  w2sb.ap()[:, D * ft + 512 * nh: D * ft + 512 * (nh + 1)],
                                  start=(ft == 0), stop=(ft == NFT - 1))
                    ve.tensor_single_scalar(h2tk[:, 512 * nh:512 * (nh + 1)],
                                            ps8[pb][:, 0:512],
                                            gates8[:, tc:tc + 1], ALU.mult)
                gp.indirect_dma_start(
                    out=out_d, out_offset=IndirectOffsetOnAxis(idxi[:, tc:tc + 1], 0),
                    in_=h2tk[:, :], in_offset=None, compute_op=ALU.add)
        b.barrier("sync")
        b.realize()
    return nc


def _run_launch3c(inputs, x3_all, cores=8, dbg=False):
    nc = build_launch3c(dbg=dbg)
    in_maps = []
    for c in range(cores):
        in_maps.append({
            "x3": np.ascontiguousarray(x3_all[c], np.float32),
            "gw": inputs["L1_ffn_gate_w"],
            "w1": inputs["L1_ffn_w1"],
            "w2": inputs["L1_ffn_w2"],
        })
    return run_bass_kernel_spmd(nc, in_maps, core_ids=list(range(cores)))



def kernel(**inputs):
    x = inputs["L0_attn_x"]

    res1 = _run_launch1(inputs, [x[c] for c in range(8)])
    x1_all = np.stack([np.asarray(res1.results[c]["x1"]) for c in range(8)])
    qkv_all = np.stack([np.asarray(res1.results[c]["qkv"]) for c in range(8)])

    res2 = _run_launch2(qkv_all, inputs["L0_ffn_proj_weight"])
    po_all = np.zeros((8, T, D), np.float32)
    for j in range(8):
        po_all[:, 512 * j:512 * (j + 1), :] = \
            np.asarray(res2.results[j]["po"]).reshape(8, 512, D)

    res3a = _run_launch3a(inputs, x1_all, po_all)
    x2_all = np.stack([np.asarray(res3a.results[c]["x2"]) for c in range(8)])

    res3b = _run_launch3b(inputs, x2_all)
    x3_all = np.stack([np.asarray(res3b.results[c]["x3"]) for c in range(8)])

    res3c = _run_launch3c(inputs, x3_all)
    out = np.stack([np.asarray(res3c.results[c]["out"]) for c in range(8)])
    return out.astype(np.float32)
